# version 24
# baseline (speedup 1.0000x reference)
import numpy as np
import ml_dtypes
import concourse.bass as bass
import concourse.mybir as mybir
from concourse.alu_op_type import AluOpType
from concourse.bass_utils import run_bass_kernel_spmd

F32 = mybir.dt.float32
BF16 = mybir.dt.bfloat16
I32 = mybir.dt.int32
ACT = mybir.ActivationFunctionType
AX = mybir.AxisListType

NCORE = 8
B_TOT = 16
SEQ = 2048
D = 1024
H = 16
E = 64
NDC = 8
NP = 8
CH = 512
NCHUNK = 4
NHC = 24
LN_EPS_PRE = 1e-5
LN_EPS_3 = 1e-3
LN127 = float(np.log(127.0))


class Tok:
    __slots__ = ("src", "idx", "queue")
    def __init__(self, src, idx, queue):
        self.src, self.idx, self.queue = src, idx, queue


class Buf:
    __slots__ = ("w", "r", "name", "psum")
    def __init__(self, name="", psum=False):
        self.w, self.r, self.name = None, [], name
        self.psum = psum


class Plan:
    ENGS = ("t", "v", "s", "g", "y")

    def __init__(self):
        self.ops = {e: [] for e in self.ENGS}
        self.cnt = {e: 0 for e in self.ENGS}
        self.qcnt = {}
        self.waited = {e: {} for e in self.ENGS}
        self.referenced = set()

    def emit(self, eng, fn, reads=(), writes=(), queue=None):
        deps = []
        for b in reads:
            if b.w is not None:
                deps.append((b.w, True))
            if b.psum and eng in ("v", "s"):
                for t in b.r:
                    if t.src in ("v", "s") and t.src != eng:
                        deps.append((t, False))
        qskip = []
        for b in writes:
            if b.w is not None:
                deps.append((b.w, False))
                if b.r:
                    qskip.append(b.w)
            for t in b.r:
                deps.append((t, False))
                qskip.append(t)
        for b in reads:
            if b.w is not None:
                qskip.append(b.w)
        waits = {}
        for t, is_raw in deps:
            if t.src == eng and not t.queue and eng == "t":
                continue
            if (queue is not None and t.queue and t.src == queue
                    and t in qskip):
                continue
            if t.queue:
                t = Tok(t.src, self.qcnt[t.src], True)
            cur = waits.get(t.src)
            if cur is None or t.idx > cur.idx:
                waits[t.src] = t
        final = []
        for src, t in waits.items():
            if self.waited[eng].get(src, -1) >= t.idx:
                continue
            self.waited[eng][src] = t.idx
            final.append(t)
            if not t.queue:
                self.referenced.add((t.src, t.idx))
        if queue is not None:
            self.qcnt[queue] = self.qcnt.get(queue, 0) + 16
            tok = Tok(queue, self.qcnt[queue], True)
        else:
            self.cnt[eng] += 1
            tok = Tok(eng, self.cnt[eng], False)
        self.ops[eng].append((final, fn, tok, queue))
        for b in reads:
            b.r.append(tok)
        for b in writes:
            b.w = tok
            b.r = []
        return tok

    def finalize(self):
        semval = {}
        for e in self.ENGS:
            c = 0
            for (waits, fn, tok, queue) in self.ops[e]:
                if tok.queue:
                    semval[(tok.src, tok.idx)] = tok.idx
                elif (tok.src, tok.idx) in self.referenced:
                    c += 1
                    semval[(tok.src, tok.idx)] = c
                else:
                    semval[(tok.src, tok.idx)] = None
        return semval


import os as _os
_LIM = _os.environ.get("KMK_LIM", "full")
_NOINJ = _os.environ.get("KMK_NOINJ", "0") == "1"
_INJL = 0 if _NOINJ else int(_os.environ.get("KMK_INJL", "3"))
_NOQB = _os.environ.get("KMK_NOQB", "0") == "1"
_NODEF = _os.environ.get("KMK_NODEF", "0") == "1"
_NOPREP = _os.environ.get("KMK_NOPREP", "1") == "1"
_LIMS = {"s": 0, "qkv_w": 0.2, "qkv_k": 0.4, "k2a": 0.45, "k2b": 0.5, "k2c": 0.55, "qkv_k2": 0.6, "qkv": 1, "attn0": 2, "half0": 3, "seq0": 4, "full": 99}


def build_program():
    nc = bass.Bass("TRN2", target_bir_lowering=False)

    dx = nc.dram_tensor("x", [2, 128, NDC * SEQ], F32, kind="ExternalInput")
    dwq = nc.dram_tensor("wq", [128, NDC * H * E], F32, kind="ExternalInput")
    dwk = nc.dram_tensor("wk", [128, NDC * H * E], F32, kind="ExternalInput")
    dwv = nc.dram_tensor("wv", [128, NDC * H * E], F32, kind="ExternalInput")
    dwo = nc.dram_tensor("wo", [128, NP * D], F32, kind="ExternalInput")
    dgb = nc.dram_tensor("gb", [128, 16], F32, kind="ExternalInput")
    dwg = nc.dram_tensor("wg", [NHC, 128, D], I32, kind="ExternalInput")
    dwu = nc.dram_tensor("wu", [NHC, 128, D], I32, kind="ExternalInput")
    dwd = nc.dram_tensor("wd", [NHC, 128, D], I32, kind="ExternalInput")
    dcb = nc.dram_tensor("cb", [128, 1792], F32, kind="ExternalInput")
    dcf = nc.dram_tensor("cf", [128, 144], F32, kind="ExternalInput")
    dout = nc.dram_tensor("o", [2, 128, NDC * SEQ], F32, kind="ExternalOutput")

    P = Plan()

    from contextlib import ExitStack
    with ExitStack() as _stack:
        ent = _stack.enter_context
        xh = ent(nc.sbuf_tensor("xh", [128, 8192], F32))
        zr = ent(nc.sbuf_tensor("zr", [128, 16384], BF16))
        kb = ent(nc.sbuf_tensor("kb", [128, NP * SEQ], BF16))
        vb = ent(nc.sbuf_tensor("vb", [128, 16 * 1024], BF16))
        wqb = ent(nc.sbuf_tensor("wqb", [128, 8192], BF16))
        wob = ent(nc.sbuf_tensor("wob", [128, 8192], BF16))
        wstag = ent(nc.sbuf_tensor("wstag", [128, 3072], I32))
        qsb_t = ent(nc.sbuf_tensor("qsb", [128, 2048], BF16))
        brt = ent(nc.sbuf_tensor("brt", [128, 2], F32))
        wgbb = ent(nc.sbuf_tensor("wgbb", [128, 2048], BF16))
        wubb = ent(nc.sbuf_tensor("wubb", [128, 2048], BF16))
        wdbb = ent(nc.sbuf_tensor("wdbb", [128, 2048], BF16))
        tsc = ent(nc.sbuf_tensor("tsc", [128, 1024], F32))
        invs = ent(nc.sbuf_tensor("invs", [128, 512], BF16))
        mub = ent(nc.sbuf_tensor("mub", [128, 512], BF16))
        rub = ent(nc.sbuf_tensor("rub", [128, 512], BF16))
        rowa = ent(nc.sbuf_tensor("rowa", [128, 512], F32))
        rowb = ent(nc.sbuf_tensor("rowb", [128, 512], F32))
        browb = ent(nc.sbuf_tensor("browb", [128, 8], BF16))
        bbias = ent(nc.sbuf_tensor("bbias", [128, 2], F32))
        k2m = ent(nc.sbuf_tensor("k2m", [128, 2], F32))
        mk2 = ent(nc.sbuf_tensor("mk2", [128, 2], F32))
        sgb = ent(nc.sbuf_tensor("sgb", [128, 16], F32))
        xst = ent(nc.sbuf_tensor("xst", [128, 2048], BF16))
        cbt = ent(nc.sbuf_tensor("cbt", [128, 1792], BF16))
        cft = ent(nc.sbuf_tensor("cft", [128, 144], F32))
        ps01 = ent(nc.psum_tensor("ps01", [128, 1024], F32))
        ps23 = ent(nc.psum_tensor("ps23", [128, 1024], F32))
        ps4 = ent(nc.psum_tensor("ps4", [128, 512], F32))
        ps5 = ent(nc.psum_tensor("ps5", [128, 512], F32))
        ps6 = ent(nc.psum_tensor("ps6", [128, 512], F32))
        ps7 = ent(nc.psum_tensor("ps7", [128, 512], F32))
        sem_t = ent(nc.semaphore("sem_t"))
        sem_v = ent(nc.semaphore("sem_v"))
        sem_s = ent(nc.semaphore("sem_s"))
        sem_g = ent(nc.semaphore("sem_g"))
        sem_y = ent(nc.semaphore("sem_y"))
        sem_qc = ent(nc.semaphore("sem_qc"))
        sem_qxh = [ent(nc.semaphore("sem_qxh%d" % i)) for i in range(8)]
        sem_qt0 = ent(nc.semaphore("sem_qt0"))
        sem_qt1 = ent(nc.semaphore("sem_qt1"))
        sem_qw0 = ent(nc.semaphore("sem_qw0"))
        sem_qw1 = ent(nc.semaphore("sem_qw1"))
        sem_qw2 = ent(nc.semaphore("sem_qw2"))
        sem_qw3 = ent(nc.semaphore("sem_qw3"))
        sem_qo = ent(nc.semaphore("sem_qo"))
        sem_qo1 = ent(nc.semaphore("sem_qo1"))
        with nc.Block() as blk0:
            @blk0.vector
            def _(v0):
                for _sm in ([sem_t, sem_v, sem_s, sem_g, sem_y, sem_qc,
                             sem_qt0, sem_qt1, sem_qw0, sem_qw1, sem_qw2, sem_qw3,
                             sem_qo, sem_qo1]
                            + list(sem_qxh)):
                    v0.sem_clear(_sm)
        block = ent(nc.Block())
        class _PsView:
            __slots__ = ("t", "coff")
            def __init__(self, t, coff):
                self.t, self.coff = t, coff
            def __getitem__(self, idx):
                if isinstance(idx, tuple):
                    rows, cols = idx
                else:
                    rows, cols = idx, slice(None)
                c0 = cols.start or 0
                c1 = 512 if cols.stop is None else cols.stop
                return self.t[rows, self.coff + c0: self.coff + c1]
        PS = [_PsView(ps01, 0), _PsView(ps01, 512), _PsView(ps23, 0), _PsView(ps23, 512),
              ps4, ps5, ps6, ps7]
        PSB = [ps01, ps23]
        SEMS = {"t": sem_t, "v": sem_v, "s": sem_s, "g": sem_g, "y": sem_y,
                "qc": sem_qc, "qt0": sem_qt0, "qt1": sem_qt1,
                "qw0": sem_qw0, "qw1": sem_qw1, "qw2": sem_qw2, "qw3": sem_qw3,
                "qo": sem_qo, "qo1": sem_qo1}
        for _i in range(8):
            SEMS["qxh%d" % _i] = sem_qxh[_i]

        c_ones64 = cbt[:, 0:64]
        c_onescol = [cbt[:, 64 + j * 64: 128 + j * 64] for j in range(8)]
        c_eselT = [cbt[:, 576 + (j * 2) * 64: 576 + (j * 2) * 64 + 64] for j in range(8)]
        c_eselB = [cbt[:, 576 + (j * 2 + 1) * 64: 576 + (j * 2 + 1) * 64 + 64] for j in range(8)]
        c_mask = cbt[:, 1600:1728]
        c_allones = cbt[:, 1728:1792]
        c_onesrow0 = cft[:, 0:128]
        _CONSTVALS = [1e-05, 0.001, -LN127, -0.0625, 0.0625, 0.125, -0.125,
                      1.0 / 1024.0, 1.0 / 127.0, 0.5, -0.5, 2.0, -1.0, 3.0]
        for _i, _cv in enumerate(_CONSTVALS):
            nc.const_aps.register(F32, float(_cv), cft[:, 128 + _i:129 + _i])

        bXH = [[Buf(f"xh{d}_{p}") for p in range(2)] for d in range(NDC)]
        bKB, bVB = Buf("kb"), Buf("vb")
        bWQ = [Buf(f"wq{i}") for i in range(8)]
        bWO = [Buf(f"wo{i}") for i in range(8)]
        bPS = [Buf(f"ps{i}", psum=True) for i in range(8)]
        bWST = [Buf("wst0"), Buf("wst1"), Buf("wst2")]
        bQS = [Buf(f"qs{i}") for i in range(4)]
        bWG = [Buf("wg0"), Buf("wg1")]
        bWU = [Buf("wu0"), Buf("wu1")]
        bWD = [Buf("wd0"), Buf("wd1")]
        bTS = [Buf("ts0"), Buf("ts1")]
        bINV, bROWA, bROWB, bROWT = Buf("invs"), Buf("rowa"), Buf("rowb"), Buf("rowt")
        bMUB = Buf("mub")
        bRUB = Buf("rub")
        bBROWB, bK2, bMK2, bGB = Buf("browb"), Buf("k2m"), Buf("mk2"), Buf("sgb")
        bBB = [Buf("bb0"), Buf("bb1")]
        bXST = [Buf("xst0"), Buf("xst1"), Buf("xst2"), Buf("xst3")]
        bCB, bCF = Buf("cb"), Buf("cf")

        zr_era = [[]]

        def new_era(bufnames):
            old_toks = []
            for b in zr_era[0]:
                if b.w is not None:
                    old_toks.append(b.w)
                old_toks.extend(b.r)
            out = []
            for n in bufnames:
                b = Buf(n)
                b.r = list(old_toks)
                out.append(b)
            zr_era[0] = list(out)
            return out

        def rebind(bufnames, old):
            toks = []
            for b in old:
                if b.w is not None:
                    toks.append(b.w)
                toks.extend(b.r)
            out = []
            for n in bufnames:
                b = Buf(n)
                b.r = list(toks)
                out.append(b)
            for b in old:
                zr_era[0].remove(b)
            zr_era[0].extend(out)
            return out

        E_ = P.emit

        E_("y", lambda y: y.dma_start(tsc[:], dcb.ap()[:, 0:1024]), writes=[bTS[0], bTS[1]], queue="qc")
        E_("y", lambda y: y.dma_start(cft[:], dcf.ap()), writes=[bCF], queue="qc")
        E_("y", lambda y: y.dma_start(sgb[:], dgb.ap()), writes=[bGB], queue="qc")
        E_("v", lambda v: v.tensor_copy(cbt[:, 0:1024], tsc[:]), reads=[bTS[0], bTS[1], bCF, bGB], writes=[bCB])
        E_("y", lambda y: y.dma_start(tsc[:, 0:768], dcb.ap()[:, 1024:1792]), writes=[bTS[0], bTS[1]], queue="qc")
        E_("v", lambda v: v.tensor_copy(cbt[:, 1024:1792], tsc[:, 0:768]), reads=[bTS[0], bTS[1]], writes=[bCB])
        E_("v", lambda v: v.memset(invs[:], 0.0), writes=[bINV])
        E_("v", lambda v: v.memset(qsb_t[:], 0.0), writes=list(bQS))
        E_("v", lambda v: v.memset(rowa[:], 0.0), writes=[bROWA])
        E_("v", lambda v: v.memset(rowb[:], 0.0), writes=[bROWB])
        E_("v", lambda v: v.memset(browb[:], 0.0), writes=[bBROWB])

        for _k in range(8):
            E_("t", lambda t, d=PS[_k][:, 0:512]: t.matmul(
                d, c_onesrow0, rowa[:, 0:512], start=True, stop=True),
                reads=[bROWA, bCF], writes=[bPS[_k]])

        nstores = [0, 0]

        def rowmath_r(stat_buf, s1_ap, s2_ap, eps, extra_bias):
            a1 = rowa[0:1, 0:512]
            t0 = tsc[0:1, 0:512]
            t32 = tsc[0:1, 512:1024]
            t64 = t0
            t96 = t32
            b1 = rowb[0:1, 0:512]
            E_("s", lambda s, a=s1_ap, d=a1: s.activation(d, a, ACT.Copy, scale=1.0 / 1024.0),
               reads=[stat_buf], writes=[bROWA])
            E_("s", lambda s, a=s2_ap, d=t0: s.activation(d, a, ACT.Copy, scale=1.0 / 1024.0),
               reads=[stat_buf], writes=[bTS[0]])
            E_("v", lambda v, a=a1, d=t32: v.tensor_tensor(d, a, a, AluOpType.mult),
               reads=[bROWA], writes=[bTS[1]])
            E_("v", lambda v, a=t0, b=t32, d=t64: v.tensor_tensor(d, a, b, AluOpType.subtract),
               reads=[bTS[1], bTS[0]], writes=[bTS[0]])
            E_("s", lambda s, a=t64, d=t96, eps=eps: s.activation(d, a, ACT.Ln, bias=float(eps), scale=1.0),
               reads=[bTS[0], bCF], writes=[bTS[1]])
            E_("s", lambda s, a=t96, d=b1, xb=extra_bias: s.activation(d, a, ACT.Exp, bias=float(xb), scale=-0.5),
               reads=[bTS[1], bCF], writes=[bROWB])

        def bcast_f32(pidx, src_t, src_buf):
            dst = PS[pidx][:, 0:512]
            src = src_t[:, 0:512]
            E_("t", lambda t, d=dst, s=src: t.matmul(d, c_onesrow0, s, start=True, stop=True),
               reads=[src_buf, bCF], writes=[bPS[pidx]])

        def _stat_slots(sl, alt):
            if alt:
                return xst[:, 0:512], bXST[0], xst[:, 512:1024], bXST[2]
            xb = xst[:, sl * 512:(sl + 1) * 512]
            bxb = bXST[2 * sl]
            x2 = xst[:, 1024 + sl * 512: 1024 + (sl + 1) * 512]
            bx2 = bXST[2 * sl + 1]
            return xb, bxb, x2, bx2

        def stats_piece_ve(xsl, xbuf, sl, use_qsb=False):
            xb, bxb, x2, bx2 = _stat_slots(sl, use_qsb)
            E_("v", lambda v, a=xsl, d=xb: v.tensor_copy(d, a),
               reads=[xbuf], writes=[bxb])
            E_("s", lambda s, a=xsl, d=x2: s.activation(d, a, ACT.Square),
               reads=[xbuf], writes=[bx2])

        def stats_piece_mm(dc, sl, pidx, use_qsb=False):
            xb, bxb, x2, bx2 = _stat_slots(sl, use_qsb)
            E_("t", lambda t, dc=dc, d=PS[pidx][0:64, :], a=xb: t.matmul(
                d, c_ones64, a, start=(dc == 0), stop=(dc == NDC - 1), tile_position=(0, 0), skip_group_check=True),
                reads=[bxb, bCB], writes=[bPS[pidx]])
            E_("t", lambda t, dc=dc, d=PS[pidx][64:128, :], a=x2: t.matmul(
                d, c_ones64, a, start=(dc == 0), stop=(dc == NDC - 1), tile_position=(0, 64),
                skip_group_check=True),
                reads=[bx2, bCB], writes=[bPS[pidx]])

        def stats_chunk(xcols, pidx, xh_bufs):
            for dc, (xsl, xbuf) in enumerate(xcols):
                sl = dc % 2
                stats_piece_ve(xsl, xbuf, sl)
                stats_piece_mm(dc, sl, pidx)

        def ffn_stat_items(tf, bank, psum_r, use_qsb=False):
            xcols = [(xh[:, dc * 1024 + tf * 512: dc * 1024 + tf * 512 + 512], bXH[dc][tf])
                     for dc in range(NDC)]
            items = []
            for dc, (xsl, xbuf) in enumerate(xcols):
                sl = dc % 2
                items.append(lambda xsl=xsl, xbuf=xbuf, sl=sl: stats_piece_ve(xsl, xbuf, sl, use_qsb))
                items.append(lambda dc=dc, sl=sl: stats_piece_mm(dc, sl, bank, use_qsb))
            items.append(lambda: rowmath_r(bPS[bank], PS[bank][0:1, :], PS[bank][64:65, :],
                                           LN_EPS_PRE, -LN127))
            items.append(None)
            def _mu():
                bcast_f32(bank, rowa, bROWA)
                E_("v", lambda v: v.tensor_copy(mub[:], PS[bank][:]),
                   reads=[bPS[bank]], writes=[bMUB])
            items.append(_mu)
            items.append(None)
            items.append(lambda: bcast_f32(bank, rowb, bROWB))
            if not psum_r:
                items.append(lambda: E_("v", lambda v: v.tensor_copy(rub[:], PS[bank][:]),
                                        reads=[bPS[bank]], writes=[bRUB]))
            items.append(None)
            return items

        def ffn_zgen_items(tf, bank, z3Bl, z3_ap_l, psum_r):
            items = []
            for dc in range(NDC):
                xsl = xh[:, dc * 1024 + tf * 512: dc * 1024 + tf * 512 + 512]
                xbuf = bXH[dc][tf]
                tslot = dc % 2
                tt = tsc[:, tslot * 512:(tslot + 1) * 512]
                dap = z3_ap_l[dc]
                dbuf = z3Bl[dc]
                def _z(xsl=xsl, xbuf=xbuf, tt=tt, dap=dap, dbuf=dbuf, tslot=tslot):
                    E_("v", lambda v, a=xsl, d=tt: v.tensor_tensor(
                        d, a, mub[:], AluOpType.subtract),
                        reads=[xbuf, bMUB], writes=[bTS[tslot]])
                    if psum_r:
                        E_("v", lambda v, a=tt, d=dap: v.tensor_tensor(
                            d, a, PS[bank][:], AluOpType.mult),
                            reads=[bTS[tslot], bPS[bank]], writes=[dbuf])
                    else:
                        E_("v", lambda v, a=tt, d=dap: v.tensor_tensor(
                            d, a, rub[:], AluOpType.mult),
                            reads=[bTS[tslot], bRUB], writes=[dbuf])
                items.append(_z)
            return items

        def zgen(xcols, mu_pidx, r_pidx, dsts):
            for dc, (xsl, xbuf) in enumerate(xcols):
                tslot = dc % 2
                tt = tsc[:, tslot * 512:(tslot + 1) * 512]
                dap, dbufs = dsts[dc]
                E_("v", lambda v, a=xsl, d=tt, m=PS[mu_pidx][:]: v.tensor_tensor(
                    d, a, m, AluOpType.subtract),
                    reads=[xbuf, bPS[mu_pidx]], writes=[bTS[tslot]])
                E_("v", lambda v, a=tt, d=dap, r=PS[r_pidx][:]: v.tensor_tensor(
                    d, a, r, AluOpType.mult),
                    reads=[bTS[tslot], bPS[r_pidx]], writes=dbufs)

        def load_weights_bf16(dsrc, dstT, dstB):
            for pc in range(16):
                slot = pc % 2
                st = tsc[:, slot * 512:(slot + 1) * 512]
                src = dsrc.ap()[:, pc * 512:(pc + 1) * 512]
                dst = dstT[:, pc * 512:(pc + 1) * 512]
                E_("y", lambda y, d=st, s=src: y.dma_start(d, s),
                   writes=[bTS[slot]], queue="qt%d" % slot)
                E_("v", lambda v, d=dst, s=st: v.tensor_copy(d, s),
                   reads=[bTS[slot]], writes=[dstB[pc // 2]])

        def load_weights_bf16_pair(jobs):
            for pc in range(16):
                for j, (dsrc, dstT, dstB) in enumerate(jobs):
                    slot = j % 2
                    st = tsc[:, slot * 512:(slot + 1) * 512]
                    src = dsrc.ap()[:, pc * 512:(pc + 1) * 512]
                    dst = dstT[:, pc * 512:(pc + 1) * 512]
                    E_("y", lambda y, d=st, s=src: y.dma_start(d, s),
                       writes=[bTS[slot]], queue="qt%d" % slot)
                    E_("v", lambda v, d=dst, s=st: v.tensor_copy(d, s),
                       reads=[bTS[slot]], writes=[dstB[pc // 2]])

        lim = _LIMS[_LIM]
        for s_i in range(2):
            if s_i == 1 and lim < 99:
                break
            xs = dx.ap()[s_i]
            os_ = dout.ap()[s_i]

            if s_i == 0:
                load_weights_bf16_pair([(dwk, wqb, bWQ), (dwv, wob, bWO)])
            z1B = new_era([f"z1_{d}" for d in range(NDC)])
            for half in range(2):
                hc0 = half * 1024
                for dc in range(NDC):
                    for pc_ in range(2):
                        dst = xh[:, dc * 1024 + pc_ * 512: dc * 1024 + pc_ * 512 + 512]
                        src = xs[:, dc * 2048 + hc0 + pc_ * 512: dc * 2048 + hc0 + pc_ * 512 + 512]
                        E_("y", lambda y, d=dst, s=src: y.dma_start(d, s),
                           writes=[bXH[dc][pc_]], queue="qxh%d" % dc)
                for pc in range(2):
                    xcols = [(xh[:, dc * 1024 + pc * 512: dc * 1024 + pc * 512 + 512], bXH[dc][pc])
                             for dc in range(NDC)]
                    stats_chunk(xcols, 2, bXH)
                    rowmath_r(bPS[2], PS[2][0:1, :], PS[2][64:65, :], LN_EPS_PRE, 0.0)
                    bcast_f32(5, rowa, bROWA)
                    bcast_f32(4, rowb, bROWB)
                    dsts = []
                    for dc in range(NDC):
                        col = dc * 2048 + hc0 + pc * 512
                        dsts.append((zr[:, col:col + 512], [z1B[dc]]))
                    zgen(xcols, 5, 4, dsts)

            if lim < 0.2:
                break
            E_("v", lambda v: v.memset(k2m[:, 0:1], -1.0), writes=[bK2])
            if lim < 0.4:
                break
            for pr in range(NP):
                for piece in range(4):
                    pq = PS[piece % 2]
                    bpq = bPS[piece % 2]
                    for dc in range(NDC):
                        zsl = zr[:, dc * 2048 + piece * 512: dc * 2048 + piece * 512 + 512]
                        wE = wqb[:, dc * 1024 + (2 * pr) * 64: dc * 1024 + (2 * pr) * 64 + 64]
                        wO_ = wqb[:, dc * 1024 + (2 * pr + 1) * 64: dc * 1024 + (2 * pr + 1) * 64 + 64]
                        E_("t", lambda t, dc=dc, d=pq[0:64, :], w=wE, z=zsl: t.matmul(
                            d, w, z, start=(dc == 0), stop=(dc == NDC - 1), tile_position=(0, 0), skip_group_check=True),
                            reads=[bWQ[dc], z1B[dc]], writes=[bpq])
                        E_("t", lambda t, dc=dc, d=pq[64:128, :], w=wO_, z=zsl: t.matmul(
                            d, w, z, start=(dc == 0), stop=(dc == NDC - 1), tile_position=(0, 64), skip_group_check=True),
                            reads=[bWQ[dc], z1B[dc]], writes=[bpq])
                    kdst = kb[:, pr * 2048 + piece * 512: pr * 2048 + piece * 512 + 512]
                    E_("v", lambda v, d=kdst, a=pq[:]: v.tensor_copy(d, a),
                       reads=[bpq], writes=[bKB])
                    if lim < 0.45:
                        continue
                    ksq = xst[:, 0:512]
                    E_("s", lambda s, a=pq[:], d=ksq: s.activation(d, a, ACT.Square),
                       reads=[bpq], writes=[bXST[0]])
                    if lim < 0.5:
                        continue
                    E_("t", lambda t, a=ksq: t.matmul(
                        PS[6][0:64, :], c_ones64, a, start=True, stop=True,
                        tile_position=(0, 0)),
                        reads=[bXST[0], bCB], writes=[bPS[6]])
                    if lim < 0.55:
                        continue
                    E_("v", lambda v: v.tensor_reduce(
                        k2m[0:1, 1:2], PS[6][0:1, :], AX.X, AluOpType.max),
                        reads=[bPS[6]], writes=[bK2])
                    if lim < 0.6:
                        continue
                    E_("v", lambda v: v.tensor_tensor(
                        k2m[0:1, 0:1], k2m[0:1, 0:1], k2m[0:1, 1:2], AluOpType.max),
                        reads=[bK2], writes=[bK2])
            load_weights_bf16(dwq, wqb, bWQ)
            if lim < 1:
                break
            E_("v", lambda v: v.tensor_scalar(
                mk2[0:1, 1:2], k2m[0:1, 0:1], -0.0625, 0.0, AluOpType.mult, AluOpType.add),
                reads=[bK2, bCF], writes=[bMK2])

            for tk in range(16):
                for hh in range(2):
                    pv = PS[4 + hh]
                    bpv = bPS[4 + hh]
                    for dc in range(NDC):
                        zsl = zr[:, dc * 2048 + tk * 128: dc * 2048 + tk * 128 + 128]
                        wsl = wob[:, dc * 1024 + hh * 512: dc * 1024 + hh * 512 + 512]
                        E_("t", lambda t, dc=dc, d=pv[:], z=zsl, w=wsl: t.matmul(
                            d, z, w, start=(dc == 0), stop=(dc == NDC - 1)),
                            reads=[bWO[dc], z1B[dc]], writes=[bpv])
                    vdst = vb[:, tk * 1024 + hh * 512: tk * 1024 + hh * 512 + 512]
                    E_("v", lambda v, d=vdst, a=pv[:]: v.tensor_copy(d, a),
                       reads=[bpv], writes=[bVB])

            load_weights_bf16(dwo, wob, bWO)

            if lim < 2:
                break
            def emit_xload(half, pc_):
                for dc in range(NDC):
                    dst = xh[:, dc * 1024 + pc_ * 512: dc * 1024 + pc_ * 512 + 512]
                    src = xs[:, dc * 2048 + half * 1024 + pc_ * 512:
                              dc * 2048 + half * 1024 + pc_ * 512 + 512]
                    E_("y", lambda y, d=dst, s=src: y.dma_start(d, s),
                       writes=[bXH[dc][pc_]], queue="qxh%d" % dc)

            for half in range(2):
                if half == 0:
                    for pc_ in range(2):
                        emit_xload(half, pc_)

                names = ([f"zcA{d}" for d in range(NDC)] + [f"zcB{d}" for d in range(NDC)]
                         + [f"an{p}" for p in range(NP)]
                         + ["pb0", "pb1", "qy2"])
                eb = new_era(names)
                zcB = [eb[0:8], eb[8:16]]
                anB = eb[16:24]
                pbB = eb[24:26]; q2B = eb[26]; y2B = eb[26]
                ZC0, AN0, PB0, Q20 = 0, 8192, 12288, 14336
                Y20 = Q20
                zc_ap = [[zr[:, ZC0 + sl * 4096 + d * 512: ZC0 + sl * 4096 + d * 512 + 512]
                          for d in range(NDC)] for sl in range(2)]
                an_ap = [zr[:, AN0 + p * 512: AN0 + p * 512 + 512] for p in range(NP)]
                qs_ap = [qsb_t[:, i * 512:(i + 1) * 512] for i in range(4)]
                q2_ap = zr[:, Q20:Q20 + 512]
                y2_ap = zr[:, Y20:Y20 + 512]

                def att_prep_items(ci):
                    xcol = ci * 512
                    slot = ci % 2
                    xcols = [(xh[:, dc * 1024 + xcol: dc * 1024 + xcol + 512], bXH[dc][ci])
                             for dc in range(NDC)]
                    items = []
                    for dc, (xsl, xbuf) in enumerate(xcols):
                        sl = dc % 2
                        items.append(lambda xsl=xsl, xbuf=xbuf, sl=sl: stats_piece_ve(xsl, xbuf, sl))
                        items.append(lambda dc=dc, sl=sl: stats_piece_mm(dc, sl, 6))
                    items.append(lambda: rowmath_r(bPS[6], PS[6][0:1, :], PS[6][64:65, :],
                                                   LN_EPS_PRE, 0.0))
                    items.append(None)
                    def _mu1():
                        bcast_f32(6, rowa, bROWA)
                        E_("v", lambda v: v.tensor_copy(mub[:], PS[6][:]),
                           reads=[bPS[6]], writes=[bMUB])
                    items.append(_mu1)
                    items.append(None)
                    items.append(lambda: bcast_f32(6, rowb, bROWB))
                    items.append(None)
                    for dc in range(NDC):
                        def _z(dc=dc):
                            xsl = xh[:, dc * 1024 + xcol: dc * 1024 + xcol + 512]
                            tslot = dc % 2
                            tt = tsc[:, tslot * 512:(tslot + 1) * 512]
                            zdst = zc_ap[slot][dc]
                            E_("v", lambda v, a=xsl, d=tt: v.tensor_tensor(
                                d, a, mub[:], AluOpType.subtract),
                                reads=[bXH[dc][xcol // 512], bMUB], writes=[bTS[tslot]])
                            E_("v", lambda v, d=zdst, a=tt: v.tensor_tensor(
                                d, a, PS[6][:], AluOpType.mult),
                                reads=[bTS[tslot], bPS[6]], writes=[zcB[slot][dc]])
                        items.append(_z)
                    return items

                def att_prep(ci):
                    for it in att_prep_items(ci):
                        if it is not None:
                            it()
                    emit_qB(0, ci % 2)

                def emit_qB_stage(pr, slot, stage):
                    qe = 2 * (pr % 2)
                    bslot = pr % 2
                    if stage == 0:
                        for dc in range(NDC):
                            wE = wqb[:, dc * 1024 + (2 * pr) * 64: dc * 1024 + (2 * pr) * 64 + 64]
                            wO_ = wqb[:, dc * 1024 + (2 * pr + 1) * 64: dc * 1024 + (2 * pr + 1) * 64 + 64]
                            E_("t", lambda t, dc=dc, w=wE, z=zc_ap[slot][dc]: t.matmul(
                                PS[7][0:64, :], w, z, start=(dc == 0), stop=(dc == NDC - 1),
                                tile_position=(0, 0), skip_group_check=True),
                                reads=[bWQ[dc], zcB[slot][dc]], writes=[bPS[7]])
                            E_("t", lambda t, dc=dc, w=wO_, z=zc_ap[slot][dc]: t.matmul(
                                PS[7][64:128, :], w, z, start=(dc == 0), stop=(dc == NDC - 1),
                                tile_position=(0, 64), skip_group_check=True),
                                reads=[bWQ[dc], zcB[slot][dc]], writes=[bPS[7]])
                        E_("v", lambda v, d=qs_ap[qe]: v.tensor_copy(d[0:64, :], PS[7][0:64, :]),
                           reads=[bPS[7]], writes=[bQS[qe]])
                        E_("v", lambda v, d=qs_ap[qe + 1]: v.tensor_copy(d[64:128, :], PS[7][64:128, :]),
                           reads=[bPS[7]], writes=[bQS[qe + 1]])
                        E_("s", lambda s: s.activation(q2_ap, PS[7][:], ACT.Square),
                           reads=[bPS[7]], writes=[q2B])
                    elif stage == 1:
                        E_("t", lambda t: t.matmul(PS[7][0:64, :], c_ones64, q2_ap,
                                                   start=True, stop=True, tile_position=(0, 0)),
                           reads=[bCB, q2B], writes=[bPS[7]])
                    elif stage == 2:
                        E_("v", lambda v: v.tensor_reduce(brt[0:1, 0:1], PS[7][0:1, :],
                                                          AX.X, AluOpType.max),
                           reads=[bPS[7]], writes=[bROWT])
                        bcol = browb[0:1, bslot:bslot + 1]
                        E_("v", lambda v, d=bcol: v.scalar_tensor_tensor(
                            d, brt[0:1, 0:1], -0.0625, mk2[0:1, 1:2],
                            AluOpType.mult, AluOpType.add),
                            reads=[bROWT, bMK2], writes=[bBROWB])
                    else:
                        bsl = browb[:, bslot:bslot + 1]
                        E_("t", lambda t, s=bsl: t.matmul(
                            PS[7][0:64, 0:1], c_allones, s, start=True, stop=True,
                            tile_position=(0, 0)),
                            reads=[bBROWB, bCB], writes=[bPS[7]])
                        E_("t", lambda t, s=bsl: t.matmul(
                            PS[7][64:128, 0:1], c_allones, s, start=True, stop=True,
                            tile_position=(0, 64)),
                            reads=[bBROWB, bCB], writes=[bPS[7]])
                        bb = bbias[:, bslot:bslot + 1]
                        E_("v", lambda v, d=bb: v.tensor_copy(d, PS[7][:, 0:1]),
                           reads=[bPS[7]], writes=[bBB[bslot]])

                def emit_qB(pr, slot):
                    for st in range(4):
                        emit_qB_stage(pr, slot, st)

                def att_heads(ci, inject=(), next_qB=None, inject_rate=1):
                    inj = list(inject)
                    ii = [0]

                    def pop_inject():
                        if ii[0] < len(inj):
                            it = inj[ii[0]]
                            ii[0] += 1
                            if it is not None:
                                it()
                    C = half * 2 + ci
                    slot = ci % 2
                    ntk_full = 4 * C
                    ntiles = ntk_full + 4
                    first_sum = {0: True, 64: True}
                    for pr in range(NP):
                        bslot = pr % 2
                        bb = bbias[:, bslot:bslot + 1]
                        abank = 4
                        first_av = {0: True, 64: True}

                        def emit_avsums(tau, parity, n, qoff, tslot2):
                            hd = 2 * pr + parity
                            pap = zr[:, PB0 + tslot2 * 1024 + parity * 512:
                                     PB0 + tslot2 * 1024 + parity * 512 + n]
                            if parity == 0:
                                av_tp, sum_tp = (0, 0), (0, 64)
                                ar0, ar1, sr0, sr1 = 0, 64, 64, 128
                            else:
                                av_tp, sum_tp = (0, 64), (0, 0)
                                ar0, ar1, sr0, sr1 = 64, 128, 0, 64
                            vsl = vb[:, tau * 1024 + hd * 64: tau * 1024 + hd * 64 + 64]
                            E_("t", lambda t, d=PS[abank][ar0:ar1, qoff:qoff + n],
                                   w=vsl, p=pap, st=first_av[ar0], tp=av_tp: t.matmul(
                                d, w, p, start=st, stop=False, tile_position=tp,
                                skip_group_check=True),
                                reads=[bVB, pbB[tslot2]], writes=[bPS[abank]])
                            first_av[ar0] = False
                            E_("t", lambda t, d=PS[5][sr0:sr1, qoff:qoff + n],
                                   w=c_onescol[pr], p=pap, st=first_sum[sr0], tp=sum_tp: t.matmul(
                                d, w, p, start=st, stop=False, tile_position=tp,
                                skip_group_check=True),
                                reads=[bCB, pbB[tslot2]], writes=[bPS[5]])
                            first_sum[sr0] = False

                        pend = []
                        for tau in range(ntiles):
                            if tau < ntk_full:
                                n, qoff = 512, 0
                            else:
                                j = tau - ntk_full
                                n, qoff = 512 - 128 * j, 128 * j
                            tslot2 = tau % 2
                            cur = []
                            for parity in range(2):
                                kT = kb[:, pr * 2048 + tau * 128: pr * 2048 + tau * 128 + 64]
                                kBm = kb[:, pr * 2048 + tau * 128 + 64: pr * 2048 + tau * 128 + 128]
                                qe = 2 * (pr % 2) + parity
                                qsl = qs_ap[qe][:, qoff:qoff + n]
                                sb_ = 2 * tslot2 + parity
                                E_("t", lambda t, d=PS[sb_][0:64, 0:n], w=kT, q=qsl: t.matmul(
                                    d, w, q, start=True, stop=True, tile_position=(0, 0)),
                                    reads=[bKB, bQS[qe]], writes=[bPS[sb_]])
                                E_("t", lambda t, d=PS[sb_][64:128, 0:n], w=kBm, q=qsl: t.matmul(
                                    d, w, q, start=True, stop=True, tile_position=(0, 64)),
                                    reads=[bKB, bQS[qe]], writes=[bPS[sb_]])
                                cur.append((tau, parity, n, qoff, tslot2))
                            if ntk_full == 0 and n < 512:
                                for parity in range(2):
                                    sb_ = 2 * tslot2 + parity
                                    pp = zr[:, PB0 + tslot2 * 1024 + parity * 512:
                                            PB0 + tslot2 * 1024 + parity * 512 + n]
                                    E_("s", lambda s, d=pp, a=PS[sb_][:, 0:n], b=bb:
                                        s.activation(d, a, ACT.Exp, bias=b, scale=0.125),
                                        reads=[bPS[sb_], bBB[bslot]],
                                        writes=[pbB[tslot2]])
                            else:
                                pap = zr[:, PB0 + tslot2 * 1024: PB0 + tslot2 * 1024 + 512 + n]
                                E_("s", lambda s, d=pap, a=PSB[tslot2][:, 0:512 + n], b=bb: s.activation(
                                    d, a, ACT.Exp, bias=b, scale=0.125),
                                    reads=[bPS[2 * tslot2], bPS[2 * tslot2 + 1], bBB[bslot]],
                                    writes=[pbB[tslot2]])
                            if tau >= ntk_full:
                                for parity in range(2):
                                    pm = zr[:, PB0 + tslot2 * 1024 + parity * 512:
                                            PB0 + tslot2 * 1024 + parity * 512 + 128]
                                    E_("v", lambda v, d=pm: v.tensor_tensor(
                                        d, d, c_mask, AluOpType.mult),
                                        reads=[bCB], writes=[pbB[tslot2]])
                            for job in pend:
                                emit_avsums(*job)
                            pend = cur
                            if pr + 1 < NP:
                                if _NOQB:
                                    if tau == 0:
                                        emit_qB(pr + 1, slot)
                                elif tau < 4:
                                    emit_qB_stage(pr + 1, slot, tau)
                            elif next_qB is not None and tau < 4:
                                emit_qB_stage(0, next_qB, tau)
                            for _r in range(inject_rate):
                                pop_inject()
                        for job in pend:
                            emit_avsums(*job)
                        E_("v", lambda v, d=an_ap[pr], a=PS[abank][:]: v.tensor_copy(d, a),
                           reads=[bPS[abank]], writes=[anB[pr]])
                    while ii[0] < len(inj):
                        pop_inject()

                def epi_recips(ci):
                    def _rec(lo, hi):
                        def g(v):
                            with nc.allow_low_precision(reason="bf16 1/S scale for softmax (0.4%)"):
                                return v.reciprocal(invs[lo:hi, :], PS[5][lo:hi, :])
                        return g
                    E_("v", _rec(0, 8), reads=[bPS[5]], writes=[bINV])
                    E_("v", _rec(64, 72), reads=[bPS[5]], writes=[bINV])

                def epi_items(ci, ybks, ibks, lbank, defer):
                    xcol = ci * 512
                    slot = ci % 2
                    if defer:
                        y2ap, y2buf = xst[:, 512:1024], bXST[2]
                    else:
                        y2ap, y2buf = y2_ap, y2B
                    items = []
                    for pr in range(NP):
                        def _ia(pr=pr, ibank=ibks[pr % 2]):
                            E_("t", lambda t, d=PS[ibank][0:64, :]: t.matmul(
                                d, c_eselT[pr], invs[:], start=True, stop=True, tile_position=(0, 0)),
                                reads=[bINV, bCB], writes=[bPS[ibank]])
                            E_("t", lambda t, d=PS[ibank][64:128, :]: t.matmul(
                                d, c_eselB[pr], invs[:], start=True, stop=True, tile_position=(0, 64)),
                                reads=[bINV, bCB], writes=[bPS[ibank]])
                            E_("v", lambda v, d=an_ap[pr], i=PS[ibank][:]: v.tensor_tensor(
                                d, d, i, AluOpType.mult),
                                reads=[bPS[ibank]], writes=[anB[pr]])
                        items.append(_ia)
                    for d_ in range(NDC):
                        def _wo(d_=d_, ybk=ybks[d_ % 2]):
                            for pr in range(NP):
                                wT = wob[:, pr * 1024 + d_ * 128: pr * 1024 + d_ * 128 + 64]
                                wB = wob[:, pr * 1024 + d_ * 128 + 64: pr * 1024 + d_ * 128 + 128]
                                E_("t", lambda t, pr=pr, d=PS[ybk][0:64, :], w=wT, a=an_ap[pr]: t.matmul(
                                    d, w, a, start=(pr == 0), stop=(pr == NP - 1), tile_position=(0, 0), skip_group_check=True),
                                    reads=[bWO[pr], anB[pr]], writes=[bPS[ybk]])
                                E_("t", lambda t, pr=pr, d=PS[ybk][64:128, :], w=wB, a=an_ap[pr]: t.matmul(
                                    d, w, a, start=(pr == 0), stop=(pr == NP - 1), tile_position=(0, 64), skip_group_check=True),
                                    reads=[bWO[pr], anB[pr]], writes=[bPS[ybk]])
                            yb = zc_ap[slot][d_]
                            E_("v", lambda v, d=yb, y=PS[ybk][:]: v.tensor_tensor(
                                d, d, y, AluOpType.add),
                                reads=[bPS[ybk]], writes=[zcB[slot][d_]])
                            E_("s", lambda s, a=yb: s.activation(y2ap, a, ACT.Square),
                               reads=[zcB[slot][d_]], writes=[y2buf])
                            E_("t", lambda t, d_=d_, a=yb: t.matmul(
                                PS[lbank][0:64, :], c_ones64, a,
                                start=(d_ == 0), stop=(d_ == NDC - 1), tile_position=(0, 0), skip_group_check=True),
                                reads=[bCB, zcB[slot][d_]], writes=[bPS[lbank]])
                            E_("t", lambda t, d_=d_: t.matmul(
                                PS[lbank][64:128, :], c_ones64, y2ap,
                                start=(d_ == 0), stop=(d_ == NDC - 1), tile_position=(0, 64), skip_group_check=True),
                                reads=[bCB, y2buf], writes=[bPS[lbank]])
                        items.append(_wo)
                    items.append(lambda: rowmath_r(bPS[lbank], PS[lbank][0:1, :],
                                                   PS[lbank][64:65, :], LN_EPS_3, 0.0))
                    items.append(None)
                    def _mu3():
                        bcast_f32(lbank, rowa, bROWA)
                        E_("v", lambda v: v.tensor_copy(mub[:], PS[lbank][:]),
                           reads=[bPS[lbank]], writes=[bMUB])
                    items.append(_mu3)
                    items.append(None)
                    def _r3():
                        bcast_f32(lbank, rowb, bROWB)
                        E_("v", lambda v: v.tensor_copy(rub[:], PS[lbank][:]),
                           reads=[bPS[lbank]], writes=[bRUB])
                    items.append(_r3)
                    items.append(None)
                    for dp in range(4):
                        def _pb(dp=dp):
                            for d_ in (2 * dp, 2 * dp + 1):
                                tslot = d_ % 2
                                tt = tsc[:, tslot * 512:(tslot + 1) * 512]
                                yb = zc_ap[slot][d_]
                                E_("v", lambda v, a=yb, d=tt: v.tensor_tensor(
                                    d, a, mub[:], AluOpType.subtract),
                                    reads=[bMUB, zcB[slot][d_]], writes=[bTS[tslot]])
                                E_("v", lambda v, d=tt: v.tensor_tensor(
                                    d, d, rub[:], AluOpType.mult),
                                    reads=[bRUB], writes=[bTS[tslot]])
                                gcol = sgb[:, d_:d_ + 1]
                                bcol = sgb[:, 8 + d_:8 + d_ + 1]
                                E_("v", lambda v, d=tt, g=gcol, b=bcol: v.tensor_scalar(
                                    d, d, g, b, AluOpType.mult, AluOpType.add),
                                    reads=[bGB], writes=[bTS[tslot]])
                                xsl = xh[:, d_ * 1024 + xcol: d_ * 1024 + xcol + 512]
                                E_("v", lambda v, x=xsl, d=tt: v.tensor_tensor(
                                    x, x, d, AluOpType.add),
                                    reads=[bTS[tslot], bXH[d_][xcol // 512]], writes=[bXH[d_][xcol // 512]])
                        items.append(_pb)
                    return items

                def att_epilogue(ci):
                    epi_recips(ci)
                    for it in epi_items(ci, (4, 6), (5, 7), 5, False):
                        if it is not None:
                            it()

                def emit_guw_dma(hc):
                    for j, dw_ in ((0, dwg), (1, dwu)):
                        sl = (2 * hc + j) % 3
                        stg = wstag[:, sl * 1024:(sl + 1) * 1024]
                        E_("g", lambda g, d=stg, s=dw_.ap()[hc]: g.dma_start(d, s),
                           writes=[bWST[sl]], queue="qw%d" % sl)

                def emit_guw_cast(hc):
                    gsl = (2 * hc) % 3
                    usl = (2 * hc + 1) % 3
                    wsl = hc % 2
                    gb_ = wgbb[:, wsl * 1024:(wsl + 1) * 1024]
                    ub_ = wubb[:, wsl * 1024:(wsl + 1) * 1024]
                    E_("v", lambda v, d=gb_, s=wstag[:, gsl * 1024:(gsl + 1) * 1024]: v.tensor_copy(d, s),
                       reads=[bWST[gsl]], writes=[bWG[wsl]])
                    E_("s", lambda s, d=ub_, a=wstag[:, usl * 1024:(usl + 1) * 1024]: s.activation(d, a, ACT.Copy),
                       reads=[bWST[usl]], writes=[bWU[wsl]])

                def emit_wd_dma(hc):
                    sl = hc % 3
                    stg = wstag[:, sl * 1024:(sl + 1) * 1024]
                    E_("g", lambda g, d=stg, s=dwd.ap()[hc]: g.dma_start(d, s),
                       writes=[bWST[sl]], queue="qw%d" % sl)

                z3_ap = [zr[:, d * 512:(d + 1) * 512] for d in range(NDC)]

                def h_region(k):
                    if k < 8:
                        return 12288 + k * 512
                    if k < 16:
                        return 8192 + (k - 8) * 512
                    return 4096 + (k - 16) * 512
                h_ap = [zr[:, h_region(k): h_region(k) + 512] for k in range(NHC)]

                att_prep(0)
                if _NOPREP:
                    att_heads(0)
                    att_prep(1)
                else:
                    att_heads(0, inject=att_prep_items(1), next_qB=1, inject_rate=2)
                att_epilogue(0)
                z3B0 = rebind([f"z3a_{d}" for d in range(NDC)], zcB[0])
                emit_guw_dma(0)
                if _INJL < 1:
                    it0 = []
                else:
                    it0 = ([None] * 16 + ffn_stat_items(0, 6, True)
                           + ffn_zgen_items(0, 6, z3B0, z3_ap, True)
                           + [lambda: emit_guw_cast(0)])
                att_heads(1, inject=it0)
                _DEFER = (not _NODEF) and lim >= 3 and _INJL >= 2
                if _DEFER:
                    epi_recips(1)
                    epi1_gu = epi_items(1, (0, 2), (1, 1), 3, True)
                else:
                    att_epilogue(1)
                    epi1_gu = []

                if lim < 3:
                    break
                if _DEFER:
                    hB = (rebind([f"h{k}" for k in range(8)], [pbB[0], pbB[1], q2B])
                          + [None] * 16)
                else:
                    hB = rebind([f"h{k}" for k in range(NHC)], eb[8:27])
                z3B1 = None
                wkv_items = []
                if half == 1 and s_i == 0 and lim >= 99:
                    for _pc in range(16):
                        for _j, (_dsrc, _dstT, _dstB) in enumerate(
                                ((dwk, wqb, bWQ), (dwv, wob, bWO))):
                            def _ld(pc=_pc, dsrc=_dsrc, dstT=_dstT, dstB=_dstB, slot=_j % 2):
                                st = tsc[:, slot * 512:(slot + 1) * 512]
                                sap = dsrc.ap()[:, pc * 512:(pc + 1) * 512]
                                dst = dstT[:, pc * 512:(pc + 1) * 512]
                                E_("y", lambda y, d=st, s=sap: y.dma_start(d, s),
                                   writes=[bTS[slot]], queue="qt%d" % slot)
                                E_("v", lambda v, d=dst, s=st: v.tensor_copy(d, s),
                                   reads=[bTS[slot]], writes=[dstB[pc // 2]])
                            wkv_items.append(_ld)

                for tf in range(2):
                    xcol = tf * 512
                    if tf == 0:
                        z3Bl = z3B0
                        if _INJL < 1:
                            for it in (ffn_stat_items(0, 6, True)
                                       + ffn_zgen_items(0, 6, z3B0, z3_ap, True)):
                                if it is not None:
                                    it()
                            emit_guw_cast(0)
                        if _INJL >= 2:
                            _gfull = ffn_stat_items(1, 2, False, use_qsb=True)
                            _gn = int(_os.environ.get("KMK_INJN", "99"))
                            gu_inject = epi1_gu + _gfull[:_gn]
                            gu_rest = _gfull[_gn:]
                        else:
                            gu_inject = []
                            gu_rest = []
                    else:
                        z3Bl = z3B1
                        gu_inject = []
                        emit_guw_dma(0)
                        emit_guw_cast(0)
                    gii = [0]

                    def pop_gu():
                        if gii[0] < len(gu_inject):
                            it = gu_inject[gii[0]]
                            gii[0] += 1
                            if it is not None:
                                it()
                    for hc in range(NHC):
                        if tf == 0 and _DEFER:
                            if hc == 8:
                                hB[8:16] = rebind([f"h{k}" for k in range(8, 16)], anB)
                            elif hc == 16:
                                hB[16:24] = rebind([f"h{k}" for k in range(16, 24)], zcB[1])
                        wsl = hc % 2
                        if hc + 1 < NHC:
                            emit_guw_dma(hc + 1)
                        gB = 4 + 2 * ((hc + 1) % 2)
                        uB = gB + 1
                        for dc in range(NDC):
                            wgsl = wgbb[:, wsl * 1024 + dc * 128: wsl * 1024 + (dc + 1) * 128]
                            wusl = wubb[:, wsl * 1024 + dc * 128: wsl * 1024 + (dc + 1) * 128]
                            E_("t", lambda t, dc=dc, w=wgsl, z=z3_ap[dc], d=PS[gB][:]: t.matmul(
                                d, w, z, start=(dc == 0), stop=(dc == NDC - 1)),
                                reads=[bWG[wsl], z3Bl[dc]], writes=[bPS[gB]])
                            E_("t", lambda t, dc=dc, w=wusl, z=z3_ap[dc], d=PS[uB][:]: t.matmul(
                                d, w, z, start=(dc == 0), stop=(dc == NDC - 1)),
                                reads=[bWU[wsl], z3Bl[dc]], writes=[bPS[uB]])
                        if hc + 1 < NHC:
                            emit_guw_cast(hc + 1)
                        pop_gu()
                        sg_ = xst[:, 1024 + (hc % 2) * 512: 1024 + (hc % 2 + 1) * 512]
                        E_("s", lambda s, d=sg_, a=PS[gB][:]: s.activation(d, a, ACT.Silu),
                           reads=[bPS[gB]], writes=[bXST[2 + 2 * (hc % 2) - 1 if False else 2 * (hc % 2) + 1]])
                        E_("v", lambda v, d=h_ap[hc], a=sg_, u=PS[uB][:]: v.tensor_tensor(
                            d, a, u, AluOpType.mult),
                            reads=[bXST[2 * (hc % 2) + 1], bPS[uB]], writes=[hB[hc]])
                        pop_gu()
                    while gii[0] < len(gu_inject):
                        pop_gu()
                    if tf == 0:
                        for it in gu_rest:
                            if it is not None:
                                it()
                        z3B1 = rebind([f"z3b_{d}" for d in range(NDC)], z3B0)
                        dn_inject = ffn_zgen_items(1, 0, z3B1, z3_ap, False) + wkv_items[:24]
                        if _INJL < 2:
                            for it in ffn_stat_items(1, 2, False):
                                if it is not None:
                                    it()
                        if _INJL < 3:
                            for it in dn_inject:
                                it()
                            dn_inject = []
                    else:
                        dn_inject = list(wkv_items[24:])
                    dii = [0]

                    def pop_dn():
                        if dii[0] < len(dn_inject):
                            it = dn_inject[dii[0]]
                            dii[0] += 1
                            it()
                    for hc in range(3):
                        emit_wd_dma(hc)
                    for hc in range(NHC):
                        sl = hc % 3
                        wsl = hc % 2
                        stg = wstag[:, sl * 1024:(sl + 1) * 1024]
                        db_ = wdbb[:, wsl * 1024:(wsl + 1) * 1024]
                        E_("s", lambda s, d=db_, a=stg: s.activation(
                            d, a, ACT.Copy, scale=1.0 / 127.0),
                            reads=[bWST[sl], bCF], writes=[bWD[wsl]])
                        if hc + 3 < NHC:
                            emit_wd_dma(hc + 3)
                        pop_dn()
                        pop_dn()
                        for d_ in range(NDC):
                            wdsl = wdbb[:, wsl * 1024 + d_ * 128: wsl * 1024 + (d_ + 1) * 128]
                            E_("t", lambda t, w=wdsl, h=h_ap[hc], d=PS[d_][:], hc=hc: t.matmul(
                                d, w, h, start=(hc == 0), stop=(hc == NHC - 1)),
                                reads=[bWD[wsl], hB[hc]], writes=[bPS[d_]])
                    if tf == 0:
                        emit_guw_dma(0)
                    for d_ in range(NDC):
                        xsl = xh[:, d_ * 1024 + xcol: d_ * 1024 + xcol + 512]
                        E_("v", lambda v, x=xsl, f=PS[d_][:]: v.tensor_tensor(
                            x, x, f, AluOpType.add),
                            reads=[bPS[d_], bXH[d_][tf]], writes=[bXH[d_][tf]])
                        odst = os_[:, d_ * 2048 + half * 1024 + tf * 512:
                                   d_ * 2048 + half * 1024 + tf * 512 + 512]
                        E_("g", lambda ge, d=odst, s=xsl: ge.dma_start(d, s),
                           reads=[bXH[d_][tf]], queue="qo" if tf == 0 else "qo1")
                        nstores[tf] += 1
                    if half == 0 and lim >= 4:
                        emit_xload(1, tf)
                if half == 0 and lim == 3:
                    break

        P.ops["g"].append(([Tok("qo", nstores[0] * 16, True),
                           Tok("qo1", nstores[1] * 16, True)],
                          None, Tok("g", -1, False), None))

        semval = P.finalize()

        def replay(eng_obj, ename):
            for waits, fn, tok, queue in P.ops[ename]:
                for t in waits:
                    wv = t.idx if t.queue else semval[(t.src, t.idx)]
                    assert wv is not None, (ename, t.src, t.idx)
                    eng_obj.wait_ge(SEMS[t.src], wv)
                if fn is None:
                    continue
                inst = fn(eng_obj)
                if queue is not None:
                    inst.then_inc(SEMS[tok.src], 16)
                elif (tok.src, tok.idx) in P.referenced:
                    inst.then_inc(SEMS[tok.src], 1)

        @block.tensor
        def _(tensor):
            replay(tensor, "t")

        @block.vector
        def _(vector):
            replay(vector, "v")

        @block.scalar
        def _(scalar):
            replay(scalar, "s")

        @block.gpsimd
        def _(gpsimd):
            replay(gpsimd, "g")

        @block.sync
        def _(sync):
            replay(sync, "y")

    return nc, {e: len(P.ops[e]) for e in P.ENGS}


_NC_CACHE = [None]


def _host_inputs(inputs):
    x = np.asarray(inputs["L0_attn_x"])
    wq = np.asarray(inputs["L0_attn_Wq"])
    wk = np.asarray(inputs["L0_attn_Wk"])
    wv = np.asarray(inputs["L0_attn_Wv"])
    wo = np.asarray(inputs["L0_attn_Wo"])
    gam = np.asarray(inputs["L0_attn_gamma"])
    bet = np.asarray(inputs["L0_attn_beta"])
    wg = np.asarray(inputs["L0_ffn_w_gate"])
    wu = np.asarray(inputs["L0_ffn_w_up"])
    wd = np.asarray(inputs["L0_ffn_w_down"])

    def whead(w):
        return np.ascontiguousarray(
            w.reshape(H, NDC, 128, E).transpose(2, 1, 0, 3).reshape(128, NDC * H * E))

    m = {}
    m["wq"] = whead(wq)
    m["wk"] = whead(wk)
    m["wv"] = whead(wv)
    m["wo"] = np.ascontiguousarray(wo.reshape(NP, 128, D).transpose(1, 0, 2).reshape(128, NP * D))
    gb = np.zeros((128, 16), dtype=np.float32)
    gb[:, 0:8] = gam.reshape(8, 128).T
    gb[:, 8:16] = bet.reshape(8, 128).T
    m["gb"] = gb
    m["wg"] = np.ascontiguousarray(
        wg.reshape(NHC, 128, NDC, 128).transpose(0, 3, 2, 1).reshape(NHC, 128, D))
    m["wu"] = np.ascontiguousarray(
        wu.reshape(NHC, 128, NDC, 128).transpose(0, 3, 2, 1).reshape(NHC, 128, D))
    m["wd"] = np.ascontiguousarray(wd.T.reshape(NHC, 128, D))

    cbv = np.zeros((128, 1792), dtype=np.float32)
    cbv[:, 0] = 1.0
    for j in range(8):
        cbv[:, 64 + j * 64 + j] = 1.0
    for j in range(8):
        cbv[64 + j, 576 + (j * 2) * 64: 576 + (j * 2) * 64 + 64] = 1.0
        cbv[j, 576 + (j * 2 + 1) * 64: 576 + (j * 2 + 1) * 64 + 64] = 1.0
    tri = np.triu(np.ones((128, 128), dtype=np.float32))
    cbv[:, 1600:1728] = tri
    cbv[:, 1728:1792] = 1.0
    m["cb"] = cbv
    cfv = np.zeros((128, 144), dtype=np.float32)
    cfv[0, 0:128] = 1.0
    _CONSTVALS = [1e-05, 0.001, -LN127, -0.0625, 0.0625, 0.125, -0.125,
                  1.0 / 1024.0, 1.0 / 127.0, 0.5, -0.5, 2.0, -1.0, 3.0]
    for _i, _cv in enumerate(_CONSTVALS):
        cfv[:, 128 + _i] = _cv
    m["cf"] = cfv

    per_core_x = []
    for c in range(NCORE):
        xsh = np.empty((2, 128, NDC * SEQ), dtype=np.float32)
        for si in range(2):
            xi = x[2 * c + si]
            xsh[si] = xi.T.reshape(NDC, 128, SEQ).transpose(1, 0, 2).reshape(128, NDC * SEQ)
        per_core_x.append(np.ascontiguousarray(xsh))
    return m, per_core_x


def kernel(**inputs):
    m, per_core_x = _host_inputs(inputs)
    if _NC_CACHE[0] is None:
        _NC_CACHE[0] = build_program()[0]
    nc = _NC_CACHE[0]
    in_maps = []
    for c in range(NCORE):
        im = dict(m)
        im["x"] = per_core_x[c]
        in_maps.append(im)
    res = run_bass_kernel_spmd(nc, in_maps, core_ids=list(range(NCORE)))
    out = np.empty((B_TOT, SEQ, D), dtype=np.float32)
    for c in range(NCORE):
        o = res.results[c]["o"]
        for si in range(2):
            out[2 * c + si] = o[si].reshape(128, NDC, SEQ).transpose(2, 1, 0).reshape(SEQ, D)
    return out


if __name__ == "__main__":
    nc, counts = build_program()
    print("instruction counts:", counts, "total:", sum(counts.values()))


# version 25
# speedup vs baseline: 1.0033x; 1.0033x over previous
import numpy as np
import ml_dtypes
import concourse.bass as bass
import concourse.mybir as mybir
from concourse.alu_op_type import AluOpType
from concourse.bass_utils import run_bass_kernel_spmd

F32 = mybir.dt.float32
BF16 = mybir.dt.bfloat16
I32 = mybir.dt.int32
ACT = mybir.ActivationFunctionType
AX = mybir.AxisListType

NCORE = 8
B_TOT = 16
SEQ = 2048
D = 1024
H = 16
E = 64
NDC = 8
NP = 8
CH = 512
NCHUNK = 4
NHC = 24
LN_EPS_PRE = 1e-5
LN_EPS_3 = 1e-3
LN127 = float(np.log(127.0))


class Tok:
    __slots__ = ("src", "idx", "queue")
    def __init__(self, src, idx, queue):
        self.src, self.idx, self.queue = src, idx, queue


class Buf:
    __slots__ = ("w", "r", "name", "psum")
    def __init__(self, name="", psum=False):
        self.w, self.r, self.name = None, [], name
        self.psum = psum


class Plan:
    ENGS = ("t", "v", "s", "g", "y")

    def __init__(self):
        self.ops = {e: [] for e in self.ENGS}
        self.cnt = {e: 0 for e in self.ENGS}
        self.qcnt = {}
        self.waited = {e: {} for e in self.ENGS}
        self.referenced = set()

    def emit(self, eng, fn, reads=(), writes=(), queue=None):
        deps = []
        for b in reads:
            if b.w is not None:
                deps.append((b.w, True))
            if b.psum and eng in ("v", "s"):
                for t in b.r:
                    if t.src in ("v", "s") and t.src != eng:
                        deps.append((t, False))
        qskip = []
        for b in writes:
            if b.w is not None:
                deps.append((b.w, False))
                if b.r:
                    qskip.append(b.w)
            for t in b.r:
                deps.append((t, False))
                qskip.append(t)
        for b in reads:
            if b.w is not None:
                qskip.append(b.w)
        waits = {}
        for t, is_raw in deps:
            if t.src == eng and not t.queue and eng == "t":
                continue
            if (queue is not None and t.queue and t.src == queue
                    and t in qskip):
                continue
            if t.queue:
                t = Tok(t.src, self.qcnt[t.src], True)
            cur = waits.get(t.src)
            if cur is None or t.idx > cur.idx:
                waits[t.src] = t
        final = []
        for src, t in waits.items():
            if self.waited[eng].get(src, -1) >= t.idx:
                continue
            self.waited[eng][src] = t.idx
            final.append(t)
            if not t.queue:
                self.referenced.add((t.src, t.idx))
        if queue is not None:
            self.qcnt[queue] = self.qcnt.get(queue, 0) + 16
            tok = Tok(queue, self.qcnt[queue], True)
        else:
            self.cnt[eng] += 1
            tok = Tok(eng, self.cnt[eng], False)
        self.ops[eng].append((final, fn, tok, queue))
        for b in reads:
            b.r.append(tok)
        for b in writes:
            b.w = tok
            b.r = []
        return tok

    def finalize(self):
        semval = {}
        for e in self.ENGS:
            c = 0
            for (waits, fn, tok, queue) in self.ops[e]:
                if tok.queue:
                    semval[(tok.src, tok.idx)] = tok.idx
                elif (tok.src, tok.idx) in self.referenced:
                    c += 1
                    semval[(tok.src, tok.idx)] = c
                else:
                    semval[(tok.src, tok.idx)] = None
        return semval


import os as _os
_LIM = _os.environ.get("KMK_LIM", "full")
_NOINJ = _os.environ.get("KMK_NOINJ", "0") == "1"
_INJL = 0 if _NOINJ else int(_os.environ.get("KMK_INJL", "3"))
_NOQB = _os.environ.get("KMK_NOQB", "0") == "1"
_NODEF = _os.environ.get("KMK_NODEF", "0") == "1"
_NOPREP = _os.environ.get("KMK_NOPREP", "1") == "1"
_LIMS = {"s": 0, "qkv_w": 0.2, "qkv_k": 0.4, "k2a": 0.45, "k2b": 0.5, "k2c": 0.55, "qkv_k2": 0.6, "qkv": 1, "attn0": 2, "half0": 3, "seq0": 4, "full": 99}


def build_program():
    nc = bass.Bass("TRN2", target_bir_lowering=False)

    dx = nc.dram_tensor("x", [2, 128, NDC * SEQ], F32, kind="ExternalInput")
    dwq = nc.dram_tensor("wq", [128, NDC * H * E], F32, kind="ExternalInput")
    dwk = nc.dram_tensor("wk", [128, NDC * H * E], F32, kind="ExternalInput")
    dwv = nc.dram_tensor("wv", [128, NDC * H * E], F32, kind="ExternalInput")
    dwo = nc.dram_tensor("wo", [128, NP * D], F32, kind="ExternalInput")
    dgb = nc.dram_tensor("gb", [128, 16], F32, kind="ExternalInput")
    dwg = nc.dram_tensor("wg", [NHC, 128, D], I32, kind="ExternalInput")
    dwu = nc.dram_tensor("wu", [NHC, 128, D], I32, kind="ExternalInput")
    dwd = nc.dram_tensor("wd", [NHC, 128, D], I32, kind="ExternalInput")
    dcb = nc.dram_tensor("cb", [128, 1792], F32, kind="ExternalInput")
    dcf = nc.dram_tensor("cf", [128, 144], F32, kind="ExternalInput")
    dout = nc.dram_tensor("o", [2, 128, NDC * SEQ], F32, kind="ExternalOutput")

    P = Plan()

    from contextlib import ExitStack
    with ExitStack() as _stack:
        ent = _stack.enter_context
        xh = ent(nc.sbuf_tensor("xh", [128, 8192], F32))
        zr = ent(nc.sbuf_tensor("zr", [128, 16384], BF16))
        kb = ent(nc.sbuf_tensor("kb", [128, NP * SEQ], BF16))
        vb = ent(nc.sbuf_tensor("vb", [128, 16 * 1024], BF16))
        wqb = ent(nc.sbuf_tensor("wqb", [128, 8192], BF16))
        wob = ent(nc.sbuf_tensor("wob", [128, 8192], BF16))
        wstag = ent(nc.sbuf_tensor("wstag", [128, 3072], I32))
        qsb_t = ent(nc.sbuf_tensor("qsb", [128, 2048], BF16))
        brt = ent(nc.sbuf_tensor("brt", [128, 2], F32))
        wgbb = ent(nc.sbuf_tensor("wgbb", [128, 2048], BF16))
        wubb = ent(nc.sbuf_tensor("wubb", [128, 2048], BF16))
        wdbb = ent(nc.sbuf_tensor("wdbb", [128, 2048], BF16))
        tsc = ent(nc.sbuf_tensor("tsc", [128, 1024], F32))
        invs = ent(nc.sbuf_tensor("invs", [128, 512], BF16))
        mub = ent(nc.sbuf_tensor("mub", [128, 512], BF16))
        rub = ent(nc.sbuf_tensor("rub", [128, 512], BF16))
        rowa = ent(nc.sbuf_tensor("rowa", [128, 512], F32))
        rowb = ent(nc.sbuf_tensor("rowb", [128, 512], F32))
        browb = ent(nc.sbuf_tensor("browb", [128, 8], BF16))
        bbias = ent(nc.sbuf_tensor("bbias", [128, 2], F32))
        k2m = ent(nc.sbuf_tensor("k2m", [128, 2], F32))
        mk2 = ent(nc.sbuf_tensor("mk2", [128, 2], F32))
        sgb = ent(nc.sbuf_tensor("sgb", [128, 16], F32))
        xst = ent(nc.sbuf_tensor("xst", [128, 2048], BF16))
        cbt = ent(nc.sbuf_tensor("cbt", [128, 1792], BF16))
        cft = ent(nc.sbuf_tensor("cft", [128, 144], F32))
        ps01 = ent(nc.psum_tensor("ps01", [128, 1024], F32))
        ps23 = ent(nc.psum_tensor("ps23", [128, 1024], F32))
        ps4 = ent(nc.psum_tensor("ps4", [128, 512], F32))
        ps5 = ent(nc.psum_tensor("ps5", [128, 512], F32))
        ps6 = ent(nc.psum_tensor("ps6", [128, 512], F32))
        ps7 = ent(nc.psum_tensor("ps7", [128, 512], F32))
        sem_t = ent(nc.semaphore("sem_t"))
        sem_v = ent(nc.semaphore("sem_v"))
        sem_s = ent(nc.semaphore("sem_s"))
        sem_g = ent(nc.semaphore("sem_g"))
        sem_y = ent(nc.semaphore("sem_y"))
        sem_qc = ent(nc.semaphore("sem_qc"))
        sem_qxh = [ent(nc.semaphore("sem_qxh%d" % i)) for i in range(8)]
        sem_qt0 = ent(nc.semaphore("sem_qt0"))
        sem_qt1 = ent(nc.semaphore("sem_qt1"))
        sem_qw0 = ent(nc.semaphore("sem_qw0"))
        sem_qw1 = ent(nc.semaphore("sem_qw1"))
        sem_qw2 = ent(nc.semaphore("sem_qw2"))
        sem_qw3 = ent(nc.semaphore("sem_qw3"))
        sem_qo = ent(nc.semaphore("sem_qo"))
        sem_qo1 = ent(nc.semaphore("sem_qo1"))
        with nc.Block() as blk0:
            @blk0.vector
            def _(v0):
                for _sm in ([sem_t, sem_v, sem_s, sem_g, sem_y, sem_qc,
                             sem_qt0, sem_qt1, sem_qw0, sem_qw1, sem_qw2, sem_qw3,
                             sem_qo, sem_qo1]
                            + list(sem_qxh)):
                    v0.sem_clear(_sm)
        block = ent(nc.Block())
        class _PsView:
            __slots__ = ("t", "coff")
            def __init__(self, t, coff):
                self.t, self.coff = t, coff
            def __getitem__(self, idx):
                if isinstance(idx, tuple):
                    rows, cols = idx
                else:
                    rows, cols = idx, slice(None)
                c0 = cols.start or 0
                c1 = 512 if cols.stop is None else cols.stop
                return self.t[rows, self.coff + c0: self.coff + c1]
        PS = [_PsView(ps01, 0), _PsView(ps01, 512), _PsView(ps23, 0), _PsView(ps23, 512),
              ps4, ps5, ps6, ps7]
        PSB = [ps01, ps23]
        SEMS = {"t": sem_t, "v": sem_v, "s": sem_s, "g": sem_g, "y": sem_y,
                "qc": sem_qc, "qt0": sem_qt0, "qt1": sem_qt1,
                "qw0": sem_qw0, "qw1": sem_qw1, "qw2": sem_qw2, "qw3": sem_qw3,
                "qo": sem_qo, "qo1": sem_qo1}
        for _i in range(8):
            SEMS["qxh%d" % _i] = sem_qxh[_i]

        c_ones64 = cbt[:, 0:64]
        c_onescol = [cbt[:, 64 + j * 64: 128 + j * 64] for j in range(8)]
        c_eselT = [cbt[:, 576 + (j * 2) * 64: 576 + (j * 2) * 64 + 64] for j in range(8)]
        c_eselB = [cbt[:, 576 + (j * 2 + 1) * 64: 576 + (j * 2 + 1) * 64 + 64] for j in range(8)]
        c_mask = cbt[:, 1600:1728]
        c_allones = cbt[:, 1728:1792]
        c_onesrow0 = cft[:, 0:128]
        _CONSTVALS = [1e-05, 0.001, -LN127, -0.0625, 0.0625, 0.125, -0.125,
                      1.0 / 1024.0, 1.0 / 127.0, 0.5, -0.5, 2.0, -1.0, 3.0]
        for _i, _cv in enumerate(_CONSTVALS):
            nc.const_aps.register(F32, float(_cv), cft[:, 128 + _i:129 + _i])

        bXH = [[Buf(f"xh{d}_{p}") for p in range(2)] for d in range(NDC)]
        bKB, bVB = Buf("kb"), Buf("vb")
        bWQ = [Buf(f"wq{i}") for i in range(8)]
        bWO = [Buf(f"wo{i}") for i in range(8)]
        bPS = [Buf(f"ps{i}", psum=True) for i in range(8)]
        bWST = [Buf("wst0"), Buf("wst1"), Buf("wst2")]
        bQS = [Buf(f"qs{i}") for i in range(4)]
        bWG = [Buf("wg0"), Buf("wg1")]
        bWU = [Buf("wu0"), Buf("wu1")]
        bWD = [Buf("wd0"), Buf("wd1")]
        bTS = [Buf("ts0"), Buf("ts1")]
        bINV = Buf("invs")
        bROWA, bROWB, bROWT = Buf("rowa"), Buf("rowb"), Buf("rowt")
        bMUB = Buf("mub")
        bRUB = Buf("rub")
        bBROWB, bK2, bMK2, bGB = Buf("browb"), Buf("k2m"), Buf("mk2"), Buf("sgb")
        bBB = [Buf("bb0"), Buf("bb1")]
        bXST = [Buf("xst0"), Buf("xst1"), Buf("xst2"), Buf("xst3")]
        bCB, bCF = Buf("cb"), Buf("cf")

        zr_era = [[]]

        def new_era(bufnames):
            old_toks = []
            for b in zr_era[0]:
                if b.w is not None:
                    old_toks.append(b.w)
                old_toks.extend(b.r)
            out = []
            for n in bufnames:
                b = Buf(n)
                b.r = list(old_toks)
                out.append(b)
            zr_era[0] = list(out)
            return out

        def rebind(bufnames, old):
            toks = []
            for b in old:
                if b.w is not None:
                    toks.append(b.w)
                toks.extend(b.r)
            out = []
            for n in bufnames:
                b = Buf(n)
                b.r = list(toks)
                out.append(b)
            for b in old:
                zr_era[0].remove(b)
            zr_era[0].extend(out)
            return out

        E_ = P.emit

        E_("y", lambda y: y.dma_start(tsc[:], dcb.ap()[:, 0:1024]), writes=[bTS[0], bTS[1]], queue="qc")
        E_("y", lambda y: y.dma_start(cft[:], dcf.ap()), writes=[bCF], queue="qc")
        E_("y", lambda y: y.dma_start(sgb[:], dgb.ap()), writes=[bGB], queue="qc")
        E_("v", lambda v: v.tensor_copy(cbt[:, 0:1024], tsc[:]), reads=[bTS[0], bTS[1], bCF, bGB], writes=[bCB])
        E_("y", lambda y: y.dma_start(tsc[:, 0:768], dcb.ap()[:, 1024:1792]), writes=[bTS[0], bTS[1]], queue="qc")
        E_("v", lambda v: v.tensor_copy(cbt[:, 1024:1792], tsc[:, 0:768]), reads=[bTS[0], bTS[1]], writes=[bCB])
        E_("v", lambda v: v.memset(invs[:], 0.0), writes=[bINV])
        E_("v", lambda v: v.memset(qsb_t[:], 0.0), writes=list(bQS))
        E_("v", lambda v: v.memset(rowa[:], 0.0), writes=[bROWA])
        E_("v", lambda v: v.memset(rowb[:], 0.0), writes=[bROWB])
        E_("v", lambda v: v.memset(browb[:], 0.0), writes=[bBROWB])

        for _k in range(8):
            E_("t", lambda t, d=PS[_k][:, 0:512]: t.matmul(
                d, c_onesrow0, rowa[:, 0:512], start=True, stop=True),
                reads=[bROWA, bCF], writes=[bPS[_k]])

        nstores = [0, 0]

        def rowmath_r(stat_buf, s1_ap, s2_ap, eps, extra_bias):
            a1 = rowa[0:1, 0:512]
            t0 = tsc[0:1, 0:512]
            t32 = tsc[0:1, 512:1024]
            t64 = t0
            t96 = t32
            b1 = rowb[0:1, 0:512]
            E_("s", lambda s, a=s1_ap, d=a1: s.activation(d, a, ACT.Copy, scale=1.0 / 1024.0),
               reads=[stat_buf], writes=[bROWA])
            E_("s", lambda s, a=s2_ap, d=t0: s.activation(d, a, ACT.Copy, scale=1.0 / 1024.0),
               reads=[stat_buf], writes=[bTS[0]])
            E_("v", lambda v, a=a1, d=t32: v.tensor_tensor(d, a, a, AluOpType.mult),
               reads=[bROWA], writes=[bTS[1]])
            E_("v", lambda v, a=t0, b=t32, d=t64: v.tensor_tensor(d, a, b, AluOpType.subtract),
               reads=[bTS[1], bTS[0]], writes=[bTS[0]])
            E_("s", lambda s, a=t64, d=t96, eps=eps: s.activation(d, a, ACT.Ln, bias=float(eps), scale=1.0),
               reads=[bTS[0], bCF], writes=[bTS[1]])
            E_("s", lambda s, a=t96, d=b1, xb=extra_bias: s.activation(d, a, ACT.Exp, bias=float(xb), scale=-0.5),
               reads=[bTS[1], bCF], writes=[bROWB])

        def bcast_f32(pidx, src_t, src_buf):
            dst = PS[pidx][:, 0:512]
            src = src_t[:, 0:512]
            E_("t", lambda t, d=dst, s=src: t.matmul(d, c_onesrow0, s, start=True, stop=True),
               reads=[src_buf, bCF], writes=[bPS[pidx]])

        def _stat_slots(sl, alt):
            if alt:
                return xst[:, 0:512], bXST[0], xst[:, 512:1024], bXST[2]
            xb = xst[:, sl * 512:(sl + 1) * 512]
            bxb = bXST[2 * sl]
            x2 = xst[:, 1024 + sl * 512: 1024 + (sl + 1) * 512]
            bx2 = bXST[2 * sl + 1]
            return xb, bxb, x2, bx2

        def stats_piece_ve(xsl, xbuf, sl, use_qsb=False):
            xb, bxb, x2, bx2 = _stat_slots(sl, use_qsb)
            E_("v", lambda v, a=xsl, d=xb: v.tensor_copy(d, a),
               reads=[xbuf], writes=[bxb])
            E_("s", lambda s, a=xsl, d=x2: s.activation(d, a, ACT.Square),
               reads=[xbuf], writes=[bx2])

        def stats_piece_mm(dc, sl, pidx, use_qsb=False):
            xb, bxb, x2, bx2 = _stat_slots(sl, use_qsb)
            E_("t", lambda t, dc=dc, d=PS[pidx][0:64, :], a=xb: t.matmul(
                d, c_ones64, a, start=(dc == 0), stop=(dc == NDC - 1), tile_position=(0, 0), skip_group_check=True),
                reads=[bxb, bCB], writes=[bPS[pidx]])
            E_("t", lambda t, dc=dc, d=PS[pidx][64:128, :], a=x2: t.matmul(
                d, c_ones64, a, start=(dc == 0), stop=(dc == NDC - 1), tile_position=(0, 64),
                skip_group_check=True),
                reads=[bx2, bCB], writes=[bPS[pidx]])

        def stats_chunk(xcols, pidx, xh_bufs):
            for dc, (xsl, xbuf) in enumerate(xcols):
                sl = dc % 2
                stats_piece_ve(xsl, xbuf, sl)
                stats_piece_mm(dc, sl, pidx)

        def ffn_stat_items(tf, bank, psum_r, use_qsb=False):
            xcols = [(xh[:, dc * 1024 + tf * 512: dc * 1024 + tf * 512 + 512], bXH[dc][tf])
                     for dc in range(NDC)]
            items = []
            for dc, (xsl, xbuf) in enumerate(xcols):
                sl = dc % 2
                items.append(lambda xsl=xsl, xbuf=xbuf, sl=sl: stats_piece_ve(xsl, xbuf, sl, use_qsb))
                items.append(lambda dc=dc, sl=sl: stats_piece_mm(dc, sl, bank, use_qsb))
            items.append(lambda: rowmath_r(bPS[bank], PS[bank][0:1, :], PS[bank][64:65, :],
                                           LN_EPS_PRE, -LN127))
            items.append(None)
            def _mu():
                bcast_f32(bank, rowa, bROWA)
                E_("v", lambda v: v.tensor_copy(mub[:], PS[bank][:]),
                   reads=[bPS[bank]], writes=[bMUB])
            items.append(_mu)
            items.append(None)
            items.append(lambda: bcast_f32(bank, rowb, bROWB))
            if not psum_r:
                items.append(lambda: E_("v", lambda v: v.tensor_copy(rub[:], PS[bank][:]),
                                        reads=[bPS[bank]], writes=[bRUB]))
            items.append(None)
            return items

        def ffn_zgen_items(tf, bank, z3Bl, z3_ap_l, psum_r):
            items = []
            for dc in range(NDC):
                xsl = xh[:, dc * 1024 + tf * 512: dc * 1024 + tf * 512 + 512]
                xbuf = bXH[dc][tf]
                tslot = dc % 2
                tt = tsc[:, tslot * 512:(tslot + 1) * 512]
                dap = z3_ap_l[dc]
                dbuf = z3Bl[dc]
                def _z(xsl=xsl, xbuf=xbuf, tt=tt, dap=dap, dbuf=dbuf, tslot=tslot):
                    E_("v", lambda v, a=xsl, d=tt: v.tensor_tensor(
                        d, a, mub[:], AluOpType.subtract),
                        reads=[xbuf, bMUB], writes=[bTS[tslot]])
                    if psum_r:
                        E_("v", lambda v, a=tt, d=dap: v.tensor_tensor(
                            d, a, PS[bank][:], AluOpType.mult),
                            reads=[bTS[tslot], bPS[bank]], writes=[dbuf])
                    else:
                        E_("v", lambda v, a=tt, d=dap: v.tensor_tensor(
                            d, a, rub[:], AluOpType.mult),
                            reads=[bTS[tslot], bRUB], writes=[dbuf])
                items.append(_z)
            return items

        def zgen(xcols, mu_pidx, r_pidx, dsts):
            for dc, (xsl, xbuf) in enumerate(xcols):
                tslot = dc % 2
                tt = tsc[:, tslot * 512:(tslot + 1) * 512]
                dap, dbufs = dsts[dc]
                E_("v", lambda v, a=xsl, d=tt, m=PS[mu_pidx][:]: v.tensor_tensor(
                    d, a, m, AluOpType.subtract),
                    reads=[xbuf, bPS[mu_pidx]], writes=[bTS[tslot]])
                E_("v", lambda v, a=tt, d=dap, r=PS[r_pidx][:]: v.tensor_tensor(
                    d, a, r, AluOpType.mult),
                    reads=[bTS[tslot], bPS[r_pidx]], writes=dbufs)

        def load_weights_bf16(dsrc, dstT, dstB):
            for pc in range(16):
                slot = pc % 2
                st = tsc[:, slot * 512:(slot + 1) * 512]
                src = dsrc.ap()[:, pc * 512:(pc + 1) * 512]
                dst = dstT[:, pc * 512:(pc + 1) * 512]
                E_("y", lambda y, d=st, s=src: y.dma_start(d, s),
                   writes=[bTS[slot]], queue="qt%d" % slot)
                E_("v", lambda v, d=dst, s=st: v.tensor_copy(d, s),
                   reads=[bTS[slot]], writes=[dstB[pc // 2]])

        def load_weights_bf16_pair(jobs):
            for pc in range(16):
                for j, (dsrc, dstT, dstB) in enumerate(jobs):
                    slot = j % 2
                    st = tsc[:, slot * 512:(slot + 1) * 512]
                    src = dsrc.ap()[:, pc * 512:(pc + 1) * 512]
                    dst = dstT[:, pc * 512:(pc + 1) * 512]
                    E_("y", lambda y, d=st, s=src: y.dma_start(d, s),
                       writes=[bTS[slot]], queue="qt%d" % slot)
                    E_("v", lambda v, d=dst, s=st: v.tensor_copy(d, s),
                       reads=[bTS[slot]], writes=[dstB[pc // 2]])

        lim = _LIMS[_LIM]
        for s_i in range(2):
            if s_i == 1 and lim < 99:
                break
            xs = dx.ap()[s_i]
            os_ = dout.ap()[s_i]

            if s_i == 0:
                load_weights_bf16_pair([(dwk, wqb, bWQ), (dwv, wob, bWO)])
            z1B = new_era([f"z1_{d}" for d in range(NDC)])
            for half in range(2):
                hc0 = half * 1024
                for dc in range(NDC):
                    for pc_ in range(2):
                        dst = xh[:, dc * 1024 + pc_ * 512: dc * 1024 + pc_ * 512 + 512]
                        src = xs[:, dc * 2048 + hc0 + pc_ * 512: dc * 2048 + hc0 + pc_ * 512 + 512]
                        E_("y", lambda y, d=dst, s=src: y.dma_start(d, s),
                           writes=[bXH[dc][pc_]], queue="qxh%d" % dc)
                for pc in range(2):
                    xcols = [(xh[:, dc * 1024 + pc * 512: dc * 1024 + pc * 512 + 512], bXH[dc][pc])
                             for dc in range(NDC)]
                    stats_chunk(xcols, 2, bXH)
                    rowmath_r(bPS[2], PS[2][0:1, :], PS[2][64:65, :], LN_EPS_PRE, 0.0)
                    bcast_f32(5, rowa, bROWA)
                    bcast_f32(4, rowb, bROWB)
                    dsts = []
                    for dc in range(NDC):
                        col = dc * 2048 + hc0 + pc * 512
                        dsts.append((zr[:, col:col + 512], [z1B[dc]]))
                    zgen(xcols, 5, 4, dsts)

            if lim < 0.2:
                break
            E_("v", lambda v: v.memset(k2m[:, 0:1], -1.0), writes=[bK2])
            if lim < 0.4:
                break
            for pr in range(NP):
                for piece in range(4):
                    pq = PS[piece % 2]
                    bpq = bPS[piece % 2]
                    for dc in range(NDC):
                        zsl = zr[:, dc * 2048 + piece * 512: dc * 2048 + piece * 512 + 512]
                        wE = wqb[:, dc * 1024 + (2 * pr) * 64: dc * 1024 + (2 * pr) * 64 + 64]
                        wO_ = wqb[:, dc * 1024 + (2 * pr + 1) * 64: dc * 1024 + (2 * pr + 1) * 64 + 64]
                        E_("t", lambda t, dc=dc, d=pq[0:64, :], w=wE, z=zsl: t.matmul(
                            d, w, z, start=(dc == 0), stop=(dc == NDC - 1), tile_position=(0, 0), skip_group_check=True),
                            reads=[bWQ[dc], z1B[dc]], writes=[bpq])
                        E_("t", lambda t, dc=dc, d=pq[64:128, :], w=wO_, z=zsl: t.matmul(
                            d, w, z, start=(dc == 0), stop=(dc == NDC - 1), tile_position=(0, 64), skip_group_check=True),
                            reads=[bWQ[dc], z1B[dc]], writes=[bpq])
                    kdst = kb[:, pr * 2048 + piece * 512: pr * 2048 + piece * 512 + 512]
                    E_("v", lambda v, d=kdst, a=pq[:]: v.tensor_copy(d, a),
                       reads=[bpq], writes=[bKB])
                    if lim < 0.45:
                        continue
                    ksq = xst[:, 0:512]
                    E_("s", lambda s, a=pq[:], d=ksq: s.activation(d, a, ACT.Square),
                       reads=[bpq], writes=[bXST[0]])
                    if lim < 0.5:
                        continue
                    E_("t", lambda t, a=ksq: t.matmul(
                        PS[6][0:64, :], c_ones64, a, start=True, stop=True,
                        tile_position=(0, 0)),
                        reads=[bXST[0], bCB], writes=[bPS[6]])
                    if lim < 0.55:
                        continue
                    E_("v", lambda v: v.tensor_reduce(
                        k2m[0:1, 1:2], PS[6][0:1, :], AX.X, AluOpType.max),
                        reads=[bPS[6]], writes=[bK2])
                    if lim < 0.6:
                        continue
                    E_("v", lambda v: v.tensor_tensor(
                        k2m[0:1, 0:1], k2m[0:1, 0:1], k2m[0:1, 1:2], AluOpType.max),
                        reads=[bK2], writes=[bK2])
            load_weights_bf16(dwq, wqb, bWQ)
            if lim < 1:
                break
            E_("v", lambda v: v.tensor_scalar(
                mk2[0:1, 1:2], k2m[0:1, 0:1], -0.0625, 0.0, AluOpType.mult, AluOpType.add),
                reads=[bK2, bCF], writes=[bMK2])

            for tk in range(16):
                for hh in range(2):
                    pv = PS[4 + hh]
                    bpv = bPS[4 + hh]
                    for dc in range(NDC):
                        zsl = zr[:, dc * 2048 + tk * 128: dc * 2048 + tk * 128 + 128]
                        wsl = wob[:, dc * 1024 + hh * 512: dc * 1024 + hh * 512 + 512]
                        E_("t", lambda t, dc=dc, d=pv[:], z=zsl, w=wsl: t.matmul(
                            d, z, w, start=(dc == 0), stop=(dc == NDC - 1)),
                            reads=[bWO[dc], z1B[dc]], writes=[bpv])
                    vdst = vb[:, tk * 1024 + hh * 512: tk * 1024 + hh * 512 + 512]
                    E_("v", lambda v, d=vdst, a=pv[:]: v.tensor_copy(d, a),
                       reads=[bpv], writes=[bVB])

            load_weights_bf16(dwo, wob, bWO)

            if lim < 2:
                break
            def emit_xload(half, pc_):
                for dc in range(NDC):
                    dst = xh[:, dc * 1024 + pc_ * 512: dc * 1024 + pc_ * 512 + 512]
                    src = xs[:, dc * 2048 + half * 1024 + pc_ * 512:
                              dc * 2048 + half * 1024 + pc_ * 512 + 512]
                    E_("y", lambda y, d=dst, s=src: y.dma_start(d, s),
                       writes=[bXH[dc][pc_]], queue="qxh%d" % dc)

            for half in range(2):
                if half == 0:
                    for pc_ in range(2):
                        emit_xload(half, pc_)

                names = ([f"zcA{d}" for d in range(NDC)] + [f"zcB{d}" for d in range(NDC)]
                         + [f"an{p}" for p in range(NP)]
                         + ["pb0", "pb1", "qy2"])
                eb = new_era(names)
                zcB = [eb[0:8], eb[8:16]]
                anB = eb[16:24]
                pbB = eb[24:26]; q2B = eb[26]; y2B = eb[26]
                ZC0, AN0, PB0, Q20 = 0, 8192, 12288, 14336
                Y20 = Q20
                zc_ap = [[zr[:, ZC0 + sl * 4096 + d * 512: ZC0 + sl * 4096 + d * 512 + 512]
                          for d in range(NDC)] for sl in range(2)]
                an_ap = [zr[:, AN0 + p * 512: AN0 + p * 512 + 512] for p in range(NP)]
                qs_ap = [qsb_t[:, i * 512:(i + 1) * 512] for i in range(4)]
                q2_ap = zr[:, Q20:Q20 + 512]
                y2_ap = zr[:, Y20:Y20 + 512]

                def att_prep_items(ci):
                    xcol = ci * 512
                    slot = ci % 2
                    xcols = [(xh[:, dc * 1024 + xcol: dc * 1024 + xcol + 512], bXH[dc][ci])
                             for dc in range(NDC)]
                    items = []
                    for dc, (xsl, xbuf) in enumerate(xcols):
                        sl = dc % 2
                        items.append(lambda xsl=xsl, xbuf=xbuf, sl=sl: stats_piece_ve(xsl, xbuf, sl))
                        items.append(lambda dc=dc, sl=sl: stats_piece_mm(dc, sl, 6))
                    items.append(lambda: rowmath_r(bPS[6], PS[6][0:1, :], PS[6][64:65, :],
                                                   LN_EPS_PRE, 0.0))
                    items.append(None)
                    def _mu1():
                        bcast_f32(6, rowa, bROWA)
                        E_("v", lambda v: v.tensor_copy(mub[:], PS[6][:]),
                           reads=[bPS[6]], writes=[bMUB])
                    items.append(_mu1)
                    items.append(None)
                    items.append(lambda: bcast_f32(6, rowb, bROWB))
                    items.append(None)
                    for dc in range(NDC):
                        def _z(dc=dc):
                            xsl = xh[:, dc * 1024 + xcol: dc * 1024 + xcol + 512]
                            tslot = dc % 2
                            tt = tsc[:, tslot * 512:(tslot + 1) * 512]
                            zdst = zc_ap[slot][dc]
                            E_("v", lambda v, a=xsl, d=tt: v.tensor_tensor(
                                d, a, mub[:], AluOpType.subtract),
                                reads=[bXH[dc][xcol // 512], bMUB], writes=[bTS[tslot]])
                            E_("v", lambda v, d=zdst, a=tt: v.tensor_tensor(
                                d, a, PS[6][:], AluOpType.mult),
                                reads=[bTS[tslot], bPS[6]], writes=[zcB[slot][dc]])
                        items.append(_z)
                    return items

                def att_prep(ci):
                    for it in att_prep_items(ci):
                        if it is not None:
                            it()
                    emit_qB(0, ci % 2)

                def emit_qB_stage(pr, slot, stage):
                    qe = 2 * (pr % 2)
                    bslot = pr % 2
                    if stage == 0:
                        for dc in range(NDC):
                            wE = wqb[:, dc * 1024 + (2 * pr) * 64: dc * 1024 + (2 * pr) * 64 + 64]
                            wO_ = wqb[:, dc * 1024 + (2 * pr + 1) * 64: dc * 1024 + (2 * pr + 1) * 64 + 64]
                            E_("t", lambda t, dc=dc, w=wE, z=zc_ap[slot][dc]: t.matmul(
                                PS[7][0:64, :], w, z, start=(dc == 0), stop=(dc == NDC - 1),
                                tile_position=(0, 0), skip_group_check=True),
                                reads=[bWQ[dc], zcB[slot][dc]], writes=[bPS[7]])
                            E_("t", lambda t, dc=dc, w=wO_, z=zc_ap[slot][dc]: t.matmul(
                                PS[7][64:128, :], w, z, start=(dc == 0), stop=(dc == NDC - 1),
                                tile_position=(0, 64), skip_group_check=True),
                                reads=[bWQ[dc], zcB[slot][dc]], writes=[bPS[7]])
                        E_("v", lambda v, d=qs_ap[qe]: v.tensor_copy(d[0:64, :], PS[7][0:64, :]),
                           reads=[bPS[7]], writes=[bQS[qe]])
                        E_("v", lambda v, d=qs_ap[qe + 1]: v.tensor_copy(d[64:128, :], PS[7][64:128, :]),
                           reads=[bPS[7]], writes=[bQS[qe + 1]])
                        E_("s", lambda s: s.activation(q2_ap, PS[7][:], ACT.Square),
                           reads=[bPS[7]], writes=[q2B])
                    elif stage == 1:
                        E_("t", lambda t: t.matmul(PS[7][0:64, :], c_ones64, q2_ap,
                                                   start=True, stop=True, tile_position=(0, 0)),
                           reads=[bCB, q2B], writes=[bPS[7]])
                    elif stage == 2:
                        E_("v", lambda v: v.tensor_reduce(brt[0:1, 0:1], PS[7][0:1, :],
                                                          AX.X, AluOpType.max),
                           reads=[bPS[7]], writes=[bROWT])
                        bcol = browb[0:1, bslot:bslot + 1]
                        E_("v", lambda v, d=bcol: v.scalar_tensor_tensor(
                            d, brt[0:1, 0:1], -0.0625, mk2[0:1, 1:2],
                            AluOpType.mult, AluOpType.add),
                            reads=[bROWT, bMK2], writes=[bBROWB])
                    else:
                        bsl = browb[:, bslot:bslot + 1]
                        E_("t", lambda t, s=bsl: t.matmul(
                            PS[7][0:64, 0:1], c_allones, s, start=True, stop=True,
                            tile_position=(0, 0)),
                            reads=[bBROWB, bCB], writes=[bPS[7]])
                        E_("t", lambda t, s=bsl: t.matmul(
                            PS[7][64:128, 0:1], c_allones, s, start=True, stop=True,
                            tile_position=(0, 64)),
                            reads=[bBROWB, bCB], writes=[bPS[7]])
                        bb = bbias[:, bslot:bslot + 1]
                        E_("v", lambda v, d=bb: v.tensor_copy(d, PS[7][:, 0:1]),
                           reads=[bPS[7]], writes=[bBB[bslot]])

                def emit_qB(pr, slot):
                    for st in range(4):
                        emit_qB_stage(pr, slot, st)

                def att_heads(ci, inject=(), next_qB=None, inject_rate=1):
                    inj = list(inject)
                    ii = [0]

                    def pop_inject():
                        if ii[0] < len(inj):
                            it = inj[ii[0]]
                            ii[0] += 1
                            if it is not None:
                                it()
                    C = half * 2 + ci
                    slot = ci % 2
                    ntk_full = 4 * C
                    ntiles = ntk_full + 4
                    first_sum = {0: True, 64: True}
                    for pr in range(NP):
                        bslot = pr % 2
                        bb = bbias[:, bslot:bslot + 1]
                        abank = 4
                        first_av = {0: True, 64: True}

                        def emit_avsums(tau, parity, n, qoff, tslot2):
                            hd = 2 * pr + parity
                            pap = zr[:, PB0 + tslot2 * 1024 + parity * 512:
                                     PB0 + tslot2 * 1024 + parity * 512 + n]
                            if parity == 0:
                                av_tp, sum_tp = (0, 0), (0, 64)
                                ar0, ar1, sr0, sr1 = 0, 64, 64, 128
                            else:
                                av_tp, sum_tp = (0, 64), (0, 0)
                                ar0, ar1, sr0, sr1 = 64, 128, 0, 64
                            vsl = vb[:, tau * 1024 + hd * 64: tau * 1024 + hd * 64 + 64]
                            E_("t", lambda t, d=PS[abank][ar0:ar1, qoff:qoff + n],
                                   w=vsl, p=pap, st=first_av[ar0], tp=av_tp: t.matmul(
                                d, w, p, start=st, stop=False, tile_position=tp,
                                skip_group_check=True),
                                reads=[bVB, pbB[tslot2]], writes=[bPS[abank]])
                            first_av[ar0] = False
                            E_("t", lambda t, d=PS[5][sr0:sr1, qoff:qoff + n],
                                   w=c_onescol[pr], p=pap, st=first_sum[sr0], tp=sum_tp: t.matmul(
                                d, w, p, start=st, stop=False, tile_position=tp,
                                skip_group_check=True),
                                reads=[bCB, pbB[tslot2]], writes=[bPS[5]])
                            first_sum[sr0] = False

                        pend = []
                        for tau in range(ntiles):
                            if tau < ntk_full:
                                n, qoff = 512, 0
                            else:
                                j = tau - ntk_full
                                n, qoff = 512 - 128 * j, 128 * j
                            tslot2 = tau % 2
                            cur = []
                            for parity in range(2):
                                kT = kb[:, pr * 2048 + tau * 128: pr * 2048 + tau * 128 + 64]
                                kBm = kb[:, pr * 2048 + tau * 128 + 64: pr * 2048 + tau * 128 + 128]
                                qe = 2 * (pr % 2) + parity
                                qsl = qs_ap[qe][:, qoff:qoff + n]
                                sb_ = 2 * tslot2 + parity
                                E_("t", lambda t, d=PS[sb_][0:64, 0:n], w=kT, q=qsl: t.matmul(
                                    d, w, q, start=True, stop=True, tile_position=(0, 0)),
                                    reads=[bKB, bQS[qe]], writes=[bPS[sb_]])
                                E_("t", lambda t, d=PS[sb_][64:128, 0:n], w=kBm, q=qsl: t.matmul(
                                    d, w, q, start=True, stop=True, tile_position=(0, 64)),
                                    reads=[bKB, bQS[qe]], writes=[bPS[sb_]])
                                cur.append((tau, parity, n, qoff, tslot2))
                            if ntk_full == 0 and n < 512:
                                for parity in range(2):
                                    sb_ = 2 * tslot2 + parity
                                    pp = zr[:, PB0 + tslot2 * 1024 + parity * 512:
                                            PB0 + tslot2 * 1024 + parity * 512 + n]
                                    E_("s", lambda s, d=pp, a=PS[sb_][:, 0:n], b=bb:
                                        s.activation(d, a, ACT.Exp, bias=b, scale=0.125),
                                        reads=[bPS[sb_], bBB[bslot]],
                                        writes=[pbB[tslot2]])
                            else:
                                pap = zr[:, PB0 + tslot2 * 1024: PB0 + tslot2 * 1024 + 512 + n]
                                E_("s", lambda s, d=pap, a=PSB[tslot2][:, 0:512 + n], b=bb: s.activation(
                                    d, a, ACT.Exp, bias=b, scale=0.125),
                                    reads=[bPS[2 * tslot2], bPS[2 * tslot2 + 1], bBB[bslot]],
                                    writes=[pbB[tslot2]])
                            if tau >= ntk_full:
                                for parity in range(2):
                                    pm = zr[:, PB0 + tslot2 * 1024 + parity * 512:
                                            PB0 + tslot2 * 1024 + parity * 512 + 128]
                                    E_("v", lambda v, d=pm: v.tensor_tensor(
                                        d, d, c_mask, AluOpType.mult),
                                        reads=[bCB], writes=[pbB[tslot2]])
                            for job in pend:
                                emit_avsums(*job)
                            pend = cur
                            if pr + 1 < NP:
                                if _NOQB:
                                    if tau == 0:
                                        emit_qB(pr + 1, slot)
                                elif tau < 4:
                                    emit_qB_stage(pr + 1, slot, tau)
                            elif next_qB is not None and tau < 4:
                                emit_qB_stage(0, next_qB, tau)
                            for _r in range(inject_rate):
                                pop_inject()
                        for job in pend:
                            emit_avsums(*job)
                        E_("v", lambda v, d=an_ap[pr], a=PS[abank][:]: v.tensor_copy(d, a),
                           reads=[bPS[abank]], writes=[anB[pr]])
                    while ii[0] < len(inj):
                        pop_inject()

                def epi_recips(ci):
                    def _rec(lo, hi):
                        def g(v):
                            with nc.allow_low_precision(reason="bf16 1/S scale for softmax (0.4%)"):
                                return v.reciprocal(invs[lo:hi, :], PS[5][lo:hi, :])
                        return g
                    E_("v", _rec(0, 8), reads=[bPS[5]], writes=[bINV])
                    E_("v", _rec(64, 72), reads=[bPS[5]], writes=[bINV])

                def epi_items(ci, ybks, ibks, lbank, defer):
                    xcol = ci * 512
                    slot = ci % 2
                    if defer:
                        y2ap, y2buf = xst[:, 512:1024], bXST[2]
                    else:
                        y2ap, y2buf = y2_ap, y2B
                    items = []
                    for pr in range(NP):
                        def _ia(pr=pr, ibank=ibks[pr % 2]):
                            E_("t", lambda t, d=PS[ibank][0:64, :]: t.matmul(
                                d, c_eselT[pr], invs[:], start=True, stop=True, tile_position=(0, 0)),
                                reads=[bINV, bCB], writes=[bPS[ibank]])
                            E_("t", lambda t, d=PS[ibank][64:128, :]: t.matmul(
                                d, c_eselB[pr], invs[:], start=True, stop=True, tile_position=(0, 64)),
                                reads=[bINV, bCB], writes=[bPS[ibank]])
                            E_("v", lambda v, d=an_ap[pr], i=PS[ibank][:]: v.tensor_tensor(
                                d, d, i, AluOpType.mult),
                                reads=[bPS[ibank]], writes=[anB[pr]])
                        items.append(_ia)
                    for d_ in range(NDC):
                        def _wo(d_=d_, ybk=ybks[d_ % 2]):
                            for pr in range(NP):
                                wT = wob[:, pr * 1024 + d_ * 128: pr * 1024 + d_ * 128 + 64]
                                wB = wob[:, pr * 1024 + d_ * 128 + 64: pr * 1024 + d_ * 128 + 128]
                                E_("t", lambda t, pr=pr, d=PS[ybk][0:64, :], w=wT, a=an_ap[pr]: t.matmul(
                                    d, w, a, start=(pr == 0), stop=(pr == NP - 1), tile_position=(0, 0), skip_group_check=True),
                                    reads=[bWO[pr], anB[pr]], writes=[bPS[ybk]])
                                E_("t", lambda t, pr=pr, d=PS[ybk][64:128, :], w=wB, a=an_ap[pr]: t.matmul(
                                    d, w, a, start=(pr == 0), stop=(pr == NP - 1), tile_position=(0, 64), skip_group_check=True),
                                    reads=[bWO[pr], anB[pr]], writes=[bPS[ybk]])
                            yb = zc_ap[slot][d_]
                            E_("v", lambda v, d=yb, y=PS[ybk][:]: v.tensor_tensor(
                                d, d, y, AluOpType.add),
                                reads=[bPS[ybk]], writes=[zcB[slot][d_]])
                            E_("s", lambda s, a=yb: s.activation(y2ap, a, ACT.Square),
                               reads=[zcB[slot][d_]], writes=[y2buf])
                            E_("t", lambda t, d_=d_, a=yb: t.matmul(
                                PS[lbank][0:64, :], c_ones64, a,
                                start=(d_ == 0), stop=(d_ == NDC - 1), tile_position=(0, 0), skip_group_check=True),
                                reads=[bCB, zcB[slot][d_]], writes=[bPS[lbank]])
                            E_("t", lambda t, d_=d_: t.matmul(
                                PS[lbank][64:128, :], c_ones64, y2ap,
                                start=(d_ == 0), stop=(d_ == NDC - 1), tile_position=(0, 64), skip_group_check=True),
                                reads=[bCB, y2buf], writes=[bPS[lbank]])
                        items.append(_wo)
                    items.append(lambda: rowmath_r(bPS[lbank], PS[lbank][0:1, :],
                                                   PS[lbank][64:65, :], LN_EPS_3, 0.0))
                    items.append(None)
                    def _mu3():
                        bcast_f32(lbank, rowa, bROWA)
                        E_("v", lambda v: v.tensor_copy(mub[:], PS[lbank][:]),
                           reads=[bPS[lbank]], writes=[bMUB])
                    items.append(_mu3)
                    items.append(None)
                    def _r3():
                        bcast_f32(lbank, rowb, bROWB)
                        E_("v", lambda v: v.tensor_copy(rub[:], PS[lbank][:]),
                           reads=[bPS[lbank]], writes=[bRUB])
                    items.append(_r3)
                    items.append(None)
                    for dp in range(4):
                        def _pb(dp=dp):
                            for d_ in (2 * dp, 2 * dp + 1):
                                tslot = d_ % 2
                                tt = tsc[:, tslot * 512:(tslot + 1) * 512]
                                yb = zc_ap[slot][d_]
                                E_("v", lambda v, a=yb, d=tt: v.tensor_tensor(
                                    d, a, mub[:], AluOpType.subtract),
                                    reads=[bMUB, zcB[slot][d_]], writes=[bTS[tslot]])
                                E_("v", lambda v, d=tt: v.tensor_tensor(
                                    d, d, rub[:], AluOpType.mult),
                                    reads=[bRUB], writes=[bTS[tslot]])
                                gcol = sgb[:, d_:d_ + 1]
                                bcol = sgb[:, 8 + d_:8 + d_ + 1]
                                E_("v", lambda v, d=tt, g=gcol, b=bcol: v.tensor_scalar(
                                    d, d, g, b, AluOpType.mult, AluOpType.add),
                                    reads=[bGB], writes=[bTS[tslot]])
                                xsl = xh[:, d_ * 1024 + xcol: d_ * 1024 + xcol + 512]
                                E_("v", lambda v, x=xsl, d=tt: v.tensor_tensor(
                                    x, x, d, AluOpType.add),
                                    reads=[bTS[tslot], bXH[d_][xcol // 512]], writes=[bXH[d_][xcol // 512]])
                        items.append(_pb)
                    return items

                def att_epilogue(ci):
                    epi_recips(ci)
                    for it in epi_items(ci, (4, 6), (5, 7), 5, False):
                        if it is not None:
                            it()

                def emit_guw_dma(hc):
                    for j, dw_ in ((0, dwg), (1, dwu)):
                        sl = (2 * hc + j) % 3
                        stg = wstag[:, sl * 1024:(sl + 1) * 1024]
                        E_("g", lambda g, d=stg, s=dw_.ap()[hc]: g.dma_start(d, s),
                           writes=[bWST[sl]], queue="qw%d" % sl)

                def emit_guw_cast(hc):
                    gsl = (2 * hc) % 3
                    usl = (2 * hc + 1) % 3
                    wsl = hc % 2
                    gb_ = wgbb[:, wsl * 1024:(wsl + 1) * 1024]
                    ub_ = wubb[:, wsl * 1024:(wsl + 1) * 1024]
                    E_("v", lambda v, d=gb_, s=wstag[:, gsl * 1024:(gsl + 1) * 1024]: v.tensor_copy(d, s),
                       reads=[bWST[gsl]], writes=[bWG[wsl]])
                    E_("s", lambda s, d=ub_, a=wstag[:, usl * 1024:(usl + 1) * 1024]: s.activation(d, a, ACT.Copy),
                       reads=[bWST[usl]], writes=[bWU[wsl]])

                def emit_wd_dma(hc):
                    sl = hc % 3
                    stg = wstag[:, sl * 1024:(sl + 1) * 1024]
                    E_("g", lambda g, d=stg, s=dwd.ap()[hc]: g.dma_start(d, s),
                       writes=[bWST[sl]], queue="qw%d" % sl)

                z3_ap = [zr[:, d * 512:(d + 1) * 512] for d in range(NDC)]

                def h_region(k):
                    if k < 8:
                        return 12288 + k * 512
                    if k < 16:
                        return 8192 + (k - 8) * 512
                    return 4096 + (k - 16) * 512
                h_ap = [zr[:, h_region(k): h_region(k) + 512] for k in range(NHC)]

                att_prep(0)
                if _NOPREP:
                    att_heads(0)
                    att_prep(1)
                else:
                    att_heads(0, inject=att_prep_items(1), next_qB=1, inject_rate=2)
                att_epilogue(0)
                z3B0 = rebind([f"z3a_{d}" for d in range(NDC)], zcB[0])
                emit_guw_dma(0)
                if _INJL < 1:
                    it0 = []
                else:
                    it0 = ([None] * 16 + ffn_stat_items(0, 6, True)
                           + ffn_zgen_items(0, 6, z3B0, z3_ap, True)
                           + [lambda: emit_guw_cast(0)])
                att_heads(1, inject=it0)
                _DEFER = (not _NODEF) and lim >= 3 and _INJL >= 2
                if _DEFER:
                    epi_recips(1)
                    epi1_gu = epi_items(1, (0, 2), (1, 1), 3, True)
                else:
                    att_epilogue(1)
                    epi1_gu = []

                if lim < 3:
                    break
                if _DEFER:
                    hB = (rebind([f"h{k}" for k in range(8)], [pbB[0], pbB[1], q2B])
                          + [None] * 16)
                else:
                    hB = rebind([f"h{k}" for k in range(NHC)], eb[8:27])
                z3B1 = None
                wkv_items = []
                if half == 1 and s_i == 0 and lim >= 99:
                    for _pc in range(16):
                        for _j, (_dsrc, _dstT, _dstB) in enumerate(
                                ((dwk, wqb, bWQ), (dwv, wob, bWO))):
                            def _ld(pc=_pc, dsrc=_dsrc, dstT=_dstT, dstB=_dstB, slot=_j % 2):
                                st = tsc[:, slot * 512:(slot + 1) * 512]
                                sap = dsrc.ap()[:, pc * 512:(pc + 1) * 512]
                                dst = dstT[:, pc * 512:(pc + 1) * 512]
                                E_("y", lambda y, d=st, s=sap: y.dma_start(d, s),
                                   writes=[bTS[slot]], queue="qt%d" % slot)
                                E_("v", lambda v, d=dst, s=st: v.tensor_copy(d, s),
                                   reads=[bTS[slot]], writes=[dstB[pc // 2]])
                            wkv_items.append(_ld)

                for tf in range(2):
                    xcol = tf * 512
                    if tf == 0:
                        z3Bl = z3B0
                        if _INJL < 1:
                            for it in (ffn_stat_items(0, 6, True)
                                       + ffn_zgen_items(0, 6, z3B0, z3_ap, True)):
                                if it is not None:
                                    it()
                            emit_guw_cast(0)
                        if _INJL >= 2:
                            _gfull = ffn_stat_items(1, 2, False, use_qsb=True)
                            _gn = int(_os.environ.get("KMK_INJN", "99"))
                            gu_inject = epi1_gu + _gfull[:_gn]
                            gu_rest = _gfull[_gn:]
                        else:
                            gu_inject = []
                            gu_rest = []
                    else:
                        z3Bl = z3B1
                        gu_inject = []
                        emit_guw_dma(0)
                        emit_guw_cast(0)
                    gii = [0]

                    def pop_gu():
                        if gii[0] < len(gu_inject):
                            it = gu_inject[gii[0]]
                            gii[0] += 1
                            if it is not None:
                                it()
                    for hc in range(NHC):
                        if tf == 0 and _DEFER:
                            if hc == 8:
                                hB[8:16] = rebind([f"h{k}" for k in range(8, 16)], anB)
                            elif hc == 16:
                                hB[16:24] = rebind([f"h{k}" for k in range(16, 24)], zcB[1])
                        wsl = hc % 2
                        if hc + 1 < NHC:
                            emit_guw_dma(hc + 1)
                        gB = 4 + 2 * ((hc + 1) % 2)
                        uB = gB + 1
                        for dc in range(NDC):
                            wgsl = wgbb[:, wsl * 1024 + dc * 128: wsl * 1024 + (dc + 1) * 128]
                            wusl = wubb[:, wsl * 1024 + dc * 128: wsl * 1024 + (dc + 1) * 128]
                            E_("t", lambda t, dc=dc, w=wgsl, z=z3_ap[dc], d=PS[gB][:]: t.matmul(
                                d, w, z, start=(dc == 0), stop=(dc == NDC - 1)),
                                reads=[bWG[wsl], z3Bl[dc]], writes=[bPS[gB]])
                            E_("t", lambda t, dc=dc, w=wusl, z=z3_ap[dc], d=PS[uB][:]: t.matmul(
                                d, w, z, start=(dc == 0), stop=(dc == NDC - 1)),
                                reads=[bWU[wsl], z3Bl[dc]], writes=[bPS[uB]])
                        if hc + 1 < NHC:
                            emit_guw_cast(hc + 1)
                        pop_gu()
                        sg_ = xst[:, 1024 + (hc % 2) * 512: 1024 + (hc % 2 + 1) * 512]
                        E_("s", lambda s, d=sg_, a=PS[gB][:]: s.activation(d, a, ACT.Silu),
                           reads=[bPS[gB]], writes=[bXST[2 + 2 * (hc % 2) - 1 if False else 2 * (hc % 2) + 1]])
                        E_("v", lambda v, d=h_ap[hc], a=sg_, u=PS[uB][:]: v.tensor_tensor(
                            d, a, u, AluOpType.mult),
                            reads=[bXST[2 * (hc % 2) + 1], bPS[uB]], writes=[hB[hc]])
                        pop_gu()
                    while gii[0] < len(gu_inject):
                        pop_gu()
                    if tf == 0:
                        for it in gu_rest:
                            if it is not None:
                                it()
                        z3B1 = rebind([f"z3b_{d}" for d in range(NDC)], z3B0)
                        dn_inject = ffn_zgen_items(1, 0, z3B1, z3_ap, False) + wkv_items[:24]
                        if _INJL < 2:
                            for it in ffn_stat_items(1, 2, False):
                                if it is not None:
                                    it()
                        if _INJL < 3:
                            for it in dn_inject:
                                it()
                            dn_inject = []
                    else:
                        dn_inject = list(wkv_items[24:])
                    dii = [0]

                    def pop_dn():
                        if dii[0] < len(dn_inject):
                            it = dn_inject[dii[0]]
                            dii[0] += 1
                            it()
                    for hc in range(3):
                        emit_wd_dma(hc)
                    for hc in range(NHC):
                        sl = hc % 3
                        wsl = hc % 2
                        stg = wstag[:, sl * 1024:(sl + 1) * 1024]
                        db_ = wdbb[:, wsl * 1024:(wsl + 1) * 1024]
                        E_("s", lambda s, d=db_, a=stg: s.activation(
                            d, a, ACT.Copy, scale=1.0 / 127.0),
                            reads=[bWST[sl], bCF], writes=[bWD[wsl]])
                        if hc + 3 < NHC:
                            emit_wd_dma(hc + 3)
                        pop_dn()
                        pop_dn()
                        if tf == 0 and hc == 23:
                            emit_guw_dma(0)
                        for d_ in range(NDC):
                            wdsl = wdbb[:, wsl * 1024 + d_ * 128: wsl * 1024 + (d_ + 1) * 128]
                            E_("t", lambda t, w=wdsl, h=h_ap[hc], d=PS[d_][:], hc=hc: t.matmul(
                                d, w, h, start=(hc == 0), stop=(hc == NHC - 1)),
                                reads=[bWD[wsl], hB[hc]], writes=[bPS[d_]])
                    for d_ in range(NDC):
                        xsl = xh[:, d_ * 1024 + xcol: d_ * 1024 + xcol + 512]
                        E_("v", lambda v, x=xsl, f=PS[d_][:]: v.tensor_tensor(
                            x, x, f, AluOpType.add),
                            reads=[bPS[d_], bXH[d_][tf]], writes=[bXH[d_][tf]])
                        odst = os_[:, d_ * 2048 + half * 1024 + tf * 512:
                                   d_ * 2048 + half * 1024 + tf * 512 + 512]
                        E_("g", lambda ge, d=odst, s=xsl: ge.dma_start(d, s),
                           reads=[bXH[d_][tf]], queue="qo" if tf == 0 else "qo1")
                        nstores[tf] += 1
                    if half == 0 and lim >= 4:
                        emit_xload(1, tf)
                if half == 0 and lim == 3:
                    break

        P.ops["g"].append(([Tok("qo", nstores[0] * 16, True),
                           Tok("qo1", nstores[1] * 16, True)],
                          None, Tok("g", -1, False), None))

        semval = P.finalize()

        def replay(eng_obj, ename):
            for waits, fn, tok, queue in P.ops[ename]:
                for t in waits:
                    wv = t.idx if t.queue else semval[(t.src, t.idx)]
                    assert wv is not None, (ename, t.src, t.idx)
                    eng_obj.wait_ge(SEMS[t.src], wv)
                if fn is None:
                    continue
                inst = fn(eng_obj)
                if queue is not None:
                    inst.then_inc(SEMS[tok.src], 16)
                elif (tok.src, tok.idx) in P.referenced:
                    inst.then_inc(SEMS[tok.src], 1)

        @block.tensor
        def _(tensor):
            replay(tensor, "t")

        @block.vector
        def _(vector):
            replay(vector, "v")

        @block.scalar
        def _(scalar):
            replay(scalar, "s")

        @block.gpsimd
        def _(gpsimd):
            replay(gpsimd, "g")

        @block.sync
        def _(sync):
            replay(sync, "y")

    return nc, {e: len(P.ops[e]) for e in P.ENGS}


_NC_CACHE = [None]


def _host_inputs(inputs):
    x = np.asarray(inputs["L0_attn_x"])
    wq = np.asarray(inputs["L0_attn_Wq"])
    wk = np.asarray(inputs["L0_attn_Wk"])
    wv = np.asarray(inputs["L0_attn_Wv"])
    wo = np.asarray(inputs["L0_attn_Wo"])
    gam = np.asarray(inputs["L0_attn_gamma"])
    bet = np.asarray(inputs["L0_attn_beta"])
    wg = np.asarray(inputs["L0_ffn_w_gate"])
    wu = np.asarray(inputs["L0_ffn_w_up"])
    wd = np.asarray(inputs["L0_ffn_w_down"])

    def whead(w):
        return np.ascontiguousarray(
            w.reshape(H, NDC, 128, E).transpose(2, 1, 0, 3).reshape(128, NDC * H * E))

    m = {}
    m["wq"] = whead(wq)
    m["wk"] = whead(wk)
    m["wv"] = whead(wv)
    m["wo"] = np.ascontiguousarray(wo.reshape(NP, 128, D).transpose(1, 0, 2).reshape(128, NP * D))
    gb = np.zeros((128, 16), dtype=np.float32)
    gb[:, 0:8] = gam.reshape(8, 128).T
    gb[:, 8:16] = bet.reshape(8, 128).T
    m["gb"] = gb
    m["wg"] = np.ascontiguousarray(
        wg.reshape(NHC, 128, NDC, 128).transpose(0, 3, 2, 1).reshape(NHC, 128, D))
    m["wu"] = np.ascontiguousarray(
        wu.reshape(NHC, 128, NDC, 128).transpose(0, 3, 2, 1).reshape(NHC, 128, D))
    m["wd"] = np.ascontiguousarray(wd.T.reshape(NHC, 128, D))

    cbv = np.zeros((128, 1792), dtype=np.float32)
    cbv[:, 0] = 1.0
    for j in range(8):
        cbv[:, 64 + j * 64 + j] = 1.0
    for j in range(8):
        cbv[64 + j, 576 + (j * 2) * 64: 576 + (j * 2) * 64 + 64] = 1.0
        cbv[j, 576 + (j * 2 + 1) * 64: 576 + (j * 2 + 1) * 64 + 64] = 1.0
    tri = np.triu(np.ones((128, 128), dtype=np.float32))
    cbv[:, 1600:1728] = tri
    cbv[:, 1728:1792] = 1.0
    m["cb"] = cbv
    cfv = np.zeros((128, 144), dtype=np.float32)
    cfv[0, 0:128] = 1.0
    _CONSTVALS = [1e-05, 0.001, -LN127, -0.0625, 0.0625, 0.125, -0.125,
                  1.0 / 1024.0, 1.0 / 127.0, 0.5, -0.5, 2.0, -1.0, 3.0]
    for _i, _cv in enumerate(_CONSTVALS):
        cfv[:, 128 + _i] = _cv
    m["cf"] = cfv

    per_core_x = []
    for c in range(NCORE):
        xsh = np.empty((2, 128, NDC * SEQ), dtype=np.float32)
        for si in range(2):
            xi = x[2 * c + si]
            xsh[si] = xi.T.reshape(NDC, 128, SEQ).transpose(1, 0, 2).reshape(128, NDC * SEQ)
        per_core_x.append(np.ascontiguousarray(xsh))
    return m, per_core_x


def kernel(**inputs):
    m, per_core_x = _host_inputs(inputs)
    if _NC_CACHE[0] is None:
        _NC_CACHE[0] = build_program()[0]
    nc = _NC_CACHE[0]
    in_maps = []
    for c in range(NCORE):
        im = dict(m)
        im["x"] = per_core_x[c]
        in_maps.append(im)
    res = run_bass_kernel_spmd(nc, in_maps, core_ids=list(range(NCORE)))
    out = np.empty((B_TOT, SEQ, D), dtype=np.float32)
    for c in range(NCORE):
        o = res.results[c]["o"]
        for si in range(2):
            out[2 * c + si] = o[si].reshape(128, NDC, SEQ).transpose(2, 1, 0).reshape(SEQ, D)
    return out


if __name__ == "__main__":
    nc, counts = build_program()
    print("instruction counts:", counts, "total:", sum(counts.values()))


# version 26
# speedup vs baseline: 1.0062x; 1.0029x over previous
import numpy as np
import ml_dtypes
import concourse.bass as bass
import concourse.mybir as mybir
from concourse.alu_op_type import AluOpType
from concourse.bass_utils import run_bass_kernel_spmd

F32 = mybir.dt.float32
BF16 = mybir.dt.bfloat16
I32 = mybir.dt.int32
ACT = mybir.ActivationFunctionType
AX = mybir.AxisListType

NCORE = 8
B_TOT = 16
SEQ = 2048
D = 1024
H = 16
E = 64
NDC = 8
NP = 8
CH = 512
NCHUNK = 4
NHC = 24
LN_EPS_PRE = 1e-5
LN_EPS_3 = 1e-3
LN127 = float(np.log(127.0))


class Tok:
    __slots__ = ("src", "idx", "queue")
    def __init__(self, src, idx, queue):
        self.src, self.idx, self.queue = src, idx, queue


class Buf:
    __slots__ = ("w", "r", "name", "psum")
    def __init__(self, name="", psum=False):
        self.w, self.r, self.name = None, [], name
        self.psum = psum


class Plan:
    ENGS = ("t", "v", "s", "g", "y")

    def __init__(self):
        self.ops = {e: [] for e in self.ENGS}
        self.cnt = {e: 0 for e in self.ENGS}
        self.qcnt = {}
        self.waited = {e: {} for e in self.ENGS}
        self.referenced = set()

    def emit(self, eng, fn, reads=(), writes=(), queue=None):
        deps = []
        for b in reads:
            if b.w is not None:
                deps.append((b.w, True))
            if b.psum and eng in ("v", "s"):
                for t in b.r:
                    if t.src in ("v", "s") and t.src != eng:
                        deps.append((t, False))
        qskip = []
        for b in writes:
            if b.w is not None:
                deps.append((b.w, False))
                if b.r:
                    qskip.append(b.w)
            for t in b.r:
                deps.append((t, False))
                qskip.append(t)
        for b in reads:
            if b.w is not None:
                qskip.append(b.w)
        waits = {}
        for t, is_raw in deps:
            if t.src == eng and not t.queue and eng == "t":
                continue
            if (queue is not None and t.queue and t.src == queue
                    and t in qskip):
                continue
            if t.queue:
                t = Tok(t.src, self.qcnt[t.src], True)
            cur = waits.get(t.src)
            if cur is None or t.idx > cur.idx:
                waits[t.src] = t
        final = []
        for src, t in waits.items():
            if self.waited[eng].get(src, -1) >= t.idx:
                continue
            self.waited[eng][src] = t.idx
            final.append(t)
            if not t.queue:
                self.referenced.add((t.src, t.idx))
        if queue is not None:
            self.qcnt[queue] = self.qcnt.get(queue, 0) + 16
            tok = Tok(queue, self.qcnt[queue], True)
        else:
            self.cnt[eng] += 1
            tok = Tok(eng, self.cnt[eng], False)
        self.ops[eng].append((final, fn, tok, queue))
        for b in reads:
            b.r.append(tok)
        for b in writes:
            b.w = tok
            b.r = []
        return tok

    def finalize(self):
        semval = {}
        for e in self.ENGS:
            c = 0
            for (waits, fn, tok, queue) in self.ops[e]:
                if tok.queue:
                    semval[(tok.src, tok.idx)] = tok.idx
                elif (tok.src, tok.idx) in self.referenced:
                    c += 1
                    semval[(tok.src, tok.idx)] = c
                else:
                    semval[(tok.src, tok.idx)] = None
        return semval


import os as _os
_LIM = _os.environ.get("KMK_LIM", "full")
_NOINJ = _os.environ.get("KMK_NOINJ", "0") == "1"
_INJL = 0 if _NOINJ else int(_os.environ.get("KMK_INJL", "3"))
_NOQB = _os.environ.get("KMK_NOQB", "0") == "1"
_NODEF = _os.environ.get("KMK_NODEF", "0") == "1"
_NOPREP = _os.environ.get("KMK_NOPREP", "1") == "1"
_LIMS = {"s": 0, "qkv_w": 0.2, "qkv_k": 0.4, "k2a": 0.45, "k2b": 0.5, "k2c": 0.55, "qkv_k2": 0.6, "qkv": 1, "attn0": 2, "half0": 3, "seq0": 4, "full": 99}


def build_program():
    nc = bass.Bass("TRN2", target_bir_lowering=False)

    dx = nc.dram_tensor("x", [2, 128, NDC * SEQ], F32, kind="ExternalInput")
    dwq = nc.dram_tensor("wq", [128, NDC * H * E], F32, kind="ExternalInput")
    dwk = nc.dram_tensor("wk", [128, NDC * H * E], F32, kind="ExternalInput")
    dwv = nc.dram_tensor("wv", [128, NDC * H * E], F32, kind="ExternalInput")
    dwo = nc.dram_tensor("wo", [128, NP * D], F32, kind="ExternalInput")
    dgb = nc.dram_tensor("gb", [128, 16], F32, kind="ExternalInput")
    dwg = nc.dram_tensor("wg", [NHC, 128, D], I32, kind="ExternalInput")
    dwu = nc.dram_tensor("wu", [NHC, 128, D], I32, kind="ExternalInput")
    dwd = nc.dram_tensor("wd", [NHC, 128, D], I32, kind="ExternalInput")
    dcb = nc.dram_tensor("cb", [128, 1792], F32, kind="ExternalInput")
    dcf = nc.dram_tensor("cf", [128, 144], F32, kind="ExternalInput")
    dout = nc.dram_tensor("o", [2, 128, NDC * SEQ], F32, kind="ExternalOutput")

    P = Plan()

    from contextlib import ExitStack
    with ExitStack() as _stack:
        ent = _stack.enter_context
        xh = ent(nc.sbuf_tensor("xh", [128, 8192], F32))
        zr = ent(nc.sbuf_tensor("zr", [128, 16384], BF16))
        kb = ent(nc.sbuf_tensor("kb", [128, NP * SEQ], BF16))
        vb = ent(nc.sbuf_tensor("vb", [128, 16 * 1024], BF16))
        wqb = ent(nc.sbuf_tensor("wqb", [128, 8192], BF16))
        wob = ent(nc.sbuf_tensor("wob", [128, 8192], BF16))
        wstag = ent(nc.sbuf_tensor("wstag", [128, 3072], I32))
        qsb_t = ent(nc.sbuf_tensor("qsb", [128, 2048], BF16))
        brt = ent(nc.sbuf_tensor("brt", [128, 2], F32))
        wgbb = ent(nc.sbuf_tensor("wgbb", [128, 2048], BF16))
        wubb = ent(nc.sbuf_tensor("wubb", [128, 2048], BF16))
        wdbb = ent(nc.sbuf_tensor("wdbb", [128, 2048], BF16))
        tsc = ent(nc.sbuf_tensor("tsc", [128, 1024], F32))
        invs = ent(nc.sbuf_tensor("invs", [128, 512], BF16))
        mub = ent(nc.sbuf_tensor("mub", [128, 512], BF16))
        rub = ent(nc.sbuf_tensor("rub", [128, 512], BF16))
        rowa = ent(nc.sbuf_tensor("rowa", [128, 512], F32))
        rowb = ent(nc.sbuf_tensor("rowb", [128, 512], F32))
        browb = ent(nc.sbuf_tensor("browb", [128, 8], BF16))
        bbias = ent(nc.sbuf_tensor("bbias", [128, 2], F32))
        k2m = ent(nc.sbuf_tensor("k2m", [128, 2], F32))
        mk2 = ent(nc.sbuf_tensor("mk2", [128, 2], F32))
        sgb = ent(nc.sbuf_tensor("sgb", [128, 16], F32))
        xst = ent(nc.sbuf_tensor("xst", [128, 2048], BF16))
        cbt = ent(nc.sbuf_tensor("cbt", [128, 1792], BF16))
        cft = ent(nc.sbuf_tensor("cft", [128, 144], F32))
        ps01 = ent(nc.psum_tensor("ps01", [128, 1024], F32))
        ps23 = ent(nc.psum_tensor("ps23", [128, 1024], F32))
        ps4 = ent(nc.psum_tensor("ps4", [128, 512], F32))
        ps5 = ent(nc.psum_tensor("ps5", [128, 512], F32))
        ps6 = ent(nc.psum_tensor("ps6", [128, 512], F32))
        ps7 = ent(nc.psum_tensor("ps7", [128, 512], F32))
        sem_t = ent(nc.semaphore("sem_t"))
        sem_v = ent(nc.semaphore("sem_v"))
        sem_s = ent(nc.semaphore("sem_s"))
        sem_g = ent(nc.semaphore("sem_g"))
        sem_y = ent(nc.semaphore("sem_y"))
        sem_qc = ent(nc.semaphore("sem_qc"))
        sem_qxh = [ent(nc.semaphore("sem_qxh%d" % i)) for i in range(8)]
        sem_qt0 = ent(nc.semaphore("sem_qt0"))
        sem_qt1 = ent(nc.semaphore("sem_qt1"))
        sem_qw0 = ent(nc.semaphore("sem_qw0"))
        sem_qw1 = ent(nc.semaphore("sem_qw1"))
        sem_qw2 = ent(nc.semaphore("sem_qw2"))
        sem_qw3 = ent(nc.semaphore("sem_qw3"))
        sem_qo = ent(nc.semaphore("sem_qo"))
        sem_qo1 = ent(nc.semaphore("sem_qo1"))
        with nc.Block() as blk0:
            @blk0.vector
            def _(v0):
                for _sm in ([sem_t, sem_v, sem_s, sem_g, sem_y, sem_qc,
                             sem_qt0, sem_qt1, sem_qw0, sem_qw1, sem_qw2, sem_qw3,
                             sem_qo, sem_qo1]
                            + list(sem_qxh)):
                    v0.sem_clear(_sm)
        block = ent(nc.Block())
        class _PsView:
            __slots__ = ("t", "coff")
            def __init__(self, t, coff):
                self.t, self.coff = t, coff
            def __getitem__(self, idx):
                if isinstance(idx, tuple):
                    rows, cols = idx
                else:
                    rows, cols = idx, slice(None)
                c0 = cols.start or 0
                c1 = 512 if cols.stop is None else cols.stop
                return self.t[rows, self.coff + c0: self.coff + c1]
        PS = [_PsView(ps01, 0), _PsView(ps01, 512), _PsView(ps23, 0), _PsView(ps23, 512),
              ps4, ps5, ps6, ps7]
        PSB = [ps01, ps23]
        SEMS = {"t": sem_t, "v": sem_v, "s": sem_s, "g": sem_g, "y": sem_y,
                "qc": sem_qc, "qt0": sem_qt0, "qt1": sem_qt1,
                "qw0": sem_qw0, "qw1": sem_qw1, "qw2": sem_qw2, "qw3": sem_qw3,
                "qo": sem_qo, "qo1": sem_qo1}
        for _i in range(8):
            SEMS["qxh%d" % _i] = sem_qxh[_i]

        c_ones64 = cbt[:, 0:64]
        c_onescol = [cbt[:, 64 + j * 64: 128 + j * 64] for j in range(8)]
        c_eselT = [cbt[:, 576 + (j * 2) * 64: 576 + (j * 2) * 64 + 64] for j in range(8)]
        c_eselB = [cbt[:, 576 + (j * 2 + 1) * 64: 576 + (j * 2 + 1) * 64 + 64] for j in range(8)]
        c_mask = cbt[:, 1600:1728]
        c_allones = cbt[:, 1728:1792]
        c_onesrow0 = cft[:, 0:128]
        _CONSTVALS = [1e-05, 0.001, -LN127, -0.0625, 0.0625, 0.125, -0.125,
                      1.0 / 1024.0, 1.0 / 127.0, 0.5, -0.5, 2.0, -1.0, 3.0]
        for _i, _cv in enumerate(_CONSTVALS):
            nc.const_aps.register(F32, float(_cv), cft[:, 128 + _i:129 + _i])

        bXH = [[Buf(f"xh{d}_{p}") for p in range(2)] for d in range(NDC)]
        bKB, bVB = Buf("kb"), Buf("vb")
        bWQ = [Buf(f"wq{i}") for i in range(8)]
        bWO = [Buf(f"wo{i}") for i in range(8)]
        bPS = [Buf(f"ps{i}", psum=True) for i in range(8)]
        bWST = [Buf("wst0"), Buf("wst1"), Buf("wst2")]
        bQS = [Buf(f"qs{i}") for i in range(4)]
        bWG = [Buf("wg0"), Buf("wg1")]
        bWU = [Buf("wu0"), Buf("wu1")]
        bWD = [Buf("wd0"), Buf("wd1")]
        bTS = [Buf("ts0"), Buf("ts1")]
        bINV, bROWA, bROWB, bROWT = Buf("invs"), Buf("rowa"), Buf("rowb"), Buf("rowt")
        bMUB = Buf("mub")
        bRUB = Buf("rub")
        bBROWB, bK2, bMK2, bGB = Buf("browb"), Buf("k2m"), Buf("mk2"), Buf("sgb")
        bBB = [Buf("bb0"), Buf("bb1")]
        bXST = [Buf("xst0"), Buf("xst1"), Buf("xst2"), Buf("xst3")]
        bCB, bCF = Buf("cb"), Buf("cf")

        zr_era = [[]]

        def new_era(bufnames):
            old_toks = []
            for b in zr_era[0]:
                if b.w is not None:
                    old_toks.append(b.w)
                old_toks.extend(b.r)
            out = []
            for n in bufnames:
                b = Buf(n)
                b.r = list(old_toks)
                out.append(b)
            zr_era[0] = list(out)
            return out

        def rebind(bufnames, old):
            toks = []
            for b in old:
                if b.w is not None:
                    toks.append(b.w)
                toks.extend(b.r)
            out = []
            for n in bufnames:
                b = Buf(n)
                b.r = list(toks)
                out.append(b)
            for b in old:
                zr_era[0].remove(b)
            zr_era[0].extend(out)
            return out

        E_ = P.emit

        E_("y", lambda y: y.dma_start(tsc[:], dcb.ap()[:, 0:1024]), writes=[bTS[0], bTS[1]], queue="qc")
        E_("y", lambda y: y.dma_start(cft[:], dcf.ap()), writes=[bCF], queue="qc")
        E_("y", lambda y: y.dma_start(sgb[:], dgb.ap()), writes=[bGB], queue="qc")
        E_("v", lambda v: v.tensor_copy(cbt[:, 0:1024], tsc[:]), reads=[bTS[0], bTS[1], bCF, bGB], writes=[bCB])
        E_("y", lambda y: y.dma_start(tsc[:, 0:768], dcb.ap()[:, 1024:1792]), writes=[bTS[0], bTS[1]], queue="qc")
        E_("v", lambda v: v.tensor_copy(cbt[:, 1024:1792], tsc[:, 0:768]), reads=[bTS[0], bTS[1]], writes=[bCB])
        E_("v", lambda v: v.memset(invs[:], 0.0), writes=[bINV])
        E_("v", lambda v: v.memset(qsb_t[:], 0.0), writes=list(bQS))
        E_("v", lambda v: v.memset(rowa[:], 0.0), writes=[bROWA])
        E_("v", lambda v: v.memset(rowb[:], 0.0), writes=[bROWB])
        E_("v", lambda v: v.memset(browb[:], 0.0), writes=[bBROWB])

        for _k in range(8):
            E_("t", lambda t, d=PS[_k][:, 0:512]: t.matmul(
                d, c_onesrow0, rowa[:, 0:512], start=True, stop=True),
                reads=[bROWA, bCF], writes=[bPS[_k]])

        nstores = [0, 0]

        def rowmath_r(stat_buf, s1_ap, s2_ap, eps, extra_bias):
            a1 = rowa[0:1, 0:512]
            t0 = tsc[0:1, 0:512]
            t32 = tsc[0:1, 512:1024]
            t64 = t0
            t96 = t32
            b1 = rowb[0:1, 0:512]
            E_("s", lambda s, a=s1_ap, d=a1: s.activation(d, a, ACT.Copy, scale=1.0 / 1024.0),
               reads=[stat_buf], writes=[bROWA])
            E_("s", lambda s, a=s2_ap, d=t0: s.activation(d, a, ACT.Copy, scale=1.0 / 1024.0),
               reads=[stat_buf], writes=[bTS[0]])
            E_("v", lambda v, a=a1, d=t32: v.tensor_tensor(d, a, a, AluOpType.mult),
               reads=[bROWA], writes=[bTS[1]])
            E_("v", lambda v, a=t0, b=t32, d=t64: v.tensor_tensor(d, a, b, AluOpType.subtract),
               reads=[bTS[1], bTS[0]], writes=[bTS[0]])
            E_("s", lambda s, a=t64, d=t96, eps=eps: s.activation(d, a, ACT.Ln, bias=float(eps), scale=1.0),
               reads=[bTS[0], bCF], writes=[bTS[1]])
            E_("s", lambda s, a=t96, d=b1, xb=extra_bias: s.activation(d, a, ACT.Exp, bias=float(xb), scale=-0.5),
               reads=[bTS[1], bCF], writes=[bROWB])

        def bcast_f32(pidx, src_t, src_buf):
            dst = PS[pidx][:, 0:512]
            src = src_t[:, 0:512]
            E_("t", lambda t, d=dst, s=src: t.matmul(d, c_onesrow0, s, start=True, stop=True),
               reads=[src_buf, bCF], writes=[bPS[pidx]])

        def _stat_slots(sl, alt):
            if alt:
                return xst[:, 0:512], bXST[0], xst[:, 512:1024], bXST[2]
            xb = xst[:, sl * 512:(sl + 1) * 512]
            bxb = bXST[2 * sl]
            x2 = xst[:, 1024 + sl * 512: 1024 + (sl + 1) * 512]
            bx2 = bXST[2 * sl + 1]
            return xb, bxb, x2, bx2

        def stats_piece_ve(xsl, xbuf, sl, use_qsb=False):
            xb, bxb, x2, bx2 = _stat_slots(sl, use_qsb)
            E_("v", lambda v, a=xsl, d=xb: v.tensor_copy(d, a),
               reads=[xbuf], writes=[bxb])
            E_("s", lambda s, a=xsl, d=x2: s.activation(d, a, ACT.Square),
               reads=[xbuf], writes=[bx2])

        def stats_piece_mm(dc, sl, pidx, use_qsb=False):
            xb, bxb, x2, bx2 = _stat_slots(sl, use_qsb)
            E_("t", lambda t, dc=dc, d=PS[pidx][0:64, :], a=xb: t.matmul(
                d, c_ones64, a, start=(dc == 0), stop=(dc == NDC - 1), tile_position=(0, 0), skip_group_check=True),
                reads=[bxb, bCB], writes=[bPS[pidx]])
            E_("t", lambda t, dc=dc, d=PS[pidx][64:128, :], a=x2: t.matmul(
                d, c_ones64, a, start=(dc == 0), stop=(dc == NDC - 1), tile_position=(0, 64),
                skip_group_check=True),
                reads=[bx2, bCB], writes=[bPS[pidx]])

        def stats_chunk(xcols, pidx, xh_bufs):
            for dc, (xsl, xbuf) in enumerate(xcols):
                sl = dc % 2
                stats_piece_ve(xsl, xbuf, sl)
                stats_piece_mm(dc, sl, pidx)

        def ffn_stat_items(tf, bank, psum_r, use_qsb=False):
            xcols = [(xh[:, dc * 1024 + tf * 512: dc * 1024 + tf * 512 + 512], bXH[dc][tf])
                     for dc in range(NDC)]
            items = []
            for dc, (xsl, xbuf) in enumerate(xcols):
                sl = dc % 2
                items.append(lambda xsl=xsl, xbuf=xbuf, sl=sl: stats_piece_ve(xsl, xbuf, sl, use_qsb))
                items.append(lambda dc=dc, sl=sl: stats_piece_mm(dc, sl, bank, use_qsb))
            items.append(lambda: rowmath_r(bPS[bank], PS[bank][0:1, :], PS[bank][64:65, :],
                                           LN_EPS_PRE, -LN127))
            items.append(None)
            def _mu():
                bcast_f32(bank, rowa, bROWA)
                E_("v", lambda v: v.tensor_copy(mub[:], PS[bank][:]),
                   reads=[bPS[bank]], writes=[bMUB])
            items.append(_mu)
            items.append(None)
            items.append(lambda: bcast_f32(bank, rowb, bROWB))
            if not psum_r:
                items.append(lambda: E_("v", lambda v: v.tensor_copy(rub[:], PS[bank][:]),
                                        reads=[bPS[bank]], writes=[bRUB]))
            items.append(None)
            return items

        def ffn_zgen_items(tf, bank, z3Bl, z3_ap_l, psum_r):
            items = []
            for dc in range(NDC):
                xsl = xh[:, dc * 1024 + tf * 512: dc * 1024 + tf * 512 + 512]
                xbuf = bXH[dc][tf]
                tslot = dc % 2
                tt = tsc[:, tslot * 512:(tslot + 1) * 512]
                dap = z3_ap_l[dc]
                dbuf = z3Bl[dc]
                def _z(xsl=xsl, xbuf=xbuf, tt=tt, dap=dap, dbuf=dbuf, tslot=tslot):
                    E_("v", lambda v, a=xsl, d=tt: v.tensor_tensor(
                        d, a, mub[:], AluOpType.subtract),
                        reads=[xbuf, bMUB], writes=[bTS[tslot]])
                    if psum_r:
                        E_("v", lambda v, a=tt, d=dap: v.tensor_tensor(
                            d, a, PS[bank][:], AluOpType.mult),
                            reads=[bTS[tslot], bPS[bank]], writes=[dbuf])
                    else:
                        E_("v", lambda v, a=tt, d=dap: v.tensor_tensor(
                            d, a, rub[:], AluOpType.mult),
                            reads=[bTS[tslot], bRUB], writes=[dbuf])
                items.append(_z)
            return items

        def zgen(xcols, mu_pidx, r_pidx, dsts):
            for dc, (xsl, xbuf) in enumerate(xcols):
                tslot = dc % 2
                tt = tsc[:, tslot * 512:(tslot + 1) * 512]
                dap, dbufs = dsts[dc]
                E_("v", lambda v, a=xsl, d=tt, m=PS[mu_pidx][:]: v.tensor_tensor(
                    d, a, m, AluOpType.subtract),
                    reads=[xbuf, bPS[mu_pidx]], writes=[bTS[tslot]])
                E_("v", lambda v, a=tt, d=dap, r=PS[r_pidx][:]: v.tensor_tensor(
                    d, a, r, AluOpType.mult),
                    reads=[bTS[tslot], bPS[r_pidx]], writes=dbufs)

        def load_weights_bf16(dsrc, dstT, dstB):
            for pc in range(16):
                slot = pc % 2
                st = tsc[:, slot * 512:(slot + 1) * 512]
                src = dsrc.ap()[:, pc * 512:(pc + 1) * 512]
                dst = dstT[:, pc * 512:(pc + 1) * 512]
                E_("y", lambda y, d=st, s=src: y.dma_start(d, s),
                   writes=[bTS[slot]], queue="qt%d" % slot)
                E_("v", lambda v, d=dst, s=st: v.tensor_copy(d, s),
                   reads=[bTS[slot]], writes=[dstB[pc // 2]])

        def load_weights_bf16_pair(jobs):
            for pc in range(16):
                for j, (dsrc, dstT, dstB) in enumerate(jobs):
                    slot = j % 2
                    st = tsc[:, slot * 512:(slot + 1) * 512]
                    src = dsrc.ap()[:, pc * 512:(pc + 1) * 512]
                    dst = dstT[:, pc * 512:(pc + 1) * 512]
                    E_("y", lambda y, d=st, s=src: y.dma_start(d, s),
                       writes=[bTS[slot]], queue="qt%d" % slot)
                    E_("v", lambda v, d=dst, s=st: v.tensor_copy(d, s),
                       reads=[bTS[slot]], writes=[dstB[pc // 2]])

        lim = _LIMS[_LIM]
        for s_i in range(2):
            if s_i == 1 and lim < 99:
                break
            xs = dx.ap()[s_i]
            os_ = dout.ap()[s_i]

            z1B = new_era([f"z1_{d}" for d in range(NDC)])

            def emit_sload(hc0):
                for dc in range(NDC):
                    for pc_ in range(2):
                        dst = xh[:, dc * 1024 + pc_ * 512: dc * 1024 + pc_ * 512 + 512]
                        src = xs[:, dc * 2048 + hc0 + pc_ * 512: dc * 2048 + hc0 + pc_ * 512 + 512]
                        E_("y", lambda y, d=dst, s=src: y.dma_start(d, s),
                           writes=[bXH[dc][pc_]], queue="qxh%d" % dc)

            emit_sload(0)
            if s_i == 0:
                load_weights_bf16_pair([(dwk, wqb, bWQ), (dwv, wob, bWO)])
            for half in range(2):
                hc0 = half * 1024
                if half == 1:
                    emit_sload(hc0)
                for pc in range(2):
                    xcols = [(xh[:, dc * 1024 + pc * 512: dc * 1024 + pc * 512 + 512], bXH[dc][pc])
                             for dc in range(NDC)]
                    stats_chunk(xcols, 2, bXH)
                    rowmath_r(bPS[2], PS[2][0:1, :], PS[2][64:65, :], LN_EPS_PRE, 0.0)
                    bcast_f32(5, rowa, bROWA)
                    bcast_f32(4, rowb, bROWB)
                    dsts = []
                    for dc in range(NDC):
                        col = dc * 2048 + hc0 + pc * 512
                        dsts.append((zr[:, col:col + 512], [z1B[dc]]))
                    zgen(xcols, 5, 4, dsts)

            if lim < 0.2:
                break
            E_("v", lambda v: v.memset(k2m[:, 0:1], -1.0), writes=[bK2])
            if lim < 0.4:
                break
            for pr in range(NP):
                for piece in range(4):
                    pq = PS[piece % 2]
                    bpq = bPS[piece % 2]
                    for dc in range(NDC):
                        zsl = zr[:, dc * 2048 + piece * 512: dc * 2048 + piece * 512 + 512]
                        wE = wqb[:, dc * 1024 + (2 * pr) * 64: dc * 1024 + (2 * pr) * 64 + 64]
                        wO_ = wqb[:, dc * 1024 + (2 * pr + 1) * 64: dc * 1024 + (2 * pr + 1) * 64 + 64]
                        E_("t", lambda t, dc=dc, d=pq[0:64, :], w=wE, z=zsl: t.matmul(
                            d, w, z, start=(dc == 0), stop=(dc == NDC - 1), tile_position=(0, 0), skip_group_check=True),
                            reads=[bWQ[dc], z1B[dc]], writes=[bpq])
                        E_("t", lambda t, dc=dc, d=pq[64:128, :], w=wO_, z=zsl: t.matmul(
                            d, w, z, start=(dc == 0), stop=(dc == NDC - 1), tile_position=(0, 64), skip_group_check=True),
                            reads=[bWQ[dc], z1B[dc]], writes=[bpq])
                    kdst = kb[:, pr * 2048 + piece * 512: pr * 2048 + piece * 512 + 512]
                    E_("v", lambda v, d=kdst, a=pq[:]: v.tensor_copy(d, a),
                       reads=[bpq], writes=[bKB])
                    if lim < 0.45:
                        continue
                    ksq = xst[:, 0:512]
                    E_("s", lambda s, a=pq[:], d=ksq: s.activation(d, a, ACT.Square),
                       reads=[bpq], writes=[bXST[0]])
                    if lim < 0.5:
                        continue
                    E_("t", lambda t, a=ksq: t.matmul(
                        PS[6][0:64, :], c_ones64, a, start=True, stop=True,
                        tile_position=(0, 0)),
                        reads=[bXST[0], bCB], writes=[bPS[6]])
                    if lim < 0.55:
                        continue
                    E_("v", lambda v: v.tensor_reduce(
                        k2m[0:1, 1:2], PS[6][0:1, :], AX.X, AluOpType.max),
                        reads=[bPS[6]], writes=[bK2])
                    if lim < 0.6:
                        continue
                    E_("v", lambda v: v.tensor_tensor(
                        k2m[0:1, 0:1], k2m[0:1, 0:1], k2m[0:1, 1:2], AluOpType.max),
                        reads=[bK2], writes=[bK2])
            load_weights_bf16(dwq, wqb, bWQ)
            if lim < 1:
                break
            E_("v", lambda v: v.tensor_scalar(
                mk2[0:1, 1:2], k2m[0:1, 0:1], -0.0625, 0.0, AluOpType.mult, AluOpType.add),
                reads=[bK2, bCF], writes=[bMK2])

            for tk in range(16):
                for hh in range(2):
                    pv = PS[4 + hh]
                    bpv = bPS[4 + hh]
                    for dc in range(NDC):
                        zsl = zr[:, dc * 2048 + tk * 128: dc * 2048 + tk * 128 + 128]
                        wsl = wob[:, dc * 1024 + hh * 512: dc * 1024 + hh * 512 + 512]
                        E_("t", lambda t, dc=dc, d=pv[:], z=zsl, w=wsl: t.matmul(
                            d, z, w, start=(dc == 0), stop=(dc == NDC - 1)),
                            reads=[bWO[dc], z1B[dc]], writes=[bpv])
                    vdst = vb[:, tk * 1024 + hh * 512: tk * 1024 + hh * 512 + 512]
                    E_("v", lambda v, d=vdst, a=pv[:]: v.tensor_copy(d, a),
                       reads=[bpv], writes=[bVB])

            load_weights_bf16(dwo, wob, bWO)

            if lim < 2:
                break
            def emit_xload(half, pc_):
                for dc in range(NDC):
                    dst = xh[:, dc * 1024 + pc_ * 512: dc * 1024 + pc_ * 512 + 512]
                    src = xs[:, dc * 2048 + half * 1024 + pc_ * 512:
                              dc * 2048 + half * 1024 + pc_ * 512 + 512]
                    E_("y", lambda y, d=dst, s=src: y.dma_start(d, s),
                       writes=[bXH[dc][pc_]], queue="qxh%d" % dc)

            for half in range(2):
                if half == 0:
                    for pc_ in range(2):
                        emit_xload(half, pc_)

                names = ([f"zcA{d}" for d in range(NDC)] + [f"zcB{d}" for d in range(NDC)]
                         + [f"an{p}" for p in range(NP)]
                         + ["pb0", "pb1", "qy2"])
                eb = new_era(names)
                zcB = [eb[0:8], eb[8:16]]
                anB = eb[16:24]
                pbB = eb[24:26]; q2B = eb[26]; y2B = eb[26]
                ZC0, AN0, PB0, Q20 = 0, 8192, 12288, 14336
                Y20 = Q20
                zc_ap = [[zr[:, ZC0 + sl * 4096 + d * 512: ZC0 + sl * 4096 + d * 512 + 512]
                          for d in range(NDC)] for sl in range(2)]
                an_ap = [zr[:, AN0 + p * 512: AN0 + p * 512 + 512] for p in range(NP)]
                qs_ap = [qsb_t[:, i * 512:(i + 1) * 512] for i in range(4)]
                q2_ap = zr[:, Q20:Q20 + 512]
                y2_ap = zr[:, Y20:Y20 + 512]

                def att_prep_items(ci):
                    xcol = ci * 512
                    slot = ci % 2
                    xcols = [(xh[:, dc * 1024 + xcol: dc * 1024 + xcol + 512], bXH[dc][ci])
                             for dc in range(NDC)]
                    items = []
                    for dc, (xsl, xbuf) in enumerate(xcols):
                        sl = dc % 2
                        items.append(lambda xsl=xsl, xbuf=xbuf, sl=sl: stats_piece_ve(xsl, xbuf, sl))
                        items.append(lambda dc=dc, sl=sl: stats_piece_mm(dc, sl, 6))
                    items.append(lambda: rowmath_r(bPS[6], PS[6][0:1, :], PS[6][64:65, :],
                                                   LN_EPS_PRE, 0.0))
                    items.append(None)
                    def _mu1():
                        bcast_f32(6, rowa, bROWA)
                        E_("v", lambda v: v.tensor_copy(mub[:], PS[6][:]),
                           reads=[bPS[6]], writes=[bMUB])
                    items.append(_mu1)
                    items.append(None)
                    items.append(lambda: bcast_f32(6, rowb, bROWB))
                    items.append(None)
                    for dc in range(NDC):
                        def _z(dc=dc):
                            xsl = xh[:, dc * 1024 + xcol: dc * 1024 + xcol + 512]
                            tslot = dc % 2
                            tt = tsc[:, tslot * 512:(tslot + 1) * 512]
                            zdst = zc_ap[slot][dc]
                            E_("v", lambda v, a=xsl, d=tt: v.tensor_tensor(
                                d, a, mub[:], AluOpType.subtract),
                                reads=[bXH[dc][xcol // 512], bMUB], writes=[bTS[tslot]])
                            E_("v", lambda v, d=zdst, a=tt: v.tensor_tensor(
                                d, a, PS[6][:], AluOpType.mult),
                                reads=[bTS[tslot], bPS[6]], writes=[zcB[slot][dc]])
                        items.append(_z)
                    return items

                def att_prep(ci):
                    for it in att_prep_items(ci):
                        if it is not None:
                            it()
                    emit_qB(0, ci % 2)

                def emit_qB_stage(pr, slot, stage):
                    qe = 2 * (pr % 2)
                    bslot = pr % 2
                    if stage == 0:
                        for dc in range(NDC):
                            wE = wqb[:, dc * 1024 + (2 * pr) * 64: dc * 1024 + (2 * pr) * 64 + 64]
                            wO_ = wqb[:, dc * 1024 + (2 * pr + 1) * 64: dc * 1024 + (2 * pr + 1) * 64 + 64]
                            E_("t", lambda t, dc=dc, w=wE, z=zc_ap[slot][dc]: t.matmul(
                                PS[7][0:64, :], w, z, start=(dc == 0), stop=(dc == NDC - 1),
                                tile_position=(0, 0), skip_group_check=True),
                                reads=[bWQ[dc], zcB[slot][dc]], writes=[bPS[7]])
                            E_("t", lambda t, dc=dc, w=wO_, z=zc_ap[slot][dc]: t.matmul(
                                PS[7][64:128, :], w, z, start=(dc == 0), stop=(dc == NDC - 1),
                                tile_position=(0, 64), skip_group_check=True),
                                reads=[bWQ[dc], zcB[slot][dc]], writes=[bPS[7]])
                        E_("v", lambda v, d=qs_ap[qe]: v.tensor_copy(d[0:64, :], PS[7][0:64, :]),
                           reads=[bPS[7]], writes=[bQS[qe]])
                        E_("v", lambda v, d=qs_ap[qe + 1]: v.tensor_copy(d[64:128, :], PS[7][64:128, :]),
                           reads=[bPS[7]], writes=[bQS[qe + 1]])
                        E_("s", lambda s: s.activation(q2_ap, PS[7][:], ACT.Square),
                           reads=[bPS[7]], writes=[q2B])
                    elif stage == 1:
                        E_("t", lambda t: t.matmul(PS[7][0:64, :], c_ones64, q2_ap,
                                                   start=True, stop=True, tile_position=(0, 0)),
                           reads=[bCB, q2B], writes=[bPS[7]])
                    elif stage == 2:
                        E_("v", lambda v: v.tensor_reduce(brt[0:1, 0:1], PS[7][0:1, :],
                                                          AX.X, AluOpType.max),
                           reads=[bPS[7]], writes=[bROWT])
                        bcol = browb[0:1, bslot:bslot + 1]
                        E_("v", lambda v, d=bcol: v.scalar_tensor_tensor(
                            d, brt[0:1, 0:1], -0.0625, mk2[0:1, 1:2],
                            AluOpType.mult, AluOpType.add),
                            reads=[bROWT, bMK2], writes=[bBROWB])
                    else:
                        bsl = browb[:, bslot:bslot + 1]
                        E_("t", lambda t, s=bsl: t.matmul(
                            PS[7][0:64, 0:1], c_allones, s, start=True, stop=True,
                            tile_position=(0, 0)),
                            reads=[bBROWB, bCB], writes=[bPS[7]])
                        E_("t", lambda t, s=bsl: t.matmul(
                            PS[7][64:128, 0:1], c_allones, s, start=True, stop=True,
                            tile_position=(0, 64)),
                            reads=[bBROWB, bCB], writes=[bPS[7]])
                        bb = bbias[:, bslot:bslot + 1]
                        E_("v", lambda v, d=bb: v.tensor_copy(d, PS[7][:, 0:1]),
                           reads=[bPS[7]], writes=[bBB[bslot]])

                def emit_qB(pr, slot):
                    for st in range(4):
                        emit_qB_stage(pr, slot, st)

                def att_heads(ci, inject=(), next_qB=None, inject_rate=1):
                    inj = list(inject)
                    ii = [0]

                    def pop_inject():
                        if ii[0] < len(inj):
                            it = inj[ii[0]]
                            ii[0] += 1
                            if it is not None:
                                it()
                    C = half * 2 + ci
                    slot = ci % 2
                    ntk_full = 4 * C
                    ntiles = ntk_full + 4
                    first_sum = {0: True, 64: True}
                    for pr in range(NP):
                        bslot = pr % 2
                        bb = bbias[:, bslot:bslot + 1]
                        abank = 4
                        first_av = {0: True, 64: True}

                        def emit_avsums(tau, parity, n, qoff, tslot2):
                            hd = 2 * pr + parity
                            pap = zr[:, PB0 + tslot2 * 1024 + parity * 512:
                                     PB0 + tslot2 * 1024 + parity * 512 + n]
                            if parity == 0:
                                av_tp, sum_tp = (0, 0), (0, 64)
                                ar0, ar1, sr0, sr1 = 0, 64, 64, 128
                            else:
                                av_tp, sum_tp = (0, 64), (0, 0)
                                ar0, ar1, sr0, sr1 = 64, 128, 0, 64
                            vsl = vb[:, tau * 1024 + hd * 64: tau * 1024 + hd * 64 + 64]
                            E_("t", lambda t, d=PS[abank][ar0:ar1, qoff:qoff + n],
                                   w=vsl, p=pap, st=first_av[ar0], tp=av_tp: t.matmul(
                                d, w, p, start=st, stop=False, tile_position=tp,
                                skip_group_check=True),
                                reads=[bVB, pbB[tslot2]], writes=[bPS[abank]])
                            first_av[ar0] = False
                            E_("t", lambda t, d=PS[5][sr0:sr1, qoff:qoff + n],
                                   w=c_onescol[pr], p=pap, st=first_sum[sr0], tp=sum_tp: t.matmul(
                                d, w, p, start=st, stop=False, tile_position=tp,
                                skip_group_check=True),
                                reads=[bCB, pbB[tslot2]], writes=[bPS[5]])
                            first_sum[sr0] = False

                        pend = []
                        for tau in range(ntiles):
                            if tau < ntk_full:
                                n, qoff = 512, 0
                            else:
                                j = tau - ntk_full
                                n, qoff = 512 - 128 * j, 128 * j
                            tslot2 = tau % 2
                            cur = []
                            for parity in range(2):
                                kT = kb[:, pr * 2048 + tau * 128: pr * 2048 + tau * 128 + 64]
                                kBm = kb[:, pr * 2048 + tau * 128 + 64: pr * 2048 + tau * 128 + 128]
                                qe = 2 * (pr % 2) + parity
                                qsl = qs_ap[qe][:, qoff:qoff + n]
                                sb_ = 2 * tslot2 + parity
                                E_("t", lambda t, d=PS[sb_][0:64, 0:n], w=kT, q=qsl: t.matmul(
                                    d, w, q, start=True, stop=True, tile_position=(0, 0)),
                                    reads=[bKB, bQS[qe]], writes=[bPS[sb_]])
                                E_("t", lambda t, d=PS[sb_][64:128, 0:n], w=kBm, q=qsl: t.matmul(
                                    d, w, q, start=True, stop=True, tile_position=(0, 64)),
                                    reads=[bKB, bQS[qe]], writes=[bPS[sb_]])
                                cur.append((tau, parity, n, qoff, tslot2))
                            if ntk_full == 0 and n < 512:
                                for parity in range(2):
                                    sb_ = 2 * tslot2 + parity
                                    pp = zr[:, PB0 + tslot2 * 1024 + parity * 512:
                                            PB0 + tslot2 * 1024 + parity * 512 + n]
                                    E_("s", lambda s, d=pp, a=PS[sb_][:, 0:n], b=bb:
                                        s.activation(d, a, ACT.Exp, bias=b, scale=0.125),
                                        reads=[bPS[sb_], bBB[bslot]],
                                        writes=[pbB[tslot2]])
                            else:
                                pap = zr[:, PB0 + tslot2 * 1024: PB0 + tslot2 * 1024 + 512 + n]
                                E_("s", lambda s, d=pap, a=PSB[tslot2][:, 0:512 + n], b=bb: s.activation(
                                    d, a, ACT.Exp, bias=b, scale=0.125),
                                    reads=[bPS[2 * tslot2], bPS[2 * tslot2 + 1], bBB[bslot]],
                                    writes=[pbB[tslot2]])
                            if tau >= ntk_full:
                                for parity in range(2):
                                    pm = zr[:, PB0 + tslot2 * 1024 + parity * 512:
                                            PB0 + tslot2 * 1024 + parity * 512 + 128]
                                    E_("v", lambda v, d=pm: v.tensor_tensor(
                                        d, d, c_mask, AluOpType.mult),
                                        reads=[bCB], writes=[pbB[tslot2]])
                            for job in pend:
                                emit_avsums(*job)
                            pend = cur
                            if pr + 1 < NP:
                                if _NOQB:
                                    if tau == 0:
                                        emit_qB(pr + 1, slot)
                                elif tau < 4:
                                    emit_qB_stage(pr + 1, slot, tau)
                            elif next_qB is not None and tau < 4:
                                emit_qB_stage(0, next_qB, tau)
                            for _r in range(inject_rate):
                                pop_inject()
                        for job in pend:
                            emit_avsums(*job)
                        E_("v", lambda v, d=an_ap[pr], a=PS[abank][:]: v.tensor_copy(d, a),
                           reads=[bPS[abank]], writes=[anB[pr]])
                    while ii[0] < len(inj):
                        pop_inject()

                def epi_recips(ci):
                    def _rec(lo, hi):
                        def g(v):
                            with nc.allow_low_precision(reason="bf16 1/S scale for softmax (0.4%)"):
                                return v.reciprocal(invs[lo:hi, :], PS[5][lo:hi, :])
                        return g
                    E_("v", _rec(0, 8), reads=[bPS[5]], writes=[bINV])
                    E_("v", _rec(64, 72), reads=[bPS[5]], writes=[bINV])

                def epi_items(ci, ybks, ibks, lbank, defer):
                    xcol = ci * 512
                    slot = ci % 2
                    if defer:
                        y2ap, y2buf = xst[:, 512:1024], bXST[2]
                    else:
                        y2ap, y2buf = y2_ap, y2B
                    items = []
                    for pr in range(NP):
                        def _ia(pr=pr, ibank=ibks[pr % 2]):
                            E_("t", lambda t, d=PS[ibank][0:64, :]: t.matmul(
                                d, c_eselT[pr], invs[:], start=True, stop=True, tile_position=(0, 0)),
                                reads=[bINV, bCB], writes=[bPS[ibank]])
                            E_("t", lambda t, d=PS[ibank][64:128, :]: t.matmul(
                                d, c_eselB[pr], invs[:], start=True, stop=True, tile_position=(0, 64)),
                                reads=[bINV, bCB], writes=[bPS[ibank]])
                            E_("v", lambda v, d=an_ap[pr], i=PS[ibank][:]: v.tensor_tensor(
                                d, d, i, AluOpType.mult),
                                reads=[bPS[ibank]], writes=[anB[pr]])
                        items.append(_ia)
                    for d_ in range(NDC):
                        def _wo(d_=d_, ybk=ybks[d_ % 2]):
                            for pr in range(NP):
                                wT = wob[:, pr * 1024 + d_ * 128: pr * 1024 + d_ * 128 + 64]
                                wB = wob[:, pr * 1024 + d_ * 128 + 64: pr * 1024 + d_ * 128 + 128]
                                E_("t", lambda t, pr=pr, d=PS[ybk][0:64, :], w=wT, a=an_ap[pr]: t.matmul(
                                    d, w, a, start=(pr == 0), stop=(pr == NP - 1), tile_position=(0, 0), skip_group_check=True),
                                    reads=[bWO[pr], anB[pr]], writes=[bPS[ybk]])
                                E_("t", lambda t, pr=pr, d=PS[ybk][64:128, :], w=wB, a=an_ap[pr]: t.matmul(
                                    d, w, a, start=(pr == 0), stop=(pr == NP - 1), tile_position=(0, 64), skip_group_check=True),
                                    reads=[bWO[pr], anB[pr]], writes=[bPS[ybk]])
                            yb = zc_ap[slot][d_]
                            E_("v", lambda v, d=yb, y=PS[ybk][:]: v.tensor_tensor(
                                d, d, y, AluOpType.add),
                                reads=[bPS[ybk]], writes=[zcB[slot][d_]])
                            E_("s", lambda s, a=yb: s.activation(y2ap, a, ACT.Square),
                               reads=[zcB[slot][d_]], writes=[y2buf])
                            E_("t", lambda t, d_=d_, a=yb: t.matmul(
                                PS[lbank][0:64, :], c_ones64, a,
                                start=(d_ == 0), stop=(d_ == NDC - 1), tile_position=(0, 0), skip_group_check=True),
                                reads=[bCB, zcB[slot][d_]], writes=[bPS[lbank]])
                            E_("t", lambda t, d_=d_: t.matmul(
                                PS[lbank][64:128, :], c_ones64, y2ap,
                                start=(d_ == 0), stop=(d_ == NDC - 1), tile_position=(0, 64), skip_group_check=True),
                                reads=[bCB, y2buf], writes=[bPS[lbank]])
                        items.append(_wo)
                    items.append(lambda: rowmath_r(bPS[lbank], PS[lbank][0:1, :],
                                                   PS[lbank][64:65, :], LN_EPS_3, 0.0))
                    items.append(None)
                    def _mu3():
                        bcast_f32(lbank, rowa, bROWA)
                        E_("v", lambda v: v.tensor_copy(mub[:], PS[lbank][:]),
                           reads=[bPS[lbank]], writes=[bMUB])
                    items.append(_mu3)
                    items.append(None)
                    def _r3():
                        bcast_f32(lbank, rowb, bROWB)
                        E_("v", lambda v: v.tensor_copy(rub[:], PS[lbank][:]),
                           reads=[bPS[lbank]], writes=[bRUB])
                    items.append(_r3)
                    items.append(None)
                    for dp in range(4):
                        def _pb(dp=dp):
                            for d_ in (2 * dp, 2 * dp + 1):
                                tslot = d_ % 2
                                tt = tsc[:, tslot * 512:(tslot + 1) * 512]
                                yb = zc_ap[slot][d_]
                                E_("v", lambda v, a=yb, d=tt: v.tensor_tensor(
                                    d, a, mub[:], AluOpType.subtract),
                                    reads=[bMUB, zcB[slot][d_]], writes=[bTS[tslot]])
                                E_("v", lambda v, d=tt: v.tensor_tensor(
                                    d, d, rub[:], AluOpType.mult),
                                    reads=[bRUB], writes=[bTS[tslot]])
                                gcol = sgb[:, d_:d_ + 1]
                                bcol = sgb[:, 8 + d_:8 + d_ + 1]
                                E_("v", lambda v, d=tt, g=gcol, b=bcol: v.tensor_scalar(
                                    d, d, g, b, AluOpType.mult, AluOpType.add),
                                    reads=[bGB], writes=[bTS[tslot]])
                                xsl = xh[:, d_ * 1024 + xcol: d_ * 1024 + xcol + 512]
                                E_("v", lambda v, x=xsl, d=tt: v.tensor_tensor(
                                    x, x, d, AluOpType.add),
                                    reads=[bTS[tslot], bXH[d_][xcol // 512]], writes=[bXH[d_][xcol // 512]])
                        items.append(_pb)
                    return items

                def att_epilogue(ci):
                    epi_recips(ci)
                    for it in epi_items(ci, (4, 6), (5, 7), 5, False):
                        if it is not None:
                            it()

                def emit_guw_dma(hc):
                    for j, dw_ in ((0, dwg), (1, dwu)):
                        sl = (2 * hc + j) % 3
                        stg = wstag[:, sl * 1024:(sl + 1) * 1024]
                        E_("g", lambda g, d=stg, s=dw_.ap()[hc]: g.dma_start(d, s),
                           writes=[bWST[sl]], queue="qw%d" % sl)

                def emit_guw_cast(hc):
                    gsl = (2 * hc) % 3
                    usl = (2 * hc + 1) % 3
                    wsl = hc % 2
                    gb_ = wgbb[:, wsl * 1024:(wsl + 1) * 1024]
                    ub_ = wubb[:, wsl * 1024:(wsl + 1) * 1024]
                    E_("v", lambda v, d=gb_, s=wstag[:, gsl * 1024:(gsl + 1) * 1024]: v.tensor_copy(d, s),
                       reads=[bWST[gsl]], writes=[bWG[wsl]])
                    E_("s", lambda s, d=ub_, a=wstag[:, usl * 1024:(usl + 1) * 1024]: s.activation(d, a, ACT.Copy),
                       reads=[bWST[usl]], writes=[bWU[wsl]])

                def emit_wd_dma(hc):
                    sl = hc % 3
                    stg = wstag[:, sl * 1024:(sl + 1) * 1024]
                    E_("g", lambda g, d=stg, s=dwd.ap()[hc]: g.dma_start(d, s),
                       writes=[bWST[sl]], queue="qw%d" % sl)

                z3_ap = [zr[:, d * 512:(d + 1) * 512] for d in range(NDC)]

                def h_region(k):
                    if k < 8:
                        return 12288 + k * 512
                    if k < 16:
                        return 8192 + (k - 8) * 512
                    return 4096 + (k - 16) * 512
                h_ap = [zr[:, h_region(k): h_region(k) + 512] for k in range(NHC)]

                att_prep(0)
                if _NOPREP:
                    att_heads(0)
                    att_prep(1)
                else:
                    att_heads(0, inject=att_prep_items(1), next_qB=1, inject_rate=2)
                att_epilogue(0)
                z3B0 = rebind([f"z3a_{d}" for d in range(NDC)], zcB[0])
                emit_guw_dma(0)
                if _INJL < 1:
                    it0 = []
                else:
                    it0 = ([None] * 16 + ffn_stat_items(0, 6, True)
                           + ffn_zgen_items(0, 6, z3B0, z3_ap, True)
                           + [lambda: emit_guw_cast(0)])
                att_heads(1, inject=it0)
                _DEFER = (not _NODEF) and lim >= 3 and _INJL >= 2
                if _DEFER:
                    epi_recips(1)
                    epi1_gu = epi_items(1, (0, 2), (1, 1), 3, True)
                else:
                    att_epilogue(1)
                    epi1_gu = []

                if lim < 3:
                    break
                if _DEFER:
                    hB = (rebind([f"h{k}" for k in range(8)], [pbB[0], pbB[1], q2B])
                          + [None] * 16)
                else:
                    hB = rebind([f"h{k}" for k in range(NHC)], eb[8:27])
                z3B1 = None
                wkv_items = []
                if half == 1 and s_i == 0 and lim >= 99:
                    for _pc in range(16):
                        for _j, (_dsrc, _dstT, _dstB) in enumerate(
                                ((dwk, wqb, bWQ), (dwv, wob, bWO))):
                            def _ld(pc=_pc, dsrc=_dsrc, dstT=_dstT, dstB=_dstB, slot=_j % 2):
                                st = tsc[:, slot * 512:(slot + 1) * 512]
                                sap = dsrc.ap()[:, pc * 512:(pc + 1) * 512]
                                dst = dstT[:, pc * 512:(pc + 1) * 512]
                                E_("y", lambda y, d=st, s=sap: y.dma_start(d, s),
                                   writes=[bTS[slot]], queue="qt%d" % slot)
                                E_("v", lambda v, d=dst, s=st: v.tensor_copy(d, s),
                                   reads=[bTS[slot]], writes=[dstB[pc // 2]])
                            wkv_items.append(_ld)

                for tf in range(2):
                    xcol = tf * 512
                    if tf == 0:
                        z3Bl = z3B0
                        if _INJL < 1:
                            for it in (ffn_stat_items(0, 6, True)
                                       + ffn_zgen_items(0, 6, z3B0, z3_ap, True)):
                                if it is not None:
                                    it()
                            emit_guw_cast(0)
                        if _INJL >= 2:
                            _gfull = ffn_stat_items(1, 2, False, use_qsb=True)
                            _gn = int(_os.environ.get("KMK_INJN", "99"))
                            gu_inject = epi1_gu + _gfull[:_gn]
                            gu_rest = _gfull[_gn:]
                        else:
                            gu_inject = []
                            gu_rest = []
                    else:
                        z3Bl = z3B1
                        gu_inject = []
                        emit_guw_dma(0)
                        emit_guw_cast(0)
                    gii = [0]

                    def pop_gu():
                        if gii[0] < len(gu_inject):
                            it = gu_inject[gii[0]]
                            gii[0] += 1
                            if it is not None:
                                it()
                    for hc in range(NHC):
                        if tf == 0 and _DEFER:
                            if hc == 8:
                                hB[8:16] = rebind([f"h{k}" for k in range(8, 16)], anB)
                            elif hc == 16:
                                hB[16:24] = rebind([f"h{k}" for k in range(16, 24)], zcB[1])
                        wsl = hc % 2
                        if hc + 1 < NHC:
                            emit_guw_dma(hc + 1)
                        gB = 4 + 2 * ((hc + 1) % 2)
                        uB = gB + 1
                        for dc in range(NDC):
                            wgsl = wgbb[:, wsl * 1024 + dc * 128: wsl * 1024 + (dc + 1) * 128]
                            wusl = wubb[:, wsl * 1024 + dc * 128: wsl * 1024 + (dc + 1) * 128]
                            E_("t", lambda t, dc=dc, w=wgsl, z=z3_ap[dc], d=PS[gB][:]: t.matmul(
                                d, w, z, start=(dc == 0), stop=(dc == NDC - 1)),
                                reads=[bWG[wsl], z3Bl[dc]], writes=[bPS[gB]])
                            E_("t", lambda t, dc=dc, w=wusl, z=z3_ap[dc], d=PS[uB][:]: t.matmul(
                                d, w, z, start=(dc == 0), stop=(dc == NDC - 1)),
                                reads=[bWU[wsl], z3Bl[dc]], writes=[bPS[uB]])
                        if hc + 1 < NHC:
                            emit_guw_cast(hc + 1)
                        pop_gu()
                        sg_ = xst[:, 1024 + (hc % 2) * 512: 1024 + (hc % 2 + 1) * 512]
                        E_("s", lambda s, d=sg_, a=PS[gB][:]: s.activation(d, a, ACT.Silu),
                           reads=[bPS[gB]], writes=[bXST[2 + 2 * (hc % 2) - 1 if False else 2 * (hc % 2) + 1]])
                        E_("v", lambda v, d=h_ap[hc], a=sg_, u=PS[uB][:]: v.tensor_tensor(
                            d, a, u, AluOpType.mult),
                            reads=[bXST[2 * (hc % 2) + 1], bPS[uB]], writes=[hB[hc]])
                        pop_gu()
                    while gii[0] < len(gu_inject):
                        pop_gu()
                    if tf == 0:
                        for it in gu_rest:
                            if it is not None:
                                it()
                        z3B1 = rebind([f"z3b_{d}" for d in range(NDC)], z3B0)
                        dn_inject = ffn_zgen_items(1, 0, z3B1, z3_ap, False) + wkv_items[:24]
                        if _INJL < 2:
                            for it in ffn_stat_items(1, 2, False):
                                if it is not None:
                                    it()
                        if _INJL < 3:
                            for it in dn_inject:
                                it()
                            dn_inject = []
                    else:
                        dn_inject = list(wkv_items[24:])
                    dii = [0]

                    def pop_dn():
                        if dii[0] < len(dn_inject):
                            it = dn_inject[dii[0]]
                            dii[0] += 1
                            it()
                    for hc in range(3):
                        emit_wd_dma(hc)
                    for hc in range(NHC):
                        sl = hc % 3
                        wsl = hc % 2
                        stg = wstag[:, sl * 1024:(sl + 1) * 1024]
                        db_ = wdbb[:, wsl * 1024:(wsl + 1) * 1024]
                        E_("s", lambda s, d=db_, a=stg: s.activation(
                            d, a, ACT.Copy, scale=1.0 / 127.0),
                            reads=[bWST[sl], bCF], writes=[bWD[wsl]])
                        if hc + 3 < NHC:
                            emit_wd_dma(hc + 3)
                        pop_dn()
                        pop_dn()
                        for d_ in range(NDC):
                            wdsl = wdbb[:, wsl * 1024 + d_ * 128: wsl * 1024 + (d_ + 1) * 128]
                            E_("t", lambda t, w=wdsl, h=h_ap[hc], d=PS[d_][:], hc=hc: t.matmul(
                                d, w, h, start=(hc == 0), stop=(hc == NHC - 1)),
                                reads=[bWD[wsl], hB[hc]], writes=[bPS[d_]])
                    if tf == 0:
                        emit_guw_dma(0)
                    for d_ in range(NDC):
                        xsl = xh[:, d_ * 1024 + xcol: d_ * 1024 + xcol + 512]
                        E_("v", lambda v, x=xsl, f=PS[d_][:]: v.tensor_tensor(
                            x, x, f, AluOpType.add),
                            reads=[bPS[d_], bXH[d_][tf]], writes=[bXH[d_][tf]])
                        odst = os_[:, d_ * 2048 + half * 1024 + tf * 512:
                                   d_ * 2048 + half * 1024 + tf * 512 + 512]
                        E_("g", lambda ge, d=odst, s=xsl: ge.dma_start(d, s),
                           reads=[bXH[d_][tf]], queue="qo" if tf == 0 else "qo1")
                        nstores[tf] += 1
                    if half == 0 and lim >= 4:
                        emit_xload(1, tf)
                if half == 0 and lim == 3:
                    break

        P.ops["g"].append(([Tok("qo", nstores[0] * 16, True),
                           Tok("qo1", nstores[1] * 16, True)],
                          None, Tok("g", -1, False), None))

        semval = P.finalize()

        def replay(eng_obj, ename):
            for waits, fn, tok, queue in P.ops[ename]:
                for t in waits:
                    wv = t.idx if t.queue else semval[(t.src, t.idx)]
                    assert wv is not None, (ename, t.src, t.idx)
                    eng_obj.wait_ge(SEMS[t.src], wv)
                if fn is None:
                    continue
                inst = fn(eng_obj)
                if queue is not None:
                    inst.then_inc(SEMS[tok.src], 16)
                elif (tok.src, tok.idx) in P.referenced:
                    inst.then_inc(SEMS[tok.src], 1)

        @block.tensor
        def _(tensor):
            replay(tensor, "t")

        @block.vector
        def _(vector):
            replay(vector, "v")

        @block.scalar
        def _(scalar):
            replay(scalar, "s")

        @block.gpsimd
        def _(gpsimd):
            replay(gpsimd, "g")

        @block.sync
        def _(sync):
            replay(sync, "y")

    return nc, {e: len(P.ops[e]) for e in P.ENGS}


_NC_CACHE = [None]


def _host_inputs(inputs):
    x = np.asarray(inputs["L0_attn_x"])
    wq = np.asarray(inputs["L0_attn_Wq"])
    wk = np.asarray(inputs["L0_attn_Wk"])
    wv = np.asarray(inputs["L0_attn_Wv"])
    wo = np.asarray(inputs["L0_attn_Wo"])
    gam = np.asarray(inputs["L0_attn_gamma"])
    bet = np.asarray(inputs["L0_attn_beta"])
    wg = np.asarray(inputs["L0_ffn_w_gate"])
    wu = np.asarray(inputs["L0_ffn_w_up"])
    wd = np.asarray(inputs["L0_ffn_w_down"])

    def whead(w):
        return np.ascontiguousarray(
            w.reshape(H, NDC, 128, E).transpose(2, 1, 0, 3).reshape(128, NDC * H * E))

    m = {}
    m["wq"] = whead(wq)
    m["wk"] = whead(wk)
    m["wv"] = whead(wv)
    m["wo"] = np.ascontiguousarray(wo.reshape(NP, 128, D).transpose(1, 0, 2).reshape(128, NP * D))
    gb = np.zeros((128, 16), dtype=np.float32)
    gb[:, 0:8] = gam.reshape(8, 128).T
    gb[:, 8:16] = bet.reshape(8, 128).T
    m["gb"] = gb
    m["wg"] = np.ascontiguousarray(
        wg.reshape(NHC, 128, NDC, 128).transpose(0, 3, 2, 1).reshape(NHC, 128, D))
    m["wu"] = np.ascontiguousarray(
        wu.reshape(NHC, 128, NDC, 128).transpose(0, 3, 2, 1).reshape(NHC, 128, D))
    m["wd"] = np.ascontiguousarray(wd.T.reshape(NHC, 128, D))

    cbv = np.zeros((128, 1792), dtype=np.float32)
    cbv[:, 0] = 1.0
    for j in range(8):
        cbv[:, 64 + j * 64 + j] = 1.0
    for j in range(8):
        cbv[64 + j, 576 + (j * 2) * 64: 576 + (j * 2) * 64 + 64] = 1.0
        cbv[j, 576 + (j * 2 + 1) * 64: 576 + (j * 2 + 1) * 64 + 64] = 1.0
    tri = np.triu(np.ones((128, 128), dtype=np.float32))
    cbv[:, 1600:1728] = tri
    cbv[:, 1728:1792] = 1.0
    m["cb"] = cbv
    cfv = np.zeros((128, 144), dtype=np.float32)
    cfv[0, 0:128] = 1.0
    _CONSTVALS = [1e-05, 0.001, -LN127, -0.0625, 0.0625, 0.125, -0.125,
                  1.0 / 1024.0, 1.0 / 127.0, 0.5, -0.5, 2.0, -1.0, 3.0]
    for _i, _cv in enumerate(_CONSTVALS):
        cfv[:, 128 + _i] = _cv
    m["cf"] = cfv

    per_core_x = []
    for c in range(NCORE):
        xsh = np.empty((2, 128, NDC * SEQ), dtype=np.float32)
        for si in range(2):
            xi = x[2 * c + si]
            xsh[si] = xi.T.reshape(NDC, 128, SEQ).transpose(1, 0, 2).reshape(128, NDC * SEQ)
        per_core_x.append(np.ascontiguousarray(xsh))
    return m, per_core_x


def kernel(**inputs):
    m, per_core_x = _host_inputs(inputs)
    if _NC_CACHE[0] is None:
        _NC_CACHE[0] = build_program()[0]
    nc = _NC_CACHE[0]
    in_maps = []
    for c in range(NCORE):
        im = dict(m)
        im["x"] = per_core_x[c]
        in_maps.append(im)
    res = run_bass_kernel_spmd(nc, in_maps, core_ids=list(range(NCORE)))
    out = np.empty((B_TOT, SEQ, D), dtype=np.float32)
    for c in range(NCORE):
        o = res.results[c]["o"]
        for si in range(2):
            out[2 * c + si] = o[si].reshape(128, NDC, SEQ).transpose(2, 1, 0).reshape(SEQ, D)
    return out


if __name__ == "__main__":
    nc, counts = build_program()
    print("instruction counts:", counts, "total:", sum(counts.values()))


# version 27
# speedup vs baseline: 1.0167x; 1.0104x over previous
import numpy as np
import ml_dtypes
import concourse.bass as bass
import concourse.mybir as mybir
from concourse.alu_op_type import AluOpType
from concourse.bass_utils import run_bass_kernel_spmd

F32 = mybir.dt.float32
BF16 = mybir.dt.bfloat16
I32 = mybir.dt.int32
ACT = mybir.ActivationFunctionType
AX = mybir.AxisListType

NCORE = 8
B_TOT = 16
SEQ = 2048
D = 1024
H = 16
E = 64
NDC = 8
NP = 8
CH = 512
NCHUNK = 4
NHC = 24
LN_EPS_PRE = 1e-5
LN_EPS_3 = 1e-3
LN127 = float(np.log(127.0))


class Tok:
    __slots__ = ("src", "idx", "queue")
    def __init__(self, src, idx, queue):
        self.src, self.idx, self.queue = src, idx, queue


class Buf:
    __slots__ = ("w", "r", "name", "psum")
    def __init__(self, name="", psum=False):
        self.w, self.r, self.name = None, [], name
        self.psum = psum


class Plan:
    ENGS = ("t", "v", "s", "g", "y")

    def __init__(self):
        self.ops = {e: [] for e in self.ENGS}
        self.cnt = {e: 0 for e in self.ENGS}
        self.qcnt = {}
        self.waited = {e: {} for e in self.ENGS}
        self.referenced = set()

    def emit(self, eng, fn, reads=(), writes=(), queue=None):
        deps = []
        for b in reads:
            if b.w is not None:
                deps.append((b.w, True))
            if b.psum and eng in ("v", "s"):
                for t in b.r:
                    if t.src in ("v", "s") and t.src != eng:
                        deps.append((t, False))
        qskip = []
        for b in writes:
            if b.w is not None:
                deps.append((b.w, False))
                if b.r:
                    qskip.append(b.w)
            for t in b.r:
                deps.append((t, False))
                qskip.append(t)
        for b in reads:
            if b.w is not None:
                qskip.append(b.w)
        waits = {}
        for t, is_raw in deps:
            if t.src == eng and not t.queue and eng == "t":
                continue
            if (queue is not None and t.queue and t.src == queue
                    and t in qskip):
                continue
            if t.queue:
                t = Tok(t.src, self.qcnt[t.src], True)
            cur = waits.get(t.src)
            if cur is None or t.idx > cur.idx:
                waits[t.src] = t
        final = []
        for src, t in waits.items():
            if self.waited[eng].get(src, -1) >= t.idx:
                continue
            self.waited[eng][src] = t.idx
            final.append(t)
            if not t.queue:
                self.referenced.add((t.src, t.idx))
        if queue is not None:
            self.qcnt[queue] = self.qcnt.get(queue, 0) + 16
            tok = Tok(queue, self.qcnt[queue], True)
        else:
            self.cnt[eng] += 1
            tok = Tok(eng, self.cnt[eng], False)
        self.ops[eng].append((final, fn, tok, queue))
        for b in reads:
            b.r.append(tok)
        for b in writes:
            b.w = tok
            b.r = []
        return tok

    def finalize(self):
        semval = {}
        for e in self.ENGS:
            c = 0
            for (waits, fn, tok, queue) in self.ops[e]:
                if tok.queue:
                    semval[(tok.src, tok.idx)] = tok.idx
                elif (tok.src, tok.idx) in self.referenced:
                    c += 1
                    semval[(tok.src, tok.idx)] = c
                else:
                    semval[(tok.src, tok.idx)] = None
        return semval


import os as _os
_LIM = _os.environ.get("KMK_LIM", "full")
_NOINJ = _os.environ.get("KMK_NOINJ", "0") == "1"
_INJL = 0 if _NOINJ else int(_os.environ.get("KMK_INJL", "3"))
_NOQB = _os.environ.get("KMK_NOQB", "0") == "1"
_NODEF = _os.environ.get("KMK_NODEF", "0") == "1"
_NOPREP = _os.environ.get("KMK_NOPREP", "1") == "1"
_LIMS = {"s": 0, "qkv_w": 0.2, "qkv_k": 0.4, "k2a": 0.45, "k2b": 0.5, "k2c": 0.55, "qkv_k2": 0.6, "qkv": 1, "attn0": 2, "half0": 3, "seq0": 4, "full": 99}


def build_program():
    nc = bass.Bass("TRN2", target_bir_lowering=False)

    dx = nc.dram_tensor("x", [2, 128, NDC * SEQ], F32, kind="ExternalInput")
    dwq = nc.dram_tensor("wq", [128, NDC * H * E], F32, kind="ExternalInput")
    dwk = nc.dram_tensor("wk", [128, NDC * H * E], F32, kind="ExternalInput")
    dwv = nc.dram_tensor("wv", [128, NDC * H * E], F32, kind="ExternalInput")
    dwo = nc.dram_tensor("wo", [128, NP * D], F32, kind="ExternalInput")
    dgb = nc.dram_tensor("gb", [128, 16], F32, kind="ExternalInput")
    dwg = nc.dram_tensor("wg", [NHC, 128, D], I32, kind="ExternalInput")
    dwu = nc.dram_tensor("wu", [NHC, 128, D], I32, kind="ExternalInput")
    dwd = nc.dram_tensor("wd", [NHC, 128, D], I32, kind="ExternalInput")
    dcb = nc.dram_tensor("cb", [128, 1792], F32, kind="ExternalInput")
    dcf = nc.dram_tensor("cf", [128, 144], F32, kind="ExternalInput")
    dout = nc.dram_tensor("o", [2, 128, NDC * SEQ], F32, kind="ExternalOutput")

    P = Plan()

    from contextlib import ExitStack
    with ExitStack() as _stack:
        ent = _stack.enter_context
        xh = ent(nc.sbuf_tensor("xh", [128, 8192], F32))
        zr = ent(nc.sbuf_tensor("zr", [128, 16384], BF16))
        kb = ent(nc.sbuf_tensor("kb", [128, NP * SEQ], BF16))
        vb = ent(nc.sbuf_tensor("vb", [128, 16 * 1024], BF16))
        wqb = ent(nc.sbuf_tensor("wqb", [128, 8192], BF16))
        wob = ent(nc.sbuf_tensor("wob", [128, 8192], BF16))
        wstag = ent(nc.sbuf_tensor("wstag", [128, 3072], I32))
        qsb_t = ent(nc.sbuf_tensor("qsb", [128, 2048], BF16))
        brt = ent(nc.sbuf_tensor("brt", [128, 2], F32))
        wgbb = ent(nc.sbuf_tensor("wgbb", [128, 2048], BF16))
        wubb = ent(nc.sbuf_tensor("wubb", [128, 2048], BF16))
        wdbb = ent(nc.sbuf_tensor("wdbb", [128, 2048], BF16))
        tsc = ent(nc.sbuf_tensor("tsc", [128, 1024], F32))
        invs = ent(nc.sbuf_tensor("invs", [128, 512], BF16))
        mub = ent(nc.sbuf_tensor("mub", [128, 512], BF16))
        rub = ent(nc.sbuf_tensor("rub", [128, 512], BF16))
        rowa = ent(nc.sbuf_tensor("rowa", [128, 512], F32))
        rowb = ent(nc.sbuf_tensor("rowb", [128, 512], F32))
        browb = ent(nc.sbuf_tensor("browb", [128, 8], BF16))
        bbias = ent(nc.sbuf_tensor("bbias", [128, 2], F32))
        k2m = ent(nc.sbuf_tensor("k2m", [128, 2], F32))
        mk2 = ent(nc.sbuf_tensor("mk2", [128, 2], F32))
        sgb = ent(nc.sbuf_tensor("sgb", [128, 16], F32))
        xst = ent(nc.sbuf_tensor("xst", [128, 2048], BF16))
        cbt = ent(nc.sbuf_tensor("cbt", [128, 1792], BF16))
        cft = ent(nc.sbuf_tensor("cft", [128, 144], F32))
        ps01 = ent(nc.psum_tensor("ps01", [128, 1024], F32))
        ps23 = ent(nc.psum_tensor("ps23", [128, 1024], F32))
        ps4 = ent(nc.psum_tensor("ps4", [128, 512], F32))
        ps5 = ent(nc.psum_tensor("ps5", [128, 512], F32))
        ps6 = ent(nc.psum_tensor("ps6", [128, 512], F32))
        ps7 = ent(nc.psum_tensor("ps7", [128, 512], F32))
        sem_t = ent(nc.semaphore("sem_t"))
        sem_v = ent(nc.semaphore("sem_v"))
        sem_s = ent(nc.semaphore("sem_s"))
        sem_g = ent(nc.semaphore("sem_g"))
        sem_y = ent(nc.semaphore("sem_y"))
        sem_qc = ent(nc.semaphore("sem_qc"))
        sem_qxh = [ent(nc.semaphore("sem_qxh%d" % i)) for i in range(8)]
        sem_qt0 = ent(nc.semaphore("sem_qt0"))
        sem_qt1 = ent(nc.semaphore("sem_qt1"))
        sem_qw0 = ent(nc.semaphore("sem_qw0"))
        sem_qw1 = ent(nc.semaphore("sem_qw1"))
        sem_qw2 = ent(nc.semaphore("sem_qw2"))
        sem_qw3 = ent(nc.semaphore("sem_qw3"))
        sem_qo = ent(nc.semaphore("sem_qo"))
        sem_qo1 = ent(nc.semaphore("sem_qo1"))
        with nc.Block() as blk0:
            @blk0.vector
            def _(v0):
                for _sm in ([sem_t, sem_v, sem_s, sem_g, sem_y, sem_qc,
                             sem_qt0, sem_qt1, sem_qw0, sem_qw1, sem_qw2, sem_qw3,
                             sem_qo, sem_qo1]
                            + list(sem_qxh)):
                    v0.sem_clear(_sm)
        block = ent(nc.Block())
        class _PsView:
            __slots__ = ("t", "coff")
            def __init__(self, t, coff):
                self.t, self.coff = t, coff
            def __getitem__(self, idx):
                if isinstance(idx, tuple):
                    rows, cols = idx
                else:
                    rows, cols = idx, slice(None)
                c0 = cols.start or 0
                c1 = 512 if cols.stop is None else cols.stop
                return self.t[rows, self.coff + c0: self.coff + c1]
        PS = [_PsView(ps01, 0), _PsView(ps01, 512), _PsView(ps23, 0), _PsView(ps23, 512),
              ps4, ps5, ps6, ps7]
        PSB = [ps01, ps23]
        SEMS = {"t": sem_t, "v": sem_v, "s": sem_s, "g": sem_g, "y": sem_y,
                "qc": sem_qc, "qt0": sem_qt0, "qt1": sem_qt1,
                "qw0": sem_qw0, "qw1": sem_qw1, "qw2": sem_qw2, "qw3": sem_qw3,
                "qo": sem_qo, "qo1": sem_qo1}
        for _i in range(8):
            SEMS["qxh%d" % _i] = sem_qxh[_i]

        c_ones64 = cbt[:, 0:64]
        c_onescol = [cbt[:, 64 + j * 64: 128 + j * 64] for j in range(8)]
        c_eselT = [cbt[:, 576 + (j * 2) * 64: 576 + (j * 2) * 64 + 64] for j in range(8)]
        c_eselB = [cbt[:, 576 + (j * 2 + 1) * 64: 576 + (j * 2 + 1) * 64 + 64] for j in range(8)]
        c_mask = cbt[:, 1600:1728]
        c_allones = cbt[:, 1728:1792]
        c_onesrow0 = cft[:, 0:128]
        _CONSTVALS = [1e-05, 0.001, -LN127, -0.0625, 0.0625, 0.125, -0.125,
                      1.0 / 1024.0, 1.0 / 127.0, 0.5, -0.5, 2.0, -1.0, 3.0]
        for _i, _cv in enumerate(_CONSTVALS):
            nc.const_aps.register(F32, float(_cv), cft[:, 128 + _i:129 + _i])

        bXH = [[Buf(f"xh{d}_{p}") for p in range(2)] for d in range(NDC)]
        bKB, bVB = Buf("kb"), Buf("vb")
        bWQ = [Buf(f"wq{i}") for i in range(8)]
        bWO = [Buf(f"wo{i}") for i in range(8)]
        bPS = [Buf(f"ps{i}", psum=True) for i in range(8)]
        bWST = [Buf("wst0"), Buf("wst1"), Buf("wst2")]
        bQS = [Buf(f"qs{i}") for i in range(4)]
        bWG = [Buf("wg0"), Buf("wg1")]
        bWU = [Buf("wu0"), Buf("wu1")]
        bWD = [Buf("wd0"), Buf("wd1")]
        bTS = [Buf("ts0"), Buf("ts1")]
        bINV, bROWA, bROWB, bROWT = Buf("invs"), Buf("rowa"), Buf("rowb"), Buf("rowt")
        bMUB = Buf("mub")
        bRUB = Buf("rub")
        bBROWB, bK2, bMK2, bGB = Buf("browb"), Buf("k2m"), Buf("mk2"), Buf("sgb")
        bBB = [Buf("bb0"), Buf("bb1")]
        bXST = [Buf("xst0"), Buf("xst1"), Buf("xst2"), Buf("xst3")]
        bCB, bCF = Buf("cb"), Buf("cf")

        zr_era = [[]]

        def new_era(bufnames):
            old_toks = []
            for b in zr_era[0]:
                if b.w is not None:
                    old_toks.append(b.w)
                old_toks.extend(b.r)
            out = []
            for n in bufnames:
                b = Buf(n)
                b.r = list(old_toks)
                out.append(b)
            zr_era[0] = list(out)
            return out

        def rebind(bufnames, old):
            toks = []
            for b in old:
                if b.w is not None:
                    toks.append(b.w)
                toks.extend(b.r)
            out = []
            for n in bufnames:
                b = Buf(n)
                b.r = list(toks)
                out.append(b)
            for b in old:
                zr_era[0].remove(b)
            zr_era[0].extend(out)
            return out

        E_ = P.emit

        E_("y", lambda y: y.dma_start(tsc[:], dcb.ap()[:, 0:1024]), writes=[bTS[0], bTS[1]], queue="qc")
        E_("y", lambda y: y.dma_start(cft[:], dcf.ap()), writes=[bCF], queue="qc")
        E_("y", lambda y: y.dma_start(sgb[:], dgb.ap()), writes=[bGB], queue="qc")
        E_("v", lambda v: v.tensor_copy(cbt[:, 0:1024], tsc[:]), reads=[bTS[0], bTS[1], bCF, bGB], writes=[bCB])
        E_("y", lambda y: y.dma_start(tsc[:, 0:768], dcb.ap()[:, 1024:1792]), writes=[bTS[0], bTS[1]], queue="qc")
        E_("v", lambda v: v.tensor_copy(cbt[:, 1024:1792], tsc[:, 0:768]), reads=[bTS[0], bTS[1]], writes=[bCB])
        E_("v", lambda v: v.memset(invs[:], 0.0), writes=[bINV])
        E_("v", lambda v: v.memset(qsb_t[:], 0.0), writes=list(bQS))
        E_("v", lambda v: v.memset(rowa[:], 0.0), writes=[bROWA])
        E_("v", lambda v: v.memset(rowb[:], 0.0), writes=[bROWB])
        E_("v", lambda v: v.memset(browb[:], 0.0), writes=[bBROWB])

        for _k in range(8):
            E_("t", lambda t, d=PS[_k][:, 0:512]: t.matmul(
                d, c_onesrow0, rowa[:, 0:512], start=True, stop=True),
                reads=[bROWA, bCF], writes=[bPS[_k]])

        nstores = [0, 0]

        def rowmath_r(stat_buf, s1_ap, s2_ap, eps, extra_bias):
            a1 = rowa[0:1, 0:512]
            t0 = tsc[0:1, 0:512]
            t32 = tsc[0:1, 512:1024]
            t64 = t0
            t96 = t32
            b1 = rowb[0:1, 0:512]
            E_("s", lambda s, a=s1_ap, d=a1: s.activation(d, a, ACT.Copy, scale=1.0 / 1024.0),
               reads=[stat_buf], writes=[bROWA])
            E_("s", lambda s, a=s2_ap, d=t0: s.activation(d, a, ACT.Copy, scale=1.0 / 1024.0),
               reads=[stat_buf], writes=[bTS[0]])
            E_("v", lambda v, a=a1, d=t32: v.tensor_tensor(d, a, a, AluOpType.mult),
               reads=[bROWA], writes=[bTS[1]])
            E_("v", lambda v, a=t0, b=t32, d=t64: v.tensor_tensor(d, a, b, AluOpType.subtract),
               reads=[bTS[1], bTS[0]], writes=[bTS[0]])
            E_("s", lambda s, a=t64, d=t96, eps=eps: s.activation(d, a, ACT.Ln, bias=float(eps), scale=1.0),
               reads=[bTS[0], bCF], writes=[bTS[1]])
            E_("s", lambda s, a=t96, d=b1, xb=extra_bias: s.activation(d, a, ACT.Exp, bias=float(xb), scale=-0.5),
               reads=[bTS[1], bCF], writes=[bROWB])

        def bcast_f32(pidx, src_t, src_buf):
            dst = PS[pidx][:, 0:512]
            src = src_t[:, 0:512]
            E_("t", lambda t, d=dst, s=src: t.matmul(d, c_onesrow0, s, start=True, stop=True),
               reads=[src_buf, bCF], writes=[bPS[pidx]])

        def _stat_slots(sl, alt):
            if alt:
                return xst[:, 0:512], bXST[0], xst[:, 512:1024], bXST[2]
            xb = xst[:, sl * 512:(sl + 1) * 512]
            bxb = bXST[2 * sl]
            x2 = xst[:, 1024 + sl * 512: 1024 + (sl + 1) * 512]
            bx2 = bXST[2 * sl + 1]
            return xb, bxb, x2, bx2

        def stats_piece_ve(xsl, xbuf, sl, use_qsb=False):
            xb, bxb, x2, bx2 = _stat_slots(sl, use_qsb)
            E_("v", lambda v, a=xsl, d=xb: v.tensor_copy(d, a),
               reads=[xbuf], writes=[bxb])
            E_("s", lambda s, a=xsl, d=x2: s.activation(d, a, ACT.Square),
               reads=[xbuf], writes=[bx2])

        def stats_piece_mm(dc, sl, pidx, use_qsb=False):
            xb, bxb, x2, bx2 = _stat_slots(sl, use_qsb)
            E_("t", lambda t, dc=dc, d=PS[pidx][0:64, :], a=xb: t.matmul(
                d, c_ones64, a, start=(dc == 0), stop=(dc == NDC - 1), tile_position=(0, 0), skip_group_check=True),
                reads=[bxb, bCB], writes=[bPS[pidx]])
            E_("t", lambda t, dc=dc, d=PS[pidx][64:128, :], a=x2: t.matmul(
                d, c_ones64, a, start=(dc == 0), stop=(dc == NDC - 1), tile_position=(0, 64),
                skip_group_check=True),
                reads=[bx2, bCB], writes=[bPS[pidx]])

        def stats_chunk(xcols, pidx, xh_bufs):
            for dc, (xsl, xbuf) in enumerate(xcols):
                sl = dc % 2
                stats_piece_ve(xsl, xbuf, sl)
                stats_piece_mm(dc, sl, pidx)

        def ffn_stat_items(tf, bank, psum_r, use_qsb=False):
            xcols = [(xh[:, dc * 1024 + tf * 512: dc * 1024 + tf * 512 + 512], bXH[dc][tf])
                     for dc in range(NDC)]
            items = []
            for dc, (xsl, xbuf) in enumerate(xcols):
                sl = dc % 2
                items.append(lambda xsl=xsl, xbuf=xbuf, sl=sl: stats_piece_ve(xsl, xbuf, sl, use_qsb))
                items.append(lambda dc=dc, sl=sl: stats_piece_mm(dc, sl, bank, use_qsb))
            items.append(lambda: rowmath_r(bPS[bank], PS[bank][0:1, :], PS[bank][64:65, :],
                                           LN_EPS_PRE, -LN127))
            items.append(None)
            def _mu():
                bcast_f32(bank, rowa, bROWA)
                E_("v", lambda v: v.tensor_copy(mub[:], PS[bank][:]),
                   reads=[bPS[bank]], writes=[bMUB])
            items.append(_mu)
            items.append(None)
            items.append(lambda: bcast_f32(bank, rowb, bROWB))
            if not psum_r:
                items.append(lambda: E_("v", lambda v: v.tensor_copy(rub[:], PS[bank][:]),
                                        reads=[bPS[bank]], writes=[bRUB]))
            items.append(None)
            return items

        def ffn_zgen_items(tf, bank, z3Bl, z3_ap_l, psum_r):
            items = []
            for dc in range(NDC):
                xsl = xh[:, dc * 1024 + tf * 512: dc * 1024 + tf * 512 + 512]
                xbuf = bXH[dc][tf]
                tslot = dc % 2
                tt = tsc[:, tslot * 512:(tslot + 1) * 512]
                dap = z3_ap_l[dc]
                dbuf = z3Bl[dc]
                def _z(xsl=xsl, xbuf=xbuf, tt=tt, dap=dap, dbuf=dbuf, tslot=tslot):
                    E_("v", lambda v, a=xsl, d=tt: v.tensor_tensor(
                        d, a, mub[:], AluOpType.subtract),
                        reads=[xbuf, bMUB], writes=[bTS[tslot]])
                    if psum_r:
                        E_("v", lambda v, a=tt, d=dap: v.tensor_tensor(
                            d, a, PS[bank][:], AluOpType.mult),
                            reads=[bTS[tslot], bPS[bank]], writes=[dbuf])
                    else:
                        E_("v", lambda v, a=tt, d=dap: v.tensor_tensor(
                            d, a, rub[:], AluOpType.mult),
                            reads=[bTS[tslot], bRUB], writes=[dbuf])
                items.append(_z)
            return items

        def zgen(xcols, mu_pidx, r_pidx, dsts):
            for dc, (xsl, xbuf) in enumerate(xcols):
                tslot = dc % 2
                tt = tsc[:, tslot * 512:(tslot + 1) * 512]
                dap, dbufs = dsts[dc]
                E_("v", lambda v, a=xsl, d=tt, m=PS[mu_pidx][:]: v.tensor_tensor(
                    d, a, m, AluOpType.subtract),
                    reads=[xbuf, bPS[mu_pidx]], writes=[bTS[tslot]])
                E_("v", lambda v, a=tt, d=dap, r=PS[r_pidx][:]: v.tensor_tensor(
                    d, a, r, AluOpType.mult),
                    reads=[bTS[tslot], bPS[r_pidx]], writes=dbufs)

        def load_weights_bf16(dsrc, dstT, dstB):
            for pc in range(16):
                slot = pc % 2
                st = tsc[:, slot * 512:(slot + 1) * 512]
                src = dsrc.ap()[:, pc * 512:(pc + 1) * 512]
                dst = dstT[:, pc * 512:(pc + 1) * 512]
                E_("y", lambda y, d=st, s=src: y.dma_start(d, s),
                   writes=[bTS[slot]], queue="qt%d" % slot)
                E_("v", lambda v, d=dst, s=st: v.tensor_copy(d, s),
                   reads=[bTS[slot]], writes=[dstB[pc // 2]])

        def load_weights_bf16_pair(jobs):
            for pc in range(16):
                for j, (dsrc, dstT, dstB) in enumerate(jobs):
                    slot = j % 2
                    st = tsc[:, slot * 512:(slot + 1) * 512]
                    src = dsrc.ap()[:, pc * 512:(pc + 1) * 512]
                    dst = dstT[:, pc * 512:(pc + 1) * 512]
                    E_("y", lambda y, d=st, s=src: y.dma_start(d, s),
                       writes=[bTS[slot]], queue="qt%d" % slot)
                    E_("v", lambda v, d=dst, s=st: v.tensor_copy(d, s),
                       reads=[bTS[slot]], writes=[dstB[pc // 2]])

        lim = _LIMS[_LIM]
        for s_i in range(2):
            if s_i == 1 and lim < 99:
                break
            xs = dx.ap()[s_i]
            os_ = dout.ap()[s_i]

            z1B = new_era([f"z1_{d}" for d in range(NDC)])

            def emit_sload(hc0):
                for dc in range(NDC):
                    for pc_ in range(2):
                        dst = xh[:, dc * 1024 + pc_ * 512: dc * 1024 + pc_ * 512 + 512]
                        src = xs[:, dc * 2048 + hc0 + pc_ * 512: dc * 2048 + hc0 + pc_ * 512 + 512]
                        E_("y", lambda y, d=dst, s=src: y.dma_start(d, s),
                           writes=[bXH[dc][pc_]], queue="qxh%d" % dc)

            emit_sload(0)
            if s_i == 0:
                load_weights_bf16_pair([(dwk, wqb, bWQ), (dwv, wob, bWO)])
            for half in range(2):
                hc0 = half * 1024
                if half == 1:
                    emit_sload(hc0)
                for pc in range(2):
                    xcols = [(xh[:, dc * 1024 + pc * 512: dc * 1024 + pc * 512 + 512], bXH[dc][pc])
                             for dc in range(NDC)]
                    stats_chunk(xcols, 2, bXH)
                    rowmath_r(bPS[2], PS[2][0:1, :], PS[2][64:65, :], LN_EPS_PRE, 0.0)
                    bcast_f32(5, rowa, bROWA)
                    bcast_f32(4, rowb, bROWB)
                    dsts = []
                    for dc in range(NDC):
                        col = dc * 2048 + hc0 + pc * 512
                        dsts.append((zr[:, col:col + 512], [z1B[dc]]))
                    zgen(xcols, 5, 4, dsts)

            if lim < 0.2:
                break
            E_("v", lambda v: v.memset(k2m[:, 0:1], -1.0), writes=[bK2])
            if lim < 0.4:
                break
            for pr in range(NP):
                for piece in range(4):
                    pq = PS[piece % 2]
                    bpq = bPS[piece % 2]
                    for dc in range(NDC):
                        zsl = zr[:, dc * 2048 + piece * 512: dc * 2048 + piece * 512 + 512]
                        wE = wqb[:, dc * 1024 + (2 * pr) * 64: dc * 1024 + (2 * pr) * 64 + 64]
                        wO_ = wqb[:, dc * 1024 + (2 * pr + 1) * 64: dc * 1024 + (2 * pr + 1) * 64 + 64]
                        E_("t", lambda t, dc=dc, d=pq[0:64, :], w=wE, z=zsl: t.matmul(
                            d, w, z, start=(dc == 0), stop=(dc == NDC - 1), tile_position=(0, 0), skip_group_check=True),
                            reads=[bWQ[dc], z1B[dc]], writes=[bpq])
                        E_("t", lambda t, dc=dc, d=pq[64:128, :], w=wO_, z=zsl: t.matmul(
                            d, w, z, start=(dc == 0), stop=(dc == NDC - 1), tile_position=(0, 64), skip_group_check=True),
                            reads=[bWQ[dc], z1B[dc]], writes=[bpq])
                    kdst = kb[:, pr * 2048 + piece * 512: pr * 2048 + piece * 512 + 512]
                    E_("v", lambda v, d=kdst, a=pq[:]: v.tensor_copy(d, a),
                       reads=[bpq], writes=[bKB])
                    if lim < 0.45:
                        continue
                    ksq = xst[:, 0:512]
                    E_("s", lambda s, a=pq[:], d=ksq: s.activation(d, a, ACT.Square),
                       reads=[bpq], writes=[bXST[0]])
                    if lim < 0.5:
                        continue
                    E_("t", lambda t, a=ksq: t.matmul(
                        PS[6][0:64, :], c_ones64, a, start=True, stop=True,
                        tile_position=(0, 0)),
                        reads=[bXST[0], bCB], writes=[bPS[6]])
                    if lim < 0.55:
                        continue
                    E_("v", lambda v: v.tensor_reduce(
                        k2m[0:1, 1:2], PS[6][0:1, :], AX.X, AluOpType.max),
                        reads=[bPS[6]], writes=[bK2])
                    if lim < 0.6:
                        continue
                    E_("v", lambda v: v.tensor_tensor(
                        k2m[0:1, 0:1], k2m[0:1, 0:1], k2m[0:1, 1:2], AluOpType.max),
                        reads=[bK2], writes=[bK2])
            if lim < 1:
                break
            E_("v", lambda v: v.tensor_scalar(
                mk2[0:1, 1:2], k2m[0:1, 0:1], -0.0625, 0.0, AluOpType.mult, AluOpType.add),
                reads=[bK2, bCF], writes=[bMK2])

            for tk in range(16):
                for hh in range(2):
                    pv = PS[4 + hh]
                    bpv = bPS[4 + hh]
                    for dc in range(NDC):
                        zsl = zr[:, dc * 2048 + tk * 128: dc * 2048 + tk * 128 + 128]
                        wsl = wob[:, dc * 1024 + hh * 512: dc * 1024 + hh * 512 + 512]
                        E_("t", lambda t, dc=dc, d=pv[:], z=zsl, w=wsl: t.matmul(
                            d, z, w, start=(dc == 0), stop=(dc == NDC - 1)),
                            reads=[bWO[dc], z1B[dc]], writes=[bpv])
                    vdst = vb[:, tk * 1024 + hh * 512: tk * 1024 + hh * 512 + 512]
                    E_("v", lambda v, d=vdst, a=pv[:]: v.tensor_copy(d, a),
                       reads=[bpv], writes=[bVB])
                slot = tk % 2
                st = tsc[:, slot * 512:(slot + 1) * 512]
                wsrc = dwq.ap()[:, tk * 512:(tk + 1) * 512]
                wdst = wqb[:, tk * 512:(tk + 1) * 512]
                E_("y", lambda y, d=st, s=wsrc: y.dma_start(d, s),
                   writes=[bTS[slot]], queue="qt%d" % slot)
                E_("v", lambda v, d=wdst, s=st: v.tensor_copy(d, s),
                   reads=[bTS[slot]], writes=[bWQ[tk // 2]])

            load_weights_bf16(dwo, wob, bWO)

            if lim < 2:
                break
            def emit_xload(half, pc_):
                for dc in range(NDC):
                    dst = xh[:, dc * 1024 + pc_ * 512: dc * 1024 + pc_ * 512 + 512]
                    src = xs[:, dc * 2048 + half * 1024 + pc_ * 512:
                              dc * 2048 + half * 1024 + pc_ * 512 + 512]
                    E_("y", lambda y, d=dst, s=src: y.dma_start(d, s),
                       writes=[bXH[dc][pc_]], queue="qxh%d" % dc)

            for half in range(2):
                if half == 0:
                    for pc_ in range(2):
                        emit_xload(half, pc_)

                names = ([f"zcA{d}" for d in range(NDC)] + [f"zcB{d}" for d in range(NDC)]
                         + [f"an{p}" for p in range(NP)]
                         + ["pb0", "pb1", "qy2"])
                eb = new_era(names)
                zcB = [eb[0:8], eb[8:16]]
                anB = eb[16:24]
                pbB = eb[24:26]; q2B = eb[26]; y2B = eb[26]
                ZC0, AN0, PB0, Q20 = 0, 8192, 12288, 14336
                Y20 = Q20
                zc_ap = [[zr[:, ZC0 + sl * 4096 + d * 512: ZC0 + sl * 4096 + d * 512 + 512]
                          for d in range(NDC)] for sl in range(2)]
                an_ap = [zr[:, AN0 + p * 512: AN0 + p * 512 + 512] for p in range(NP)]
                qs_ap = [qsb_t[:, i * 512:(i + 1) * 512] for i in range(4)]
                q2_ap = zr[:, Q20:Q20 + 512]
                y2_ap = zr[:, Y20:Y20 + 512]

                def att_prep_items(ci):
                    xcol = ci * 512
                    slot = ci % 2
                    xcols = [(xh[:, dc * 1024 + xcol: dc * 1024 + xcol + 512], bXH[dc][ci])
                             for dc in range(NDC)]
                    items = []
                    for dc, (xsl, xbuf) in enumerate(xcols):
                        sl = dc % 2
                        items.append(lambda xsl=xsl, xbuf=xbuf, sl=sl: stats_piece_ve(xsl, xbuf, sl))
                        items.append(lambda dc=dc, sl=sl: stats_piece_mm(dc, sl, 6))
                    items.append(lambda: rowmath_r(bPS[6], PS[6][0:1, :], PS[6][64:65, :],
                                                   LN_EPS_PRE, 0.0))
                    items.append(None)
                    def _mu1():
                        bcast_f32(6, rowa, bROWA)
                        E_("v", lambda v: v.tensor_copy(mub[:], PS[6][:]),
                           reads=[bPS[6]], writes=[bMUB])
                    items.append(_mu1)
                    items.append(None)
                    items.append(lambda: bcast_f32(6, rowb, bROWB))
                    items.append(None)
                    for dc in range(NDC):
                        def _z(dc=dc):
                            xsl = xh[:, dc * 1024 + xcol: dc * 1024 + xcol + 512]
                            tslot = dc % 2
                            tt = tsc[:, tslot * 512:(tslot + 1) * 512]
                            zdst = zc_ap[slot][dc]
                            E_("v", lambda v, a=xsl, d=tt: v.tensor_tensor(
                                d, a, mub[:], AluOpType.subtract),
                                reads=[bXH[dc][xcol // 512], bMUB], writes=[bTS[tslot]])
                            E_("v", lambda v, d=zdst, a=tt: v.tensor_tensor(
                                d, a, PS[6][:], AluOpType.mult),
                                reads=[bTS[tslot], bPS[6]], writes=[zcB[slot][dc]])
                        items.append(_z)
                    return items

                def att_prep(ci):
                    for it in att_prep_items(ci):
                        if it is not None:
                            it()
                    emit_qB(0, ci % 2)

                def emit_qB_stage(pr, slot, stage):
                    qe = 2 * (pr % 2)
                    bslot = pr % 2
                    if stage == 0:
                        for dc in range(NDC):
                            wE = wqb[:, dc * 1024 + (2 * pr) * 64: dc * 1024 + (2 * pr) * 64 + 64]
                            wO_ = wqb[:, dc * 1024 + (2 * pr + 1) * 64: dc * 1024 + (2 * pr + 1) * 64 + 64]
                            E_("t", lambda t, dc=dc, w=wE, z=zc_ap[slot][dc]: t.matmul(
                                PS[7][0:64, :], w, z, start=(dc == 0), stop=(dc == NDC - 1),
                                tile_position=(0, 0), skip_group_check=True),
                                reads=[bWQ[dc], zcB[slot][dc]], writes=[bPS[7]])
                            E_("t", lambda t, dc=dc, w=wO_, z=zc_ap[slot][dc]: t.matmul(
                                PS[7][64:128, :], w, z, start=(dc == 0), stop=(dc == NDC - 1),
                                tile_position=(0, 64), skip_group_check=True),
                                reads=[bWQ[dc], zcB[slot][dc]], writes=[bPS[7]])
                        E_("v", lambda v, d=qs_ap[qe]: v.tensor_copy(d[0:64, :], PS[7][0:64, :]),
                           reads=[bPS[7]], writes=[bQS[qe]])
                        E_("v", lambda v, d=qs_ap[qe + 1]: v.tensor_copy(d[64:128, :], PS[7][64:128, :]),
                           reads=[bPS[7]], writes=[bQS[qe + 1]])
                        E_("s", lambda s: s.activation(q2_ap, PS[7][:], ACT.Square),
                           reads=[bPS[7]], writes=[q2B])
                    elif stage == 1:
                        E_("t", lambda t: t.matmul(PS[7][0:64, :], c_ones64, q2_ap,
                                                   start=True, stop=True, tile_position=(0, 0)),
                           reads=[bCB, q2B], writes=[bPS[7]])
                    elif stage == 2:
                        E_("v", lambda v: v.tensor_reduce(brt[0:1, 0:1], PS[7][0:1, :],
                                                          AX.X, AluOpType.max),
                           reads=[bPS[7]], writes=[bROWT])
                        bcol = browb[0:1, bslot:bslot + 1]
                        E_("v", lambda v, d=bcol: v.scalar_tensor_tensor(
                            d, brt[0:1, 0:1], -0.0625, mk2[0:1, 1:2],
                            AluOpType.mult, AluOpType.add),
                            reads=[bROWT, bMK2], writes=[bBROWB])
                    else:
                        bsl = browb[:, bslot:bslot + 1]
                        E_("t", lambda t, s=bsl: t.matmul(
                            PS[7][0:64, 0:1], c_allones, s, start=True, stop=True,
                            tile_position=(0, 0)),
                            reads=[bBROWB, bCB], writes=[bPS[7]])
                        E_("t", lambda t, s=bsl: t.matmul(
                            PS[7][64:128, 0:1], c_allones, s, start=True, stop=True,
                            tile_position=(0, 64)),
                            reads=[bBROWB, bCB], writes=[bPS[7]])
                        bb = bbias[:, bslot:bslot + 1]
                        E_("v", lambda v, d=bb: v.tensor_copy(d, PS[7][:, 0:1]),
                           reads=[bPS[7]], writes=[bBB[bslot]])

                def emit_qB(pr, slot):
                    for st in range(4):
                        emit_qB_stage(pr, slot, st)

                def att_heads(ci, inject=(), next_qB=None, inject_rate=1):
                    inj = list(inject)
                    ii = [0]

                    def pop_inject():
                        if ii[0] < len(inj):
                            it = inj[ii[0]]
                            ii[0] += 1
                            if it is not None:
                                it()
                    C = half * 2 + ci
                    slot = ci % 2
                    ntk_full = 4 * C
                    ntiles = ntk_full + 4
                    first_sum = {0: True, 64: True}
                    for pr in range(NP):
                        bslot = pr % 2
                        bb = bbias[:, bslot:bslot + 1]
                        abank = 4
                        first_av = {0: True, 64: True}

                        def emit_avsums(tau, parity, n, qoff, tslot2):
                            hd = 2 * pr + parity
                            pap = zr[:, PB0 + tslot2 * 1024 + parity * 512:
                                     PB0 + tslot2 * 1024 + parity * 512 + n]
                            if parity == 0:
                                av_tp, sum_tp = (0, 0), (0, 64)
                                ar0, ar1, sr0, sr1 = 0, 64, 64, 128
                            else:
                                av_tp, sum_tp = (0, 64), (0, 0)
                                ar0, ar1, sr0, sr1 = 64, 128, 0, 64
                            vsl = vb[:, tau * 1024 + hd * 64: tau * 1024 + hd * 64 + 64]
                            E_("t", lambda t, d=PS[abank][ar0:ar1, qoff:qoff + n],
                                   w=vsl, p=pap, st=first_av[ar0], tp=av_tp: t.matmul(
                                d, w, p, start=st, stop=False, tile_position=tp,
                                skip_group_check=True),
                                reads=[bVB, pbB[tslot2]], writes=[bPS[abank]])
                            first_av[ar0] = False
                            E_("t", lambda t, d=PS[5][sr0:sr1, qoff:qoff + n],
                                   w=c_onescol[pr], p=pap, st=first_sum[sr0], tp=sum_tp: t.matmul(
                                d, w, p, start=st, stop=False, tile_position=tp,
                                skip_group_check=True),
                                reads=[bCB, pbB[tslot2]], writes=[bPS[5]])
                            first_sum[sr0] = False

                        pend = []
                        for tau in range(ntiles):
                            if tau < ntk_full:
                                n, qoff = 512, 0
                            else:
                                j = tau - ntk_full
                                n, qoff = 512 - 128 * j, 128 * j
                            tslot2 = tau % 2
                            cur = []
                            for parity in range(2):
                                kT = kb[:, pr * 2048 + tau * 128: pr * 2048 + tau * 128 + 64]
                                kBm = kb[:, pr * 2048 + tau * 128 + 64: pr * 2048 + tau * 128 + 128]
                                qe = 2 * (pr % 2) + parity
                                qsl = qs_ap[qe][:, qoff:qoff + n]
                                sb_ = 2 * tslot2 + parity
                                E_("t", lambda t, d=PS[sb_][0:64, 0:n], w=kT, q=qsl: t.matmul(
                                    d, w, q, start=True, stop=True, tile_position=(0, 0)),
                                    reads=[bKB, bQS[qe]], writes=[bPS[sb_]])
                                E_("t", lambda t, d=PS[sb_][64:128, 0:n], w=kBm, q=qsl: t.matmul(
                                    d, w, q, start=True, stop=True, tile_position=(0, 64)),
                                    reads=[bKB, bQS[qe]], writes=[bPS[sb_]])
                                cur.append((tau, parity, n, qoff, tslot2))
                            if ntk_full == 0 and n < 512:
                                for parity in range(2):
                                    sb_ = 2 * tslot2 + parity
                                    pp = zr[:, PB0 + tslot2 * 1024 + parity * 512:
                                            PB0 + tslot2 * 1024 + parity * 512 + n]
                                    E_("s", lambda s, d=pp, a=PS[sb_][:, 0:n], b=bb:
                                        s.activation(d, a, ACT.Exp, bias=b, scale=0.125),
                                        reads=[bPS[sb_], bBB[bslot]],
                                        writes=[pbB[tslot2]])
                            else:
                                pap = zr[:, PB0 + tslot2 * 1024: PB0 + tslot2 * 1024 + 512 + n]
                                E_("s", lambda s, d=pap, a=PSB[tslot2][:, 0:512 + n], b=bb: s.activation(
                                    d, a, ACT.Exp, bias=b, scale=0.125),
                                    reads=[bPS[2 * tslot2], bPS[2 * tslot2 + 1], bBB[bslot]],
                                    writes=[pbB[tslot2]])
                            if tau >= ntk_full:
                                for parity in range(2):
                                    pm = zr[:, PB0 + tslot2 * 1024 + parity * 512:
                                            PB0 + tslot2 * 1024 + parity * 512 + 128]
                                    E_("v", lambda v, d=pm: v.tensor_tensor(
                                        d, d, c_mask, AluOpType.mult),
                                        reads=[bCB], writes=[pbB[tslot2]])
                            for job in pend:
                                emit_avsums(*job)
                            pend = cur
                            if pr + 1 < NP:
                                if _NOQB:
                                    if tau == 0:
                                        emit_qB(pr + 1, slot)
                                elif tau < 4:
                                    emit_qB_stage(pr + 1, slot, tau)
                            elif next_qB is not None and tau < 4:
                                emit_qB_stage(0, next_qB, tau)
                            for _r in range(inject_rate):
                                pop_inject()
                        for job in pend:
                            emit_avsums(*job)
                        E_("v", lambda v, d=an_ap[pr], a=PS[abank][:]: v.tensor_copy(d, a),
                           reads=[bPS[abank]], writes=[anB[pr]])
                    while ii[0] < len(inj):
                        pop_inject()

                def epi_recips(ci):
                    def _rec(lo, hi):
                        def g(v):
                            with nc.allow_low_precision(reason="bf16 1/S scale for softmax (0.4%)"):
                                return v.reciprocal(invs[lo:hi, :], PS[5][lo:hi, :])
                        return g
                    E_("v", _rec(0, 8), reads=[bPS[5]], writes=[bINV])
                    E_("v", _rec(64, 72), reads=[bPS[5]], writes=[bINV])

                def epi_items(ci, ybks, ibks, lbank, defer):
                    xcol = ci * 512
                    slot = ci % 2
                    if defer:
                        y2ap, y2buf = xst[:, 512:1024], bXST[2]
                    else:
                        y2ap, y2buf = y2_ap, y2B
                    items = []
                    for pr in range(NP):
                        def _ia(pr=pr, ibank=ibks[pr % 2]):
                            E_("t", lambda t, d=PS[ibank][0:64, :]: t.matmul(
                                d, c_eselT[pr], invs[:], start=True, stop=True, tile_position=(0, 0)),
                                reads=[bINV, bCB], writes=[bPS[ibank]])
                            E_("t", lambda t, d=PS[ibank][64:128, :]: t.matmul(
                                d, c_eselB[pr], invs[:], start=True, stop=True, tile_position=(0, 64)),
                                reads=[bINV, bCB], writes=[bPS[ibank]])
                            E_("v", lambda v, d=an_ap[pr], i=PS[ibank][:]: v.tensor_tensor(
                                d, d, i, AluOpType.mult),
                                reads=[bPS[ibank]], writes=[anB[pr]])
                        items.append(_ia)
                    for d_ in range(NDC):
                        def _wo(d_=d_, ybk=ybks[d_ % 2]):
                            for pr in range(NP):
                                wT = wob[:, pr * 1024 + d_ * 128: pr * 1024 + d_ * 128 + 64]
                                wB = wob[:, pr * 1024 + d_ * 128 + 64: pr * 1024 + d_ * 128 + 128]
                                E_("t", lambda t, pr=pr, d=PS[ybk][0:64, :], w=wT, a=an_ap[pr]: t.matmul(
                                    d, w, a, start=(pr == 0), stop=(pr == NP - 1), tile_position=(0, 0), skip_group_check=True),
                                    reads=[bWO[pr], anB[pr]], writes=[bPS[ybk]])
                                E_("t", lambda t, pr=pr, d=PS[ybk][64:128, :], w=wB, a=an_ap[pr]: t.matmul(
                                    d, w, a, start=(pr == 0), stop=(pr == NP - 1), tile_position=(0, 64), skip_group_check=True),
                                    reads=[bWO[pr], anB[pr]], writes=[bPS[ybk]])
                            yb = zc_ap[slot][d_]
                            E_("v", lambda v, d=yb, y=PS[ybk][:]: v.tensor_tensor(
                                d, d, y, AluOpType.add),
                                reads=[bPS[ybk]], writes=[zcB[slot][d_]])
                            E_("s", lambda s, a=yb: s.activation(y2ap, a, ACT.Square),
                               reads=[zcB[slot][d_]], writes=[y2buf])
                            E_("t", lambda t, d_=d_, a=yb: t.matmul(
                                PS[lbank][0:64, :], c_ones64, a,
                                start=(d_ == 0), stop=(d_ == NDC - 1), tile_position=(0, 0), skip_group_check=True),
                                reads=[bCB, zcB[slot][d_]], writes=[bPS[lbank]])
                            E_("t", lambda t, d_=d_: t.matmul(
                                PS[lbank][64:128, :], c_ones64, y2ap,
                                start=(d_ == 0), stop=(d_ == NDC - 1), tile_position=(0, 64), skip_group_check=True),
                                reads=[bCB, y2buf], writes=[bPS[lbank]])
                        items.append(_wo)
                    items.append(lambda: rowmath_r(bPS[lbank], PS[lbank][0:1, :],
                                                   PS[lbank][64:65, :], LN_EPS_3, 0.0))
                    items.append(None)
                    def _mu3():
                        bcast_f32(lbank, rowa, bROWA)
                        E_("v", lambda v: v.tensor_copy(mub[:], PS[lbank][:]),
                           reads=[bPS[lbank]], writes=[bMUB])
                    items.append(_mu3)
                    items.append(None)
                    def _r3():
                        bcast_f32(lbank, rowb, bROWB)
                        E_("v", lambda v: v.tensor_copy(rub[:], PS[lbank][:]),
                           reads=[bPS[lbank]], writes=[bRUB])
                    items.append(_r3)
                    items.append(None)
                    for dp in range(4):
                        def _pb(dp=dp):
                            for d_ in (2 * dp, 2 * dp + 1):
                                tslot = d_ % 2
                                tt = tsc[:, tslot * 512:(tslot + 1) * 512]
                                yb = zc_ap[slot][d_]
                                E_("v", lambda v, a=yb, d=tt: v.tensor_tensor(
                                    d, a, mub[:], AluOpType.subtract),
                                    reads=[bMUB, zcB[slot][d_]], writes=[bTS[tslot]])
                                E_("v", lambda v, d=tt: v.tensor_tensor(
                                    d, d, rub[:], AluOpType.mult),
                                    reads=[bRUB], writes=[bTS[tslot]])
                                gcol = sgb[:, d_:d_ + 1]
                                bcol = sgb[:, 8 + d_:8 + d_ + 1]
                                E_("v", lambda v, d=tt, g=gcol, b=bcol: v.tensor_scalar(
                                    d, d, g, b, AluOpType.mult, AluOpType.add),
                                    reads=[bGB], writes=[bTS[tslot]])
                                xsl = xh[:, d_ * 1024 + xcol: d_ * 1024 + xcol + 512]
                                E_("v", lambda v, x=xsl, d=tt: v.tensor_tensor(
                                    x, x, d, AluOpType.add),
                                    reads=[bTS[tslot], bXH[d_][xcol // 512]], writes=[bXH[d_][xcol // 512]])
                        items.append(_pb)
                    return items

                def att_epilogue(ci):
                    epi_recips(ci)
                    for it in epi_items(ci, (4, 6), (5, 7), 5, False):
                        if it is not None:
                            it()

                def emit_guw_dma(hc):
                    for j, dw_ in ((0, dwg), (1, dwu)):
                        sl = (2 * hc + j) % 3
                        stg = wstag[:, sl * 1024:(sl + 1) * 1024]
                        E_("g", lambda g, d=stg, s=dw_.ap()[hc]: g.dma_start(d, s),
                           writes=[bWST[sl]], queue="qw%d" % sl)

                def emit_guw_cast(hc):
                    gsl = (2 * hc) % 3
                    usl = (2 * hc + 1) % 3
                    wsl = hc % 2
                    gb_ = wgbb[:, wsl * 1024:(wsl + 1) * 1024]
                    ub_ = wubb[:, wsl * 1024:(wsl + 1) * 1024]
                    E_("v", lambda v, d=gb_, s=wstag[:, gsl * 1024:(gsl + 1) * 1024]: v.tensor_copy(d, s),
                       reads=[bWST[gsl]], writes=[bWG[wsl]])
                    E_("s", lambda s, d=ub_, a=wstag[:, usl * 1024:(usl + 1) * 1024]: s.activation(d, a, ACT.Copy),
                       reads=[bWST[usl]], writes=[bWU[wsl]])

                def emit_wd_dma(hc):
                    sl = hc % 3
                    stg = wstag[:, sl * 1024:(sl + 1) * 1024]
                    E_("g", lambda g, d=stg, s=dwd.ap()[hc]: g.dma_start(d, s),
                       writes=[bWST[sl]], queue="qw%d" % sl)

                z3_ap = [zr[:, d * 512:(d + 1) * 512] for d in range(NDC)]

                def h_region(k):
                    if k < 8:
                        return 12288 + k * 512
                    if k < 16:
                        return 8192 + (k - 8) * 512
                    return 4096 + (k - 16) * 512
                h_ap = [zr[:, h_region(k): h_region(k) + 512] for k in range(NHC)]

                att_prep(0)
                if _NOPREP:
                    att_heads(0)
                    att_prep(1)
                else:
                    att_heads(0, inject=att_prep_items(1), next_qB=1, inject_rate=2)
                att_epilogue(0)
                z3B0 = rebind([f"z3a_{d}" for d in range(NDC)], zcB[0])
                emit_guw_dma(0)
                if _INJL < 1:
                    it0 = []
                else:
                    it0 = ([None] * 16 + ffn_stat_items(0, 6, True)
                           + ffn_zgen_items(0, 6, z3B0, z3_ap, True)
                           + [lambda: emit_guw_cast(0)])
                att_heads(1, inject=it0)
                _DEFER = (not _NODEF) and lim >= 3 and _INJL >= 2
                if _DEFER:
                    epi_recips(1)
                    epi1_gu = epi_items(1, (0, 2), (1, 1), 3, True)
                else:
                    att_epilogue(1)
                    epi1_gu = []

                if lim < 3:
                    break
                if _DEFER:
                    hB = (rebind([f"h{k}" for k in range(8)], [pbB[0], pbB[1], q2B])
                          + [None] * 16)
                else:
                    hB = rebind([f"h{k}" for k in range(NHC)], eb[8:27])
                z3B1 = None
                wkv_items = []
                if half == 1 and s_i == 0 and lim >= 99:
                    for _pc in range(16):
                        for _j, (_dsrc, _dstT, _dstB) in enumerate(
                                ((dwk, wqb, bWQ), (dwv, wob, bWO))):
                            def _ld(pc=_pc, dsrc=_dsrc, dstT=_dstT, dstB=_dstB, slot=_j % 2):
                                st = tsc[:, slot * 512:(slot + 1) * 512]
                                sap = dsrc.ap()[:, pc * 512:(pc + 1) * 512]
                                dst = dstT[:, pc * 512:(pc + 1) * 512]
                                E_("y", lambda y, d=st, s=sap: y.dma_start(d, s),
                                   writes=[bTS[slot]], queue="qt%d" % slot)
                                E_("v", lambda v, d=dst, s=st: v.tensor_copy(d, s),
                                   reads=[bTS[slot]], writes=[dstB[pc // 2]])
                            wkv_items.append(_ld)

                for tf in range(2):
                    xcol = tf * 512
                    if tf == 0:
                        z3Bl = z3B0
                        if _INJL < 1:
                            for it in (ffn_stat_items(0, 6, True)
                                       + ffn_zgen_items(0, 6, z3B0, z3_ap, True)):
                                if it is not None:
                                    it()
                            emit_guw_cast(0)
                        if _INJL >= 2:
                            _gfull = ffn_stat_items(1, 2, False, use_qsb=True)
                            _gn = int(_os.environ.get("KMK_INJN", "99"))
                            gu_inject = epi1_gu + _gfull[:_gn]
                            gu_rest = _gfull[_gn:]
                        else:
                            gu_inject = []
                            gu_rest = []
                    else:
                        z3Bl = z3B1
                        gu_inject = []
                        emit_guw_dma(0)
                        emit_guw_cast(0)
                    gii = [0]

                    def pop_gu():
                        if gii[0] < len(gu_inject):
                            it = gu_inject[gii[0]]
                            gii[0] += 1
                            if it is not None:
                                it()
                    for hc in range(NHC):
                        if tf == 0 and _DEFER:
                            if hc == 8:
                                hB[8:16] = rebind([f"h{k}" for k in range(8, 16)], anB)
                            elif hc == 16:
                                hB[16:24] = rebind([f"h{k}" for k in range(16, 24)], zcB[1])
                        wsl = hc % 2
                        if hc + 1 < NHC:
                            emit_guw_dma(hc + 1)
                        gB = 4 + 2 * ((hc + 1) % 2)
                        uB = gB + 1
                        for dc in range(NDC):
                            wgsl = wgbb[:, wsl * 1024 + dc * 128: wsl * 1024 + (dc + 1) * 128]
                            wusl = wubb[:, wsl * 1024 + dc * 128: wsl * 1024 + (dc + 1) * 128]
                            E_("t", lambda t, dc=dc, w=wgsl, z=z3_ap[dc], d=PS[gB][:]: t.matmul(
                                d, w, z, start=(dc == 0), stop=(dc == NDC - 1)),
                                reads=[bWG[wsl], z3Bl[dc]], writes=[bPS[gB]])
                            E_("t", lambda t, dc=dc, w=wusl, z=z3_ap[dc], d=PS[uB][:]: t.matmul(
                                d, w, z, start=(dc == 0), stop=(dc == NDC - 1)),
                                reads=[bWU[wsl], z3Bl[dc]], writes=[bPS[uB]])
                        if hc + 1 < NHC:
                            emit_guw_cast(hc + 1)
                        pop_gu()
                        sg_ = xst[:, 1024 + (hc % 2) * 512: 1024 + (hc % 2 + 1) * 512]
                        E_("s", lambda s, d=sg_, a=PS[gB][:]: s.activation(d, a, ACT.Silu),
                           reads=[bPS[gB]], writes=[bXST[2 + 2 * (hc % 2) - 1 if False else 2 * (hc % 2) + 1]])
                        E_("v", lambda v, d=h_ap[hc], a=sg_, u=PS[uB][:]: v.tensor_tensor(
                            d, a, u, AluOpType.mult),
                            reads=[bXST[2 * (hc % 2) + 1], bPS[uB]], writes=[hB[hc]])
                        pop_gu()
                    while gii[0] < len(gu_inject):
                        pop_gu()
                    if tf == 0:
                        for it in gu_rest:
                            if it is not None:
                                it()
                        z3B1 = rebind([f"z3b_{d}" for d in range(NDC)], z3B0)
                        dn_inject = ffn_zgen_items(1, 0, z3B1, z3_ap, False) + wkv_items[:24]
                        if _INJL < 2:
                            for it in ffn_stat_items(1, 2, False):
                                if it is not None:
                                    it()
                        if _INJL < 3:
                            for it in dn_inject:
                                it()
                            dn_inject = []
                    else:
                        dn_inject = list(wkv_items[24:])
                    dii = [0]

                    def pop_dn():
                        if dii[0] < len(dn_inject):
                            it = dn_inject[dii[0]]
                            dii[0] += 1
                            it()
                    for hc in range(3):
                        emit_wd_dma(hc)
                    for hc in range(NHC):
                        sl = hc % 3
                        wsl = hc % 2
                        stg = wstag[:, sl * 1024:(sl + 1) * 1024]
                        db_ = wdbb[:, wsl * 1024:(wsl + 1) * 1024]
                        E_("s", lambda s, d=db_, a=stg: s.activation(
                            d, a, ACT.Copy, scale=1.0 / 127.0),
                            reads=[bWST[sl], bCF], writes=[bWD[wsl]])
                        if hc + 3 < NHC:
                            emit_wd_dma(hc + 3)
                        pop_dn()
                        pop_dn()
                        for d_ in range(NDC):
                            wdsl = wdbb[:, wsl * 1024 + d_ * 128: wsl * 1024 + (d_ + 1) * 128]
                            E_("t", lambda t, w=wdsl, h=h_ap[hc], d=PS[d_][:], hc=hc: t.matmul(
                                d, w, h, start=(hc == 0), stop=(hc == NHC - 1)),
                                reads=[bWD[wsl], hB[hc]], writes=[bPS[d_]])
                    if tf == 0:
                        emit_guw_dma(0)
                    for d_ in range(NDC):
                        xsl = xh[:, d_ * 1024 + xcol: d_ * 1024 + xcol + 512]
                        E_("v", lambda v, x=xsl, f=PS[d_][:]: v.tensor_tensor(
                            x, x, f, AluOpType.add),
                            reads=[bPS[d_], bXH[d_][tf]], writes=[bXH[d_][tf]])
                        odst = os_[:, d_ * 2048 + half * 1024 + tf * 512:
                                   d_ * 2048 + half * 1024 + tf * 512 + 512]
                        E_("g", lambda ge, d=odst, s=xsl: ge.dma_start(d, s),
                           reads=[bXH[d_][tf]], queue="qo" if tf == 0 else "qo1")
                        nstores[tf] += 1
                    if half == 0 and lim >= 4:
                        emit_xload(1, tf)
                if half == 0 and lim == 3:
                    break

        P.ops["g"].append(([Tok("qo", nstores[0] * 16, True),
                           Tok("qo1", nstores[1] * 16, True)],
                          None, Tok("g", -1, False), None))

        semval = P.finalize()

        def replay(eng_obj, ename):
            for waits, fn, tok, queue in P.ops[ename]:
                for t in waits:
                    wv = t.idx if t.queue else semval[(t.src, t.idx)]
                    assert wv is not None, (ename, t.src, t.idx)
                    eng_obj.wait_ge(SEMS[t.src], wv)
                if fn is None:
                    continue
                inst = fn(eng_obj)
                if queue is not None:
                    inst.then_inc(SEMS[tok.src], 16)
                elif (tok.src, tok.idx) in P.referenced:
                    inst.then_inc(SEMS[tok.src], 1)

        @block.tensor
        def _(tensor):
            replay(tensor, "t")

        @block.vector
        def _(vector):
            replay(vector, "v")

        @block.scalar
        def _(scalar):
            replay(scalar, "s")

        @block.gpsimd
        def _(gpsimd):
            replay(gpsimd, "g")

        @block.sync
        def _(sync):
            replay(sync, "y")

    return nc, {e: len(P.ops[e]) for e in P.ENGS}


_NC_CACHE = [None]


def _host_inputs(inputs):
    x = np.asarray(inputs["L0_attn_x"])
    wq = np.asarray(inputs["L0_attn_Wq"])
    wk = np.asarray(inputs["L0_attn_Wk"])
    wv = np.asarray(inputs["L0_attn_Wv"])
    wo = np.asarray(inputs["L0_attn_Wo"])
    gam = np.asarray(inputs["L0_attn_gamma"])
    bet = np.asarray(inputs["L0_attn_beta"])
    wg = np.asarray(inputs["L0_ffn_w_gate"])
    wu = np.asarray(inputs["L0_ffn_w_up"])
    wd = np.asarray(inputs["L0_ffn_w_down"])

    def whead(w):
        return np.ascontiguousarray(
            w.reshape(H, NDC, 128, E).transpose(2, 1, 0, 3).reshape(128, NDC * H * E))

    m = {}
    m["wq"] = whead(wq)
    m["wk"] = whead(wk)
    m["wv"] = whead(wv)
    m["wo"] = np.ascontiguousarray(wo.reshape(NP, 128, D).transpose(1, 0, 2).reshape(128, NP * D))
    gb = np.zeros((128, 16), dtype=np.float32)
    gb[:, 0:8] = gam.reshape(8, 128).T
    gb[:, 8:16] = bet.reshape(8, 128).T
    m["gb"] = gb
    m["wg"] = np.ascontiguousarray(
        wg.reshape(NHC, 128, NDC, 128).transpose(0, 3, 2, 1).reshape(NHC, 128, D))
    m["wu"] = np.ascontiguousarray(
        wu.reshape(NHC, 128, NDC, 128).transpose(0, 3, 2, 1).reshape(NHC, 128, D))
    m["wd"] = np.ascontiguousarray(wd.T.reshape(NHC, 128, D))

    cbv = np.zeros((128, 1792), dtype=np.float32)
    cbv[:, 0] = 1.0
    for j in range(8):
        cbv[:, 64 + j * 64 + j] = 1.0
    for j in range(8):
        cbv[64 + j, 576 + (j * 2) * 64: 576 + (j * 2) * 64 + 64] = 1.0
        cbv[j, 576 + (j * 2 + 1) * 64: 576 + (j * 2 + 1) * 64 + 64] = 1.0
    tri = np.triu(np.ones((128, 128), dtype=np.float32))
    cbv[:, 1600:1728] = tri
    cbv[:, 1728:1792] = 1.0
    m["cb"] = cbv
    cfv = np.zeros((128, 144), dtype=np.float32)
    cfv[0, 0:128] = 1.0
    _CONSTVALS = [1e-05, 0.001, -LN127, -0.0625, 0.0625, 0.125, -0.125,
                  1.0 / 1024.0, 1.0 / 127.0, 0.5, -0.5, 2.0, -1.0, 3.0]
    for _i, _cv in enumerate(_CONSTVALS):
        cfv[:, 128 + _i] = _cv
    m["cf"] = cfv

    per_core_x = []
    for c in range(NCORE):
        xsh = np.empty((2, 128, NDC * SEQ), dtype=np.float32)
        for si in range(2):
            xi = x[2 * c + si]
            xsh[si] = xi.T.reshape(NDC, 128, SEQ).transpose(1, 0, 2).reshape(128, NDC * SEQ)
        per_core_x.append(np.ascontiguousarray(xsh))
    return m, per_core_x


def kernel(**inputs):
    m, per_core_x = _host_inputs(inputs)
    if _NC_CACHE[0] is None:
        _NC_CACHE[0] = build_program()[0]
    nc = _NC_CACHE[0]
    in_maps = []
    for c in range(NCORE):
        im = dict(m)
        im["x"] = per_core_x[c]
        in_maps.append(im)
    res = run_bass_kernel_spmd(nc, in_maps, core_ids=list(range(NCORE)))
    out = np.empty((B_TOT, SEQ, D), dtype=np.float32)
    for c in range(NCORE):
        o = res.results[c]["o"]
        for si in range(2):
            out[2 * c + si] = o[si].reshape(128, NDC, SEQ).transpose(2, 1, 0).reshape(SEQ, D)
    return out


if __name__ == "__main__":
    nc, counts = build_program()
    print("instruction counts:", counts, "total:", sum(counts.values()))


# version 28 (final)
# speedup vs baseline: 1.0246x; 1.0078x over previous
import numpy as np
import ml_dtypes
import concourse.bass as bass
import concourse.mybir as mybir
from concourse.alu_op_type import AluOpType
from concourse.bass_utils import run_bass_kernel_spmd

F32 = mybir.dt.float32
BF16 = mybir.dt.bfloat16
I32 = mybir.dt.int32
ACT = mybir.ActivationFunctionType
AX = mybir.AxisListType

NCORE = 8
B_TOT = 16
SEQ = 2048
D = 1024
H = 16
E = 64
NDC = 8
NP = 8
CH = 512
NCHUNK = 4
NHC = 24
LN_EPS_PRE = 1e-5
LN_EPS_3 = 1e-3
LN127 = float(np.log(127.0))


class Tok:
    __slots__ = ("src", "idx", "queue")
    def __init__(self, src, idx, queue):
        self.src, self.idx, self.queue = src, idx, queue


class Buf:
    __slots__ = ("w", "r", "name", "psum")
    def __init__(self, name="", psum=False):
        self.w, self.r, self.name = None, [], name
        self.psum = psum


class Plan:
    ENGS = ("t", "v", "s", "g", "y")

    def __init__(self):
        self.ops = {e: [] for e in self.ENGS}
        self.cnt = {e: 0 for e in self.ENGS}
        self.qcnt = {}
        self.waited = {e: {} for e in self.ENGS}
        self.referenced = set()

    def emit(self, eng, fn, reads=(), writes=(), queue=None):
        deps = []
        for b in reads:
            if b.w is not None:
                deps.append((b.w, True))
            if b.psum and eng in ("v", "s"):
                for t in b.r:
                    if t.src in ("v", "s") and t.src != eng:
                        deps.append((t, False))
        qskip = []
        for b in writes:
            if b.w is not None:
                deps.append((b.w, False))
                if b.r:
                    qskip.append(b.w)
            for t in b.r:
                deps.append((t, False))
                qskip.append(t)
        for b in reads:
            if b.w is not None:
                qskip.append(b.w)
        waits = {}
        for t, is_raw in deps:
            if t.src == eng and not t.queue and eng == "t":
                continue
            if (queue is not None and t.queue and t.src == queue
                    and t in qskip):
                continue
            if t.queue:
                t = Tok(t.src, self.qcnt[t.src], True)
            cur = waits.get(t.src)
            if cur is None or t.idx > cur.idx:
                waits[t.src] = t
        final = []
        for src, t in waits.items():
            if self.waited[eng].get(src, -1) >= t.idx:
                continue
            self.waited[eng][src] = t.idx
            final.append(t)
            if not t.queue:
                self.referenced.add((t.src, t.idx))
        if queue is not None:
            self.qcnt[queue] = self.qcnt.get(queue, 0) + 16
            tok = Tok(queue, self.qcnt[queue], True)
        else:
            self.cnt[eng] += 1
            tok = Tok(eng, self.cnt[eng], False)
        self.ops[eng].append((final, fn, tok, queue))
        for b in reads:
            b.r.append(tok)
        for b in writes:
            b.w = tok
            b.r = []
        return tok

    def finalize(self):
        semval = {}
        for e in self.ENGS:
            c = 0
            for (waits, fn, tok, queue) in self.ops[e]:
                if tok.queue:
                    semval[(tok.src, tok.idx)] = tok.idx
                elif (tok.src, tok.idx) in self.referenced:
                    c += 1
                    semval[(tok.src, tok.idx)] = c
                else:
                    semval[(tok.src, tok.idx)] = None
        return semval


import os as _os
_LIM = _os.environ.get("KMK_LIM", "full")
_NOINJ = _os.environ.get("KMK_NOINJ", "0") == "1"
_INJL = 0 if _NOINJ else int(_os.environ.get("KMK_INJL", "3"))
_NOQB = _os.environ.get("KMK_NOQB", "0") == "1"
_NODEF = _os.environ.get("KMK_NODEF", "0") == "1"
_NOPREP = _os.environ.get("KMK_NOPREP", "1") == "1"
_LIMS = {"s": 0, "qkv_w": 0.2, "qkv_k": 0.4, "k2a": 0.45, "k2b": 0.5, "k2c": 0.55, "qkv_k2": 0.6, "qkv": 1, "attn0": 2, "half0": 3, "seq0": 4, "full": 99}


def build_program():
    nc = bass.Bass("TRN2", target_bir_lowering=False)

    dx = nc.dram_tensor("x", [2, 128, NDC * SEQ], F32, kind="ExternalInput")
    dwq = nc.dram_tensor("wq", [128, NDC * H * E], F32, kind="ExternalInput")
    dwk = nc.dram_tensor("wk", [128, NDC * H * E], F32, kind="ExternalInput")
    dwv = nc.dram_tensor("wv", [128, NDC * H * E], F32, kind="ExternalInput")
    dwo = nc.dram_tensor("wo", [128, NP * D], F32, kind="ExternalInput")
    dgb = nc.dram_tensor("gb", [128, 16], F32, kind="ExternalInput")
    dwg = nc.dram_tensor("wg", [NHC, 128, D], I32, kind="ExternalInput")
    dwu = nc.dram_tensor("wu", [NHC, 128, D], I32, kind="ExternalInput")
    dwd = nc.dram_tensor("wd", [NHC, 128, D], I32, kind="ExternalInput")
    dcb = nc.dram_tensor("cb", [128, 1792], F32, kind="ExternalInput")
    dcf = nc.dram_tensor("cf", [128, 144], F32, kind="ExternalInput")
    dout = nc.dram_tensor("o", [2, 128, NDC * SEQ], F32, kind="ExternalOutput")

    P = Plan()

    from contextlib import ExitStack
    with ExitStack() as _stack:
        ent = _stack.enter_context
        xh = ent(nc.sbuf_tensor("xh", [128, 8192], F32))
        zr = ent(nc.sbuf_tensor("zr", [128, 16384], BF16))
        kb = ent(nc.sbuf_tensor("kb", [128, NP * SEQ], BF16))
        vb = ent(nc.sbuf_tensor("vb", [128, 16 * 1024], BF16))
        wqb = ent(nc.sbuf_tensor("wqb", [128, 8192], BF16))
        wob = ent(nc.sbuf_tensor("wob", [128, 8192], BF16))
        wstag = ent(nc.sbuf_tensor("wstag", [128, 3072], I32))
        qsb_t = ent(nc.sbuf_tensor("qsb", [128, 2048], BF16))
        brt = ent(nc.sbuf_tensor("brt", [128, 2], F32))
        wgbb = ent(nc.sbuf_tensor("wgbb", [128, 2048], BF16))
        wubb = ent(nc.sbuf_tensor("wubb", [128, 2048], BF16))
        wdbb = ent(nc.sbuf_tensor("wdbb", [128, 2048], BF16))
        tsc = ent(nc.sbuf_tensor("tsc", [128, 1024], F32))
        invs = ent(nc.sbuf_tensor("invs", [128, 512], BF16))
        mub = ent(nc.sbuf_tensor("mub", [128, 512], BF16))
        rub = ent(nc.sbuf_tensor("rub", [128, 512], BF16))
        rowa = ent(nc.sbuf_tensor("rowa", [128, 512], F32))
        rowb = ent(nc.sbuf_tensor("rowb", [128, 512], F32))
        browb = ent(nc.sbuf_tensor("browb", [128, 8], BF16))
        bbias = ent(nc.sbuf_tensor("bbias", [128, 2], F32))
        k2m = ent(nc.sbuf_tensor("k2m", [128, 2], F32))
        mk2 = ent(nc.sbuf_tensor("mk2", [128, 2], F32))
        sgb = ent(nc.sbuf_tensor("sgb", [128, 16], F32))
        xst = ent(nc.sbuf_tensor("xst", [128, 2048], BF16))
        cbt = ent(nc.sbuf_tensor("cbt", [128, 1792], BF16))
        cft = ent(nc.sbuf_tensor("cft", [128, 144], F32))
        ps01 = ent(nc.psum_tensor("ps01", [128, 1024], F32))
        ps23 = ent(nc.psum_tensor("ps23", [128, 1024], F32))
        ps4 = ent(nc.psum_tensor("ps4", [128, 512], F32))
        ps5 = ent(nc.psum_tensor("ps5", [128, 512], F32))
        ps6 = ent(nc.psum_tensor("ps6", [128, 512], F32))
        ps7 = ent(nc.psum_tensor("ps7", [128, 512], F32))
        sem_t = ent(nc.semaphore("sem_t"))
        sem_v = ent(nc.semaphore("sem_v"))
        sem_s = ent(nc.semaphore("sem_s"))
        sem_g = ent(nc.semaphore("sem_g"))
        sem_y = ent(nc.semaphore("sem_y"))
        sem_qc = ent(nc.semaphore("sem_qc"))
        sem_qxh = [ent(nc.semaphore("sem_qxh%d" % i)) for i in range(8)]
        sem_qt0 = ent(nc.semaphore("sem_qt0"))
        sem_qt1 = ent(nc.semaphore("sem_qt1"))
        sem_qw0 = ent(nc.semaphore("sem_qw0"))
        sem_qw1 = ent(nc.semaphore("sem_qw1"))
        sem_qw2 = ent(nc.semaphore("sem_qw2"))
        sem_qw3 = ent(nc.semaphore("sem_qw3"))
        sem_qo = ent(nc.semaphore("sem_qo"))
        sem_qo1 = ent(nc.semaphore("sem_qo1"))
        with nc.Block() as blk0:
            @blk0.vector
            def _(v0):
                for _sm in ([sem_t, sem_v, sem_s, sem_g, sem_y, sem_qc,
                             sem_qt0, sem_qt1, sem_qw0, sem_qw1, sem_qw2, sem_qw3,
                             sem_qo, sem_qo1]
                            + list(sem_qxh)):
                    v0.sem_clear(_sm)
        block = ent(nc.Block())
        class _PsView:
            __slots__ = ("t", "coff")
            def __init__(self, t, coff):
                self.t, self.coff = t, coff
            def __getitem__(self, idx):
                if isinstance(idx, tuple):
                    rows, cols = idx
                else:
                    rows, cols = idx, slice(None)
                c0 = cols.start or 0
                c1 = 512 if cols.stop is None else cols.stop
                return self.t[rows, self.coff + c0: self.coff + c1]
        PS = [_PsView(ps01, 0), _PsView(ps01, 512), _PsView(ps23, 0), _PsView(ps23, 512),
              ps4, ps5, ps6, ps7]
        PSB = [ps01, ps23]
        SEMS = {"t": sem_t, "v": sem_v, "s": sem_s, "g": sem_g, "y": sem_y,
                "qc": sem_qc, "qt0": sem_qt0, "qt1": sem_qt1,
                "qw0": sem_qw0, "qw1": sem_qw1, "qw2": sem_qw2, "qw3": sem_qw3,
                "qo": sem_qo, "qo1": sem_qo1}
        for _i in range(8):
            SEMS["qxh%d" % _i] = sem_qxh[_i]

        c_ones64 = cbt[:, 0:64]
        c_onescol = [cbt[:, 64 + j * 64: 128 + j * 64] for j in range(8)]
        c_eselT = [cbt[:, 576 + (j * 2) * 64: 576 + (j * 2) * 64 + 64] for j in range(8)]
        c_eselB = [cbt[:, 576 + (j * 2 + 1) * 64: 576 + (j * 2 + 1) * 64 + 64] for j in range(8)]
        c_mask = cbt[:, 1600:1728]
        c_allones = cbt[:, 1728:1792]
        c_onesrow0 = cft[:, 0:128]
        _CONSTVALS = [1e-05, 0.001, -LN127, -0.0625, 0.0625, 0.125, -0.125,
                      1.0 / 1024.0, 1.0 / 127.0, 0.5, -0.5, 2.0, -1.0, 3.0]
        for _i, _cv in enumerate(_CONSTVALS):
            nc.const_aps.register(F32, float(_cv), cft[:, 128 + _i:129 + _i])

        bXH = [[Buf(f"xh{d}_{p}") for p in range(2)] for d in range(NDC)]
        bKB, bVB = Buf("kb"), Buf("vb")
        bWQ = [Buf(f"wq{i}") for i in range(8)]
        bWO = [Buf(f"wo{i}") for i in range(8)]
        bPS = [Buf(f"ps{i}", psum=True) for i in range(8)]
        bWST = [Buf("wst0"), Buf("wst1"), Buf("wst2")]
        bQS = [Buf(f"qs{i}") for i in range(4)]
        bWG = [Buf("wg0"), Buf("wg1")]
        bWU = [Buf("wu0"), Buf("wu1")]
        bWD = [Buf("wd0"), Buf("wd1")]
        bTS = [Buf("ts0"), Buf("ts1")]
        bINV, bROWA, bROWB, bROWT = Buf("invs"), Buf("rowa"), Buf("rowb"), Buf("rowt")
        bMUB = Buf("mub")
        bRUB = Buf("rub")
        bBROWB, bK2, bMK2, bGB = Buf("browb"), Buf("k2m"), Buf("mk2"), Buf("sgb")
        bBB = [Buf("bb0"), Buf("bb1")]
        bXST = [Buf("xst0"), Buf("xst1"), Buf("xst2"), Buf("xst3")]
        bCB, bCF = Buf("cb"), Buf("cf")

        zr_era = [[]]

        def new_era(bufnames):
            old_toks = []
            for b in zr_era[0]:
                if b.w is not None:
                    old_toks.append(b.w)
                old_toks.extend(b.r)
            out = []
            for n in bufnames:
                b = Buf(n)
                b.r = list(old_toks)
                out.append(b)
            zr_era[0] = list(out)
            return out

        def rebind(bufnames, old):
            toks = []
            for b in old:
                if b.w is not None:
                    toks.append(b.w)
                toks.extend(b.r)
            out = []
            for n in bufnames:
                b = Buf(n)
                b.r = list(toks)
                out.append(b)
            for b in old:
                zr_era[0].remove(b)
            zr_era[0].extend(out)
            return out

        E_ = P.emit

        E_("y", lambda y: y.dma_start(tsc[:], dcb.ap()[:, 0:1024]), writes=[bTS[0], bTS[1]], queue="qc")
        E_("y", lambda y: y.dma_start(cft[:], dcf.ap()), writes=[bCF], queue="qc")
        E_("y", lambda y: y.dma_start(sgb[:], dgb.ap()), writes=[bGB], queue="qc")
        E_("v", lambda v: v.tensor_copy(cbt[:, 0:1024], tsc[:]), reads=[bTS[0], bTS[1], bCF, bGB], writes=[bCB])
        E_("y", lambda y: y.dma_start(tsc[:, 0:768], dcb.ap()[:, 1024:1792]), writes=[bTS[0], bTS[1]], queue="qc")
        E_("v", lambda v: v.tensor_copy(cbt[:, 1024:1792], tsc[:, 0:768]), reads=[bTS[0], bTS[1]], writes=[bCB])
        E_("v", lambda v: v.memset(invs[:], 0.0), writes=[bINV])
        E_("v", lambda v: v.memset(qsb_t[:], 0.0), writes=list(bQS))
        E_("v", lambda v: v.memset(rowa[:], 0.0), writes=[bROWA])
        E_("v", lambda v: v.memset(rowb[:], 0.0), writes=[bROWB])
        E_("v", lambda v: v.memset(browb[:], 0.0), writes=[bBROWB])

        for _k in range(8):
            E_("t", lambda t, d=PS[_k][:, 0:512]: t.matmul(
                d, c_onesrow0, rowa[:, 0:512], start=True, stop=True),
                reads=[bROWA, bCF], writes=[bPS[_k]])

        nstores = [0, 0]

        def rowmath_r(stat_buf, s1_ap, s2_ap, eps, extra_bias):
            a1 = rowa[0:1, 0:512]
            t0 = tsc[0:1, 0:512]
            t32 = tsc[0:1, 512:1024]
            t64 = t0
            t96 = t32
            b1 = rowb[0:1, 0:512]
            E_("s", lambda s, a=s1_ap, d=a1: s.activation(d, a, ACT.Copy, scale=1.0 / 1024.0),
               reads=[stat_buf], writes=[bROWA])
            E_("s", lambda s, a=s2_ap, d=t0: s.activation(d, a, ACT.Copy, scale=1.0 / 1024.0),
               reads=[stat_buf], writes=[bTS[0]])
            E_("v", lambda v, a=a1, d=t32: v.tensor_tensor(d, a, a, AluOpType.mult),
               reads=[bROWA], writes=[bTS[1]])
            E_("v", lambda v, a=t0, b=t32, d=t64: v.tensor_tensor(d, a, b, AluOpType.subtract),
               reads=[bTS[1], bTS[0]], writes=[bTS[0]])
            E_("s", lambda s, a=t64, d=t96, eps=eps: s.activation(d, a, ACT.Ln, bias=float(eps), scale=1.0),
               reads=[bTS[0], bCF], writes=[bTS[1]])
            E_("s", lambda s, a=t96, d=b1, xb=extra_bias: s.activation(d, a, ACT.Exp, bias=float(xb), scale=-0.5),
               reads=[bTS[1], bCF], writes=[bROWB])

        def bcast_f32(pidx, src_t, src_buf):
            dst = PS[pidx][:, 0:512]
            src = src_t[:, 0:512]
            E_("t", lambda t, d=dst, s=src: t.matmul(d, c_onesrow0, s, start=True, stop=True),
               reads=[src_buf, bCF], writes=[bPS[pidx]])

        def _stat_slots(sl, alt):
            if alt:
                return xst[:, 0:512], bXST[0], xst[:, 512:1024], bXST[2]
            xb = xst[:, sl * 512:(sl + 1) * 512]
            bxb = bXST[2 * sl]
            x2 = xst[:, 1024 + sl * 512: 1024 + (sl + 1) * 512]
            bx2 = bXST[2 * sl + 1]
            return xb, bxb, x2, bx2

        def stats_piece_ve(xsl, xbuf, sl, use_qsb=False):
            xb, bxb, x2, bx2 = _stat_slots(sl, use_qsb)
            E_("v", lambda v, a=xsl, d=xb: v.tensor_copy(d, a),
               reads=[xbuf], writes=[bxb])
            E_("s", lambda s, a=xsl, d=x2: s.activation(d, a, ACT.Square),
               reads=[xbuf], writes=[bx2])

        def stats_piece_mm(dc, sl, pidx, use_qsb=False):
            xb, bxb, x2, bx2 = _stat_slots(sl, use_qsb)
            E_("t", lambda t, dc=dc, d=PS[pidx][0:64, :], a=xb: t.matmul(
                d, c_ones64, a, start=(dc == 0), stop=(dc == NDC - 1), tile_position=(0, 0), skip_group_check=True),
                reads=[bxb, bCB], writes=[bPS[pidx]])
            E_("t", lambda t, dc=dc, d=PS[pidx][64:128, :], a=x2: t.matmul(
                d, c_ones64, a, start=(dc == 0), stop=(dc == NDC - 1), tile_position=(0, 64),
                skip_group_check=True),
                reads=[bx2, bCB], writes=[bPS[pidx]])

        def stats_chunk(xcols, pidx, xh_bufs):
            for dc, (xsl, xbuf) in enumerate(xcols):
                sl = dc % 2
                stats_piece_ve(xsl, xbuf, sl)
                stats_piece_mm(dc, sl, pidx)

        def ffn_stat_items(tf, bank, psum_r, use_qsb=False):
            xcols = [(xh[:, dc * 1024 + tf * 512: dc * 1024 + tf * 512 + 512], bXH[dc][tf])
                     for dc in range(NDC)]
            items = []
            for dc, (xsl, xbuf) in enumerate(xcols):
                sl = dc % 2
                items.append(lambda xsl=xsl, xbuf=xbuf, sl=sl: stats_piece_ve(xsl, xbuf, sl, use_qsb))
                items.append(lambda dc=dc, sl=sl: stats_piece_mm(dc, sl, bank, use_qsb))
            items.append(lambda: rowmath_r(bPS[bank], PS[bank][0:1, :], PS[bank][64:65, :],
                                           LN_EPS_PRE, -LN127))
            items.append(None)
            def _mu():
                bcast_f32(bank, rowa, bROWA)
                E_("v", lambda v: v.tensor_copy(mub[:], PS[bank][:]),
                   reads=[bPS[bank]], writes=[bMUB])
            items.append(_mu)
            items.append(None)
            items.append(lambda: bcast_f32(bank, rowb, bROWB))
            if not psum_r:
                items.append(lambda: E_("v", lambda v: v.tensor_copy(rub[:], PS[bank][:]),
                                        reads=[bPS[bank]], writes=[bRUB]))
            items.append(None)
            return items

        def ffn_zgen_items(tf, bank, z3Bl, z3_ap_l, psum_r):
            items = []
            for dc in range(NDC):
                xsl = xh[:, dc * 1024 + tf * 512: dc * 1024 + tf * 512 + 512]
                xbuf = bXH[dc][tf]
                tslot = dc % 2
                tt = tsc[:, tslot * 512:(tslot + 1) * 512]
                dap = z3_ap_l[dc]
                dbuf = z3Bl[dc]
                def _z(xsl=xsl, xbuf=xbuf, tt=tt, dap=dap, dbuf=dbuf, tslot=tslot):
                    E_("v", lambda v, a=xsl, d=tt: v.tensor_tensor(
                        d, a, mub[:], AluOpType.subtract),
                        reads=[xbuf, bMUB], writes=[bTS[tslot]])
                    if psum_r:
                        E_("v", lambda v, a=tt, d=dap: v.tensor_tensor(
                            d, a, PS[bank][:], AluOpType.mult),
                            reads=[bTS[tslot], bPS[bank]], writes=[dbuf])
                    else:
                        E_("v", lambda v, a=tt, d=dap: v.tensor_tensor(
                            d, a, rub[:], AluOpType.mult),
                            reads=[bTS[tslot], bRUB], writes=[dbuf])
                items.append(_z)
            return items

        def zgen(xcols, mu_pidx, r_pidx, dsts):
            for dc, (xsl, xbuf) in enumerate(xcols):
                tslot = dc % 2
                tt = tsc[:, tslot * 512:(tslot + 1) * 512]
                dap, dbufs = dsts[dc]
                E_("v", lambda v, a=xsl, d=tt, m=PS[mu_pidx][:]: v.tensor_tensor(
                    d, a, m, AluOpType.subtract),
                    reads=[xbuf, bPS[mu_pidx]], writes=[bTS[tslot]])
                E_("v", lambda v, a=tt, d=dap, r=PS[r_pidx][:]: v.tensor_tensor(
                    d, a, r, AluOpType.mult),
                    reads=[bTS[tslot], bPS[r_pidx]], writes=dbufs)

        def load_weights_bf16(dsrc, dstT, dstB):
            for pc in range(16):
                slot = pc % 2
                st = tsc[:, slot * 512:(slot + 1) * 512]
                src = dsrc.ap()[:, pc * 512:(pc + 1) * 512]
                dst = dstT[:, pc * 512:(pc + 1) * 512]
                E_("y", lambda y, d=st, s=src: y.dma_start(d, s),
                   writes=[bTS[slot]], queue="qt%d" % slot)
                E_("v", lambda v, d=dst, s=st: v.tensor_copy(d, s),
                   reads=[bTS[slot]], writes=[dstB[pc // 2]])

        def load_weights_bf16_pair(jobs):
            for pc in range(16):
                for j, (dsrc, dstT, dstB) in enumerate(jobs):
                    slot = j % 2
                    st = tsc[:, slot * 512:(slot + 1) * 512]
                    src = dsrc.ap()[:, pc * 512:(pc + 1) * 512]
                    dst = dstT[:, pc * 512:(pc + 1) * 512]
                    E_("y", lambda y, d=st, s=src: y.dma_start(d, s),
                       writes=[bTS[slot]], queue="qt%d" % slot)
                    E_("v", lambda v, d=dst, s=st: v.tensor_copy(d, s),
                       reads=[bTS[slot]], writes=[dstB[pc // 2]])

        lim = _LIMS[_LIM]
        for s_i in range(2):
            if s_i == 1 and lim < 99:
                break
            xs = dx.ap()[s_i]
            os_ = dout.ap()[s_i]

            z1B = new_era([f"z1_{d}" for d in range(NDC)])

            def emit_sload(hc0):
                for dc in range(NDC):
                    for pc_ in range(2):
                        dst = xh[:, dc * 1024 + pc_ * 512: dc * 1024 + pc_ * 512 + 512]
                        src = xs[:, dc * 2048 + hc0 + pc_ * 512: dc * 2048 + hc0 + pc_ * 512 + 512]
                        E_("y", lambda y, d=dst, s=src: y.dma_start(d, s),
                           writes=[bXH[dc][pc_]], queue="qxh%d" % dc)

            emit_sload(0)
            if s_i == 0:
                load_weights_bf16_pair([(dwk, wqb, bWQ), (dwv, wob, bWO)])
            for half in range(2):
                hc0 = half * 1024
                if half == 1:
                    emit_sload(hc0)
                for pc in range(2):
                    xcols = [(xh[:, dc * 1024 + pc * 512: dc * 1024 + pc * 512 + 512], bXH[dc][pc])
                             for dc in range(NDC)]
                    stats_chunk(xcols, 2, bXH)
                    rowmath_r(bPS[2], PS[2][0:1, :], PS[2][64:65, :], LN_EPS_PRE, 0.0)
                    bcast_f32(5, rowa, bROWA)
                    bcast_f32(4, rowb, bROWB)
                    dsts = []
                    for dc in range(NDC):
                        col = dc * 2048 + hc0 + pc * 512
                        dsts.append((zr[:, col:col + 512], [z1B[dc]]))
                    zgen(xcols, 5, 4, dsts)

            if lim < 0.2:
                break
            E_("v", lambda v: v.memset(k2m[:, 0:1], -1.0), writes=[bK2])
            if lim < 0.4:
                break
            for pr in range(NP):
                for piece in range(4):
                    pq = PS[piece % 2]
                    bpq = bPS[piece % 2]
                    for dc in range(NDC):
                        zsl = zr[:, dc * 2048 + piece * 512: dc * 2048 + piece * 512 + 512]
                        wE = wqb[:, dc * 1024 + (2 * pr) * 64: dc * 1024 + (2 * pr) * 64 + 64]
                        wO_ = wqb[:, dc * 1024 + (2 * pr + 1) * 64: dc * 1024 + (2 * pr + 1) * 64 + 64]
                        E_("t", lambda t, dc=dc, d=pq[0:64, :], w=wE, z=zsl: t.matmul(
                            d, w, z, start=(dc == 0), stop=(dc == NDC - 1), tile_position=(0, 0), skip_group_check=True),
                            reads=[bWQ[dc], z1B[dc]], writes=[bpq])
                        E_("t", lambda t, dc=dc, d=pq[64:128, :], w=wO_, z=zsl: t.matmul(
                            d, w, z, start=(dc == 0), stop=(dc == NDC - 1), tile_position=(0, 64), skip_group_check=True),
                            reads=[bWQ[dc], z1B[dc]], writes=[bpq])
                    kdst = kb[:, pr * 2048 + piece * 512: pr * 2048 + piece * 512 + 512]
                    E_("v", lambda v, d=kdst, a=pq[:]: v.tensor_copy(d, a),
                       reads=[bpq], writes=[bKB])
                    if lim < 0.45:
                        continue
                    ksq = xst[:, 0:512]
                    E_("s", lambda s, a=pq[:], d=ksq: s.activation(d, a, ACT.Square),
                       reads=[bpq], writes=[bXST[0]])
                    if lim < 0.5:
                        continue
                    E_("t", lambda t, a=ksq: t.matmul(
                        PS[6][0:64, :], c_ones64, a, start=True, stop=True,
                        tile_position=(0, 0)),
                        reads=[bXST[0], bCB], writes=[bPS[6]])
                    if lim < 0.55:
                        continue
                    E_("v", lambda v: v.tensor_reduce(
                        k2m[0:1, 1:2], PS[6][0:1, :], AX.X, AluOpType.max),
                        reads=[bPS[6]], writes=[bK2])
                    if lim < 0.6:
                        continue
                    E_("v", lambda v: v.tensor_tensor(
                        k2m[0:1, 0:1], k2m[0:1, 0:1], k2m[0:1, 1:2], AluOpType.max),
                        reads=[bK2], writes=[bK2])
            if lim < 1:
                break
            E_("v", lambda v: v.tensor_scalar(
                mk2[0:1, 1:2], k2m[0:1, 0:1], -0.0625, 0.0, AluOpType.mult, AluOpType.add),
                reads=[bK2, bCF], writes=[bMK2])

            for tk in range(16):
                for hh in range(2):
                    pv = PS[4 + hh]
                    bpv = bPS[4 + hh]
                    for dc in range(NDC):
                        zsl = zr[:, dc * 2048 + tk * 128: dc * 2048 + tk * 128 + 128]
                        wsl = wob[:, dc * 1024 + hh * 512: dc * 1024 + hh * 512 + 512]
                        E_("t", lambda t, dc=dc, d=pv[:], z=zsl, w=wsl: t.matmul(
                            d, z, w, start=(dc == 0), stop=(dc == NDC - 1)),
                            reads=[bWO[dc], z1B[dc]], writes=[bpv])
                    vdst = vb[:, tk * 1024 + hh * 512: tk * 1024 + hh * 512 + 512]
                    E_("v", lambda v, d=vdst, a=pv[:]: v.tensor_copy(d, a),
                       reads=[bpv], writes=[bVB])
                slot = tk % 2
                st = tsc[:, slot * 512:(slot + 1) * 512]
                wsrc = dwq.ap()[:, tk * 512:(tk + 1) * 512]
                wdst = wqb[:, tk * 512:(tk + 1) * 512]
                E_("y", lambda y, d=st, s=wsrc: y.dma_start(d, s),
                   writes=[bTS[slot]], queue="qt%d" % slot)
                E_("v", lambda v, d=wdst, s=st: v.tensor_copy(d, s),
                   reads=[bTS[slot]], writes=[bWQ[tk // 2]])

            wo_items = []
            for _pc in range(16):
                def _wol(pc=_pc, slot=_pc % 2):
                    st = tsc[:, slot * 512:(slot + 1) * 512]
                    wsrc = dwo.ap()[:, pc * 512:(pc + 1) * 512]
                    wdst = wob[:, pc * 512:(pc + 1) * 512]
                    E_("y", lambda y, d=st, s=wsrc: y.dma_start(d, s),
                       writes=[bTS[slot]], queue="qt%d" % slot)
                    E_("v", lambda v, d=wdst, s=st: v.tensor_copy(d, s),
                       reads=[bTS[slot]], writes=[bWO[pc // 2]])
                wo_items.append(_wol)

            if lim < 2:
                break
            def emit_xload(half, pc_):
                for dc in range(NDC):
                    dst = xh[:, dc * 1024 + pc_ * 512: dc * 1024 + pc_ * 512 + 512]
                    src = xs[:, dc * 2048 + half * 1024 + pc_ * 512:
                              dc * 2048 + half * 1024 + pc_ * 512 + 512]
                    E_("y", lambda y, d=dst, s=src: y.dma_start(d, s),
                       writes=[bXH[dc][pc_]], queue="qxh%d" % dc)

            for half in range(2):
                if half == 0:
                    for pc_ in range(2):
                        emit_xload(half, pc_)

                names = ([f"zcA{d}" for d in range(NDC)] + [f"zcB{d}" for d in range(NDC)]
                         + [f"an{p}" for p in range(NP)]
                         + ["pb0", "pb1", "qy2"])
                eb = new_era(names)
                zcB = [eb[0:8], eb[8:16]]
                anB = eb[16:24]
                pbB = eb[24:26]; q2B = eb[26]; y2B = eb[26]
                ZC0, AN0, PB0, Q20 = 0, 8192, 12288, 14336
                Y20 = Q20
                zc_ap = [[zr[:, ZC0 + sl * 4096 + d * 512: ZC0 + sl * 4096 + d * 512 + 512]
                          for d in range(NDC)] for sl in range(2)]
                an_ap = [zr[:, AN0 + p * 512: AN0 + p * 512 + 512] for p in range(NP)]
                qs_ap = [qsb_t[:, i * 512:(i + 1) * 512] for i in range(4)]
                q2_ap = zr[:, Q20:Q20 + 512]
                y2_ap = zr[:, Y20:Y20 + 512]

                def att_prep_items(ci):
                    xcol = ci * 512
                    slot = ci % 2
                    xcols = [(xh[:, dc * 1024 + xcol: dc * 1024 + xcol + 512], bXH[dc][ci])
                             for dc in range(NDC)]
                    items = []
                    for dc, (xsl, xbuf) in enumerate(xcols):
                        sl = dc % 2
                        items.append(lambda xsl=xsl, xbuf=xbuf, sl=sl: stats_piece_ve(xsl, xbuf, sl))
                        items.append(lambda dc=dc, sl=sl: stats_piece_mm(dc, sl, 6))
                    items.append(lambda: rowmath_r(bPS[6], PS[6][0:1, :], PS[6][64:65, :],
                                                   LN_EPS_PRE, 0.0))
                    items.append(None)
                    def _mu1():
                        bcast_f32(6, rowa, bROWA)
                        E_("v", lambda v: v.tensor_copy(mub[:], PS[6][:]),
                           reads=[bPS[6]], writes=[bMUB])
                    items.append(_mu1)
                    items.append(None)
                    items.append(lambda: bcast_f32(6, rowb, bROWB))
                    items.append(None)
                    for dc in range(NDC):
                        def _z(dc=dc):
                            xsl = xh[:, dc * 1024 + xcol: dc * 1024 + xcol + 512]
                            tslot = dc % 2
                            tt = tsc[:, tslot * 512:(tslot + 1) * 512]
                            zdst = zc_ap[slot][dc]
                            E_("v", lambda v, a=xsl, d=tt: v.tensor_tensor(
                                d, a, mub[:], AluOpType.subtract),
                                reads=[bXH[dc][xcol // 512], bMUB], writes=[bTS[tslot]])
                            E_("v", lambda v, d=zdst, a=tt: v.tensor_tensor(
                                d, a, PS[6][:], AluOpType.mult),
                                reads=[bTS[tslot], bPS[6]], writes=[zcB[slot][dc]])
                        items.append(_z)
                    return items

                def att_prep(ci):
                    for it in att_prep_items(ci):
                        if it is not None:
                            it()
                    emit_qB(0, ci % 2)

                def emit_qB_stage(pr, slot, stage):
                    qe = 2 * (pr % 2)
                    bslot = pr % 2
                    if stage == 0:
                        for dc in range(NDC):
                            wE = wqb[:, dc * 1024 + (2 * pr) * 64: dc * 1024 + (2 * pr) * 64 + 64]
                            wO_ = wqb[:, dc * 1024 + (2 * pr + 1) * 64: dc * 1024 + (2 * pr + 1) * 64 + 64]
                            E_("t", lambda t, dc=dc, w=wE, z=zc_ap[slot][dc]: t.matmul(
                                PS[7][0:64, :], w, z, start=(dc == 0), stop=(dc == NDC - 1),
                                tile_position=(0, 0), skip_group_check=True),
                                reads=[bWQ[dc], zcB[slot][dc]], writes=[bPS[7]])
                            E_("t", lambda t, dc=dc, w=wO_, z=zc_ap[slot][dc]: t.matmul(
                                PS[7][64:128, :], w, z, start=(dc == 0), stop=(dc == NDC - 1),
                                tile_position=(0, 64), skip_group_check=True),
                                reads=[bWQ[dc], zcB[slot][dc]], writes=[bPS[7]])
                        E_("v", lambda v, d=qs_ap[qe]: v.tensor_copy(d[0:64, :], PS[7][0:64, :]),
                           reads=[bPS[7]], writes=[bQS[qe]])
                        E_("v", lambda v, d=qs_ap[qe + 1]: v.tensor_copy(d[64:128, :], PS[7][64:128, :]),
                           reads=[bPS[7]], writes=[bQS[qe + 1]])
                        E_("s", lambda s: s.activation(q2_ap, PS[7][:], ACT.Square),
                           reads=[bPS[7]], writes=[q2B])
                    elif stage == 1:
                        E_("t", lambda t: t.matmul(PS[7][0:64, :], c_ones64, q2_ap,
                                                   start=True, stop=True, tile_position=(0, 0)),
                           reads=[bCB, q2B], writes=[bPS[7]])
                    elif stage == 2:
                        E_("v", lambda v: v.tensor_reduce(brt[0:1, 0:1], PS[7][0:1, :],
                                                          AX.X, AluOpType.max),
                           reads=[bPS[7]], writes=[bROWT])
                        bcol = browb[0:1, bslot:bslot + 1]
                        E_("v", lambda v, d=bcol: v.scalar_tensor_tensor(
                            d, brt[0:1, 0:1], -0.0625, mk2[0:1, 1:2],
                            AluOpType.mult, AluOpType.add),
                            reads=[bROWT, bMK2], writes=[bBROWB])
                    else:
                        bsl = browb[:, bslot:bslot + 1]
                        E_("t", lambda t, s=bsl: t.matmul(
                            PS[7][0:64, 0:1], c_allones, s, start=True, stop=True,
                            tile_position=(0, 0)),
                            reads=[bBROWB, bCB], writes=[bPS[7]])
                        E_("t", lambda t, s=bsl: t.matmul(
                            PS[7][64:128, 0:1], c_allones, s, start=True, stop=True,
                            tile_position=(0, 64)),
                            reads=[bBROWB, bCB], writes=[bPS[7]])
                        bb = bbias[:, bslot:bslot + 1]
                        E_("v", lambda v, d=bb: v.tensor_copy(d, PS[7][:, 0:1]),
                           reads=[bPS[7]], writes=[bBB[bslot]])

                def emit_qB(pr, slot):
                    for st in range(4):
                        emit_qB_stage(pr, slot, st)

                def att_heads(ci, inject=(), next_qB=None, inject_rate=1):
                    inj = list(inject)
                    ii = [0]

                    def pop_inject():
                        if ii[0] < len(inj):
                            it = inj[ii[0]]
                            ii[0] += 1
                            if it is not None:
                                it()
                    C = half * 2 + ci
                    slot = ci % 2
                    ntk_full = 4 * C
                    ntiles = ntk_full + 4
                    first_sum = {0: True, 64: True}
                    for pr in range(NP):
                        bslot = pr % 2
                        bb = bbias[:, bslot:bslot + 1]
                        abank = 4
                        first_av = {0: True, 64: True}

                        def emit_avsums(tau, parity, n, qoff, tslot2):
                            hd = 2 * pr + parity
                            pap = zr[:, PB0 + tslot2 * 1024 + parity * 512:
                                     PB0 + tslot2 * 1024 + parity * 512 + n]
                            if parity == 0:
                                av_tp, sum_tp = (0, 0), (0, 64)
                                ar0, ar1, sr0, sr1 = 0, 64, 64, 128
                            else:
                                av_tp, sum_tp = (0, 64), (0, 0)
                                ar0, ar1, sr0, sr1 = 64, 128, 0, 64
                            vsl = vb[:, tau * 1024 + hd * 64: tau * 1024 + hd * 64 + 64]
                            E_("t", lambda t, d=PS[abank][ar0:ar1, qoff:qoff + n],
                                   w=vsl, p=pap, st=first_av[ar0], tp=av_tp: t.matmul(
                                d, w, p, start=st, stop=False, tile_position=tp,
                                skip_group_check=True),
                                reads=[bVB, pbB[tslot2]], writes=[bPS[abank]])
                            first_av[ar0] = False
                            E_("t", lambda t, d=PS[5][sr0:sr1, qoff:qoff + n],
                                   w=c_onescol[pr], p=pap, st=first_sum[sr0], tp=sum_tp: t.matmul(
                                d, w, p, start=st, stop=False, tile_position=tp,
                                skip_group_check=True),
                                reads=[bCB, pbB[tslot2]], writes=[bPS[5]])
                            first_sum[sr0] = False

                        pend = []
                        for tau in range(ntiles):
                            if tau < ntk_full:
                                n, qoff = 512, 0
                            else:
                                j = tau - ntk_full
                                n, qoff = 512 - 128 * j, 128 * j
                            tslot2 = tau % 2
                            cur = []
                            for parity in range(2):
                                kT = kb[:, pr * 2048 + tau * 128: pr * 2048 + tau * 128 + 64]
                                kBm = kb[:, pr * 2048 + tau * 128 + 64: pr * 2048 + tau * 128 + 128]
                                qe = 2 * (pr % 2) + parity
                                qsl = qs_ap[qe][:, qoff:qoff + n]
                                sb_ = 2 * tslot2 + parity
                                E_("t", lambda t, d=PS[sb_][0:64, 0:n], w=kT, q=qsl: t.matmul(
                                    d, w, q, start=True, stop=True, tile_position=(0, 0)),
                                    reads=[bKB, bQS[qe]], writes=[bPS[sb_]])
                                E_("t", lambda t, d=PS[sb_][64:128, 0:n], w=kBm, q=qsl: t.matmul(
                                    d, w, q, start=True, stop=True, tile_position=(0, 64)),
                                    reads=[bKB, bQS[qe]], writes=[bPS[sb_]])
                                cur.append((tau, parity, n, qoff, tslot2))
                            if ntk_full == 0 and n < 512:
                                for parity in range(2):
                                    sb_ = 2 * tslot2 + parity
                                    pp = zr[:, PB0 + tslot2 * 1024 + parity * 512:
                                            PB0 + tslot2 * 1024 + parity * 512 + n]
                                    E_("s", lambda s, d=pp, a=PS[sb_][:, 0:n], b=bb:
                                        s.activation(d, a, ACT.Exp, bias=b, scale=0.125),
                                        reads=[bPS[sb_], bBB[bslot]],
                                        writes=[pbB[tslot2]])
                            else:
                                pap = zr[:, PB0 + tslot2 * 1024: PB0 + tslot2 * 1024 + 512 + n]
                                E_("s", lambda s, d=pap, a=PSB[tslot2][:, 0:512 + n], b=bb: s.activation(
                                    d, a, ACT.Exp, bias=b, scale=0.125),
                                    reads=[bPS[2 * tslot2], bPS[2 * tslot2 + 1], bBB[bslot]],
                                    writes=[pbB[tslot2]])
                            if tau >= ntk_full:
                                for parity in range(2):
                                    pm = zr[:, PB0 + tslot2 * 1024 + parity * 512:
                                            PB0 + tslot2 * 1024 + parity * 512 + 128]
                                    E_("v", lambda v, d=pm: v.tensor_tensor(
                                        d, d, c_mask, AluOpType.mult),
                                        reads=[bCB], writes=[pbB[tslot2]])
                            for job in pend:
                                emit_avsums(*job)
                            pend = cur
                            if pr + 1 < NP:
                                if _NOQB:
                                    if tau == 0:
                                        emit_qB(pr + 1, slot)
                                elif tau < 4:
                                    emit_qB_stage(pr + 1, slot, tau)
                            elif next_qB is not None and tau < 4:
                                emit_qB_stage(0, next_qB, tau)
                            for _r in range(inject_rate):
                                pop_inject()
                        for job in pend:
                            emit_avsums(*job)
                        E_("v", lambda v, d=an_ap[pr], a=PS[abank][:]: v.tensor_copy(d, a),
                           reads=[bPS[abank]], writes=[anB[pr]])
                    while ii[0] < len(inj):
                        pop_inject()

                def epi_recips(ci):
                    def _rec(lo, hi):
                        def g(v):
                            with nc.allow_low_precision(reason="bf16 1/S scale for softmax (0.4%)"):
                                return v.reciprocal(invs[lo:hi, :], PS[5][lo:hi, :])
                        return g
                    E_("v", _rec(0, 8), reads=[bPS[5]], writes=[bINV])
                    E_("v", _rec(64, 72), reads=[bPS[5]], writes=[bINV])

                def epi_items(ci, ybks, ibks, lbank, defer):
                    xcol = ci * 512
                    slot = ci % 2
                    if defer:
                        y2ap, y2buf = xst[:, 512:1024], bXST[2]
                    else:
                        y2ap, y2buf = y2_ap, y2B
                    items = []
                    for pr in range(NP):
                        def _ia(pr=pr, ibank=ibks[pr % 2]):
                            E_("t", lambda t, d=PS[ibank][0:64, :]: t.matmul(
                                d, c_eselT[pr], invs[:], start=True, stop=True, tile_position=(0, 0)),
                                reads=[bINV, bCB], writes=[bPS[ibank]])
                            E_("t", lambda t, d=PS[ibank][64:128, :]: t.matmul(
                                d, c_eselB[pr], invs[:], start=True, stop=True, tile_position=(0, 64)),
                                reads=[bINV, bCB], writes=[bPS[ibank]])
                            E_("v", lambda v, d=an_ap[pr], i=PS[ibank][:]: v.tensor_tensor(
                                d, d, i, AluOpType.mult),
                                reads=[bPS[ibank]], writes=[anB[pr]])
                        items.append(_ia)
                    for d_ in range(NDC):
                        def _wo(d_=d_, ybk=ybks[d_ % 2]):
                            for pr in range(NP):
                                wT = wob[:, pr * 1024 + d_ * 128: pr * 1024 + d_ * 128 + 64]
                                wB = wob[:, pr * 1024 + d_ * 128 + 64: pr * 1024 + d_ * 128 + 128]
                                E_("t", lambda t, pr=pr, d=PS[ybk][0:64, :], w=wT, a=an_ap[pr]: t.matmul(
                                    d, w, a, start=(pr == 0), stop=(pr == NP - 1), tile_position=(0, 0), skip_group_check=True),
                                    reads=[bWO[pr], anB[pr]], writes=[bPS[ybk]])
                                E_("t", lambda t, pr=pr, d=PS[ybk][64:128, :], w=wB, a=an_ap[pr]: t.matmul(
                                    d, w, a, start=(pr == 0), stop=(pr == NP - 1), tile_position=(0, 64), skip_group_check=True),
                                    reads=[bWO[pr], anB[pr]], writes=[bPS[ybk]])
                            yb = zc_ap[slot][d_]
                            E_("v", lambda v, d=yb, y=PS[ybk][:]: v.tensor_tensor(
                                d, d, y, AluOpType.add),
                                reads=[bPS[ybk]], writes=[zcB[slot][d_]])
                            E_("s", lambda s, a=yb: s.activation(y2ap, a, ACT.Square),
                               reads=[zcB[slot][d_]], writes=[y2buf])
                            E_("t", lambda t, d_=d_, a=yb: t.matmul(
                                PS[lbank][0:64, :], c_ones64, a,
                                start=(d_ == 0), stop=(d_ == NDC - 1), tile_position=(0, 0), skip_group_check=True),
                                reads=[bCB, zcB[slot][d_]], writes=[bPS[lbank]])
                            E_("t", lambda t, d_=d_: t.matmul(
                                PS[lbank][64:128, :], c_ones64, y2ap,
                                start=(d_ == 0), stop=(d_ == NDC - 1), tile_position=(0, 64), skip_group_check=True),
                                reads=[bCB, y2buf], writes=[bPS[lbank]])
                        items.append(_wo)
                    items.append(lambda: rowmath_r(bPS[lbank], PS[lbank][0:1, :],
                                                   PS[lbank][64:65, :], LN_EPS_3, 0.0))
                    items.append(None)
                    def _mu3():
                        bcast_f32(lbank, rowa, bROWA)
                        E_("v", lambda v: v.tensor_copy(mub[:], PS[lbank][:]),
                           reads=[bPS[lbank]], writes=[bMUB])
                    items.append(_mu3)
                    items.append(None)
                    def _r3():
                        bcast_f32(lbank, rowb, bROWB)
                        E_("v", lambda v: v.tensor_copy(rub[:], PS[lbank][:]),
                           reads=[bPS[lbank]], writes=[bRUB])
                    items.append(_r3)
                    items.append(None)
                    for dp in range(4):
                        def _pb(dp=dp):
                            for d_ in (2 * dp, 2 * dp + 1):
                                tslot = d_ % 2
                                tt = tsc[:, tslot * 512:(tslot + 1) * 512]
                                yb = zc_ap[slot][d_]
                                E_("v", lambda v, a=yb, d=tt: v.tensor_tensor(
                                    d, a, mub[:], AluOpType.subtract),
                                    reads=[bMUB, zcB[slot][d_]], writes=[bTS[tslot]])
                                E_("v", lambda v, d=tt: v.tensor_tensor(
                                    d, d, rub[:], AluOpType.mult),
                                    reads=[bRUB], writes=[bTS[tslot]])
                                gcol = sgb[:, d_:d_ + 1]
                                bcol = sgb[:, 8 + d_:8 + d_ + 1]
                                E_("v", lambda v, d=tt, g=gcol, b=bcol: v.tensor_scalar(
                                    d, d, g, b, AluOpType.mult, AluOpType.add),
                                    reads=[bGB], writes=[bTS[tslot]])
                                xsl = xh[:, d_ * 1024 + xcol: d_ * 1024 + xcol + 512]
                                E_("v", lambda v, x=xsl, d=tt: v.tensor_tensor(
                                    x, x, d, AluOpType.add),
                                    reads=[bTS[tslot], bXH[d_][xcol // 512]], writes=[bXH[d_][xcol // 512]])
                        items.append(_pb)
                    return items

                def att_epilogue(ci):
                    epi_recips(ci)
                    for it in epi_items(ci, (4, 6), (5, 7), 5, False):
                        if it is not None:
                            it()

                def emit_guw_dma(hc):
                    for j, dw_ in ((0, dwg), (1, dwu)):
                        sl = (2 * hc + j) % 3
                        stg = wstag[:, sl * 1024:(sl + 1) * 1024]
                        E_("g", lambda g, d=stg, s=dw_.ap()[hc]: g.dma_start(d, s),
                           writes=[bWST[sl]], queue="qw%d" % sl)

                def emit_guw_cast(hc):
                    gsl = (2 * hc) % 3
                    usl = (2 * hc + 1) % 3
                    wsl = hc % 2
                    gb_ = wgbb[:, wsl * 1024:(wsl + 1) * 1024]
                    ub_ = wubb[:, wsl * 1024:(wsl + 1) * 1024]
                    E_("v", lambda v, d=gb_, s=wstag[:, gsl * 1024:(gsl + 1) * 1024]: v.tensor_copy(d, s),
                       reads=[bWST[gsl]], writes=[bWG[wsl]])
                    E_("s", lambda s, d=ub_, a=wstag[:, usl * 1024:(usl + 1) * 1024]: s.activation(d, a, ACT.Copy),
                       reads=[bWST[usl]], writes=[bWU[wsl]])

                def emit_wd_dma(hc):
                    sl = hc % 3
                    stg = wstag[:, sl * 1024:(sl + 1) * 1024]
                    E_("g", lambda g, d=stg, s=dwd.ap()[hc]: g.dma_start(d, s),
                       writes=[bWST[sl]], queue="qw%d" % sl)

                z3_ap = [zr[:, d * 512:(d + 1) * 512] for d in range(NDC)]

                def h_region(k):
                    if k < 8:
                        return 12288 + k * 512
                    if k < 16:
                        return 8192 + (k - 8) * 512
                    return 4096 + (k - 16) * 512
                h_ap = [zr[:, h_region(k): h_region(k) + 512] for k in range(NHC)]

                att_prep(0)
                _h0items = wo_items if half == 0 else []
                if _NOPREP:
                    att_heads(0, inject=_h0items)
                    att_prep(1)
                else:
                    att_heads(0, inject=att_prep_items(1) + _h0items,
                              next_qB=1, inject_rate=2)
                att_epilogue(0)
                z3B0 = rebind([f"z3a_{d}" for d in range(NDC)], zcB[0])
                emit_guw_dma(0)
                if _INJL < 1:
                    it0 = []
                else:
                    it0 = ([None] * 16 + ffn_stat_items(0, 6, True)
                           + ffn_zgen_items(0, 6, z3B0, z3_ap, True)
                           + [lambda: emit_guw_cast(0)])
                att_heads(1, inject=it0)
                _DEFER = (not _NODEF) and lim >= 3 and _INJL >= 2
                if _DEFER:
                    epi_recips(1)
                    epi1_gu = epi_items(1, (0, 2), (1, 1), 3, True)
                else:
                    att_epilogue(1)
                    epi1_gu = []

                if lim < 3:
                    break
                if _DEFER:
                    hB = (rebind([f"h{k}" for k in range(8)], [pbB[0], pbB[1], q2B])
                          + [None] * 16)
                else:
                    hB = rebind([f"h{k}" for k in range(NHC)], eb[8:27])
                z3B1 = None
                wkv_items = []
                if half == 1 and s_i == 0 and lim >= 99:
                    for _pc in range(16):
                        for _j, (_dsrc, _dstT, _dstB) in enumerate(
                                ((dwk, wqb, bWQ), (dwv, wob, bWO))):
                            def _ld(pc=_pc, dsrc=_dsrc, dstT=_dstT, dstB=_dstB, slot=_j % 2):
                                st = tsc[:, slot * 512:(slot + 1) * 512]
                                sap = dsrc.ap()[:, pc * 512:(pc + 1) * 512]
                                dst = dstT[:, pc * 512:(pc + 1) * 512]
                                E_("y", lambda y, d=st, s=sap: y.dma_start(d, s),
                                   writes=[bTS[slot]], queue="qt%d" % slot)
                                E_("v", lambda v, d=dst, s=st: v.tensor_copy(d, s),
                                   reads=[bTS[slot]], writes=[dstB[pc // 2]])
                            wkv_items.append(_ld)

                for tf in range(2):
                    xcol = tf * 512
                    if tf == 0:
                        z3Bl = z3B0
                        if _INJL < 1:
                            for it in (ffn_stat_items(0, 6, True)
                                       + ffn_zgen_items(0, 6, z3B0, z3_ap, True)):
                                if it is not None:
                                    it()
                            emit_guw_cast(0)
                        if _INJL >= 2:
                            _gfull = ffn_stat_items(1, 2, False, use_qsb=True)
                            _gn = int(_os.environ.get("KMK_INJN", "99"))
                            gu_inject = epi1_gu + _gfull[:_gn]
                            gu_rest = _gfull[_gn:]
                        else:
                            gu_inject = []
                            gu_rest = []
                    else:
                        z3Bl = z3B1
                        gu_inject = []
                        emit_guw_dma(0)
                        emit_guw_cast(0)
                    gii = [0]

                    def pop_gu():
                        if gii[0] < len(gu_inject):
                            it = gu_inject[gii[0]]
                            gii[0] += 1
                            if it is not None:
                                it()
                    for hc in range(NHC):
                        if tf == 0 and _DEFER:
                            if hc == 8:
                                hB[8:16] = rebind([f"h{k}" for k in range(8, 16)], anB)
                            elif hc == 16:
                                hB[16:24] = rebind([f"h{k}" for k in range(16, 24)], zcB[1])
                        wsl = hc % 2
                        if hc + 1 < NHC:
                            emit_guw_dma(hc + 1)
                        gB = 4 + 2 * ((hc + 1) % 2)
                        uB = gB + 1
                        for dc in range(NDC):
                            wgsl = wgbb[:, wsl * 1024 + dc * 128: wsl * 1024 + (dc + 1) * 128]
                            wusl = wubb[:, wsl * 1024 + dc * 128: wsl * 1024 + (dc + 1) * 128]
                            E_("t", lambda t, dc=dc, w=wgsl, z=z3_ap[dc], d=PS[gB][:]: t.matmul(
                                d, w, z, start=(dc == 0), stop=(dc == NDC - 1)),
                                reads=[bWG[wsl], z3Bl[dc]], writes=[bPS[gB]])
                            E_("t", lambda t, dc=dc, w=wusl, z=z3_ap[dc], d=PS[uB][:]: t.matmul(
                                d, w, z, start=(dc == 0), stop=(dc == NDC - 1)),
                                reads=[bWU[wsl], z3Bl[dc]], writes=[bPS[uB]])
                        if hc + 1 < NHC:
                            emit_guw_cast(hc + 1)
                        pop_gu()
                        sg_ = xst[:, 1024 + (hc % 2) * 512: 1024 + (hc % 2 + 1) * 512]
                        E_("s", lambda s, d=sg_, a=PS[gB][:]: s.activation(d, a, ACT.Silu),
                           reads=[bPS[gB]], writes=[bXST[2 + 2 * (hc % 2) - 1 if False else 2 * (hc % 2) + 1]])
                        E_("v", lambda v, d=h_ap[hc], a=sg_, u=PS[uB][:]: v.tensor_tensor(
                            d, a, u, AluOpType.mult),
                            reads=[bXST[2 * (hc % 2) + 1], bPS[uB]], writes=[hB[hc]])
                        pop_gu()
                    while gii[0] < len(gu_inject):
                        pop_gu()
                    if tf == 0:
                        for it in gu_rest:
                            if it is not None:
                                it()
                        z3B1 = rebind([f"z3b_{d}" for d in range(NDC)], z3B0)
                        dn_inject = ffn_zgen_items(1, 0, z3B1, z3_ap, False) + wkv_items[:24]
                        if _INJL < 2:
                            for it in ffn_stat_items(1, 2, False):
                                if it is not None:
                                    it()
                        if _INJL < 3:
                            for it in dn_inject:
                                it()
                            dn_inject = []
                    else:
                        dn_inject = list(wkv_items[24:])
                    dii = [0]

                    def pop_dn():
                        if dii[0] < len(dn_inject):
                            it = dn_inject[dii[0]]
                            dii[0] += 1
                            it()
                    for hc in range(3):
                        emit_wd_dma(hc)
                    for hc in range(NHC):
                        sl = hc % 3
                        wsl = hc % 2
                        stg = wstag[:, sl * 1024:(sl + 1) * 1024]
                        db_ = wdbb[:, wsl * 1024:(wsl + 1) * 1024]
                        E_("s", lambda s, d=db_, a=stg: s.activation(
                            d, a, ACT.Copy, scale=1.0 / 127.0),
                            reads=[bWST[sl], bCF], writes=[bWD[wsl]])
                        if hc + 3 < NHC:
                            emit_wd_dma(hc + 3)
                        pop_dn()
                        pop_dn()
                        for d_ in range(NDC):
                            wdsl = wdbb[:, wsl * 1024 + d_ * 128: wsl * 1024 + (d_ + 1) * 128]
                            E_("t", lambda t, w=wdsl, h=h_ap[hc], d=PS[d_][:], hc=hc: t.matmul(
                                d, w, h, start=(hc == 0), stop=(hc == NHC - 1)),
                                reads=[bWD[wsl], hB[hc]], writes=[bPS[d_]])
                    if tf == 0:
                        emit_guw_dma(0)
                    for d_ in range(NDC):
                        xsl = xh[:, d_ * 1024 + xcol: d_ * 1024 + xcol + 512]
                        E_("v", lambda v, x=xsl, f=PS[d_][:]: v.tensor_tensor(
                            x, x, f, AluOpType.add),
                            reads=[bPS[d_], bXH[d_][tf]], writes=[bXH[d_][tf]])
                        odst = os_[:, d_ * 2048 + half * 1024 + tf * 512:
                                   d_ * 2048 + half * 1024 + tf * 512 + 512]
                        E_("g", lambda ge, d=odst, s=xsl: ge.dma_start(d, s),
                           reads=[bXH[d_][tf]], queue="qo" if tf == 0 else "qo1")
                        nstores[tf] += 1
                    if half == 0 and lim >= 4:
                        emit_xload(1, tf)
                if half == 0 and lim == 3:
                    break

        P.ops["g"].append(([Tok("qo", nstores[0] * 16, True),
                           Tok("qo1", nstores[1] * 16, True)],
                          None, Tok("g", -1, False), None))

        semval = P.finalize()

        def replay(eng_obj, ename):
            for waits, fn, tok, queue in P.ops[ename]:
                for t in waits:
                    wv = t.idx if t.queue else semval[(t.src, t.idx)]
                    assert wv is not None, (ename, t.src, t.idx)
                    eng_obj.wait_ge(SEMS[t.src], wv)
                if fn is None:
                    continue
                inst = fn(eng_obj)
                if queue is not None:
                    inst.then_inc(SEMS[tok.src], 16)
                elif (tok.src, tok.idx) in P.referenced:
                    inst.then_inc(SEMS[tok.src], 1)

        @block.tensor
        def _(tensor):
            replay(tensor, "t")

        @block.vector
        def _(vector):
            replay(vector, "v")

        @block.scalar
        def _(scalar):
            replay(scalar, "s")

        @block.gpsimd
        def _(gpsimd):
            replay(gpsimd, "g")

        @block.sync
        def _(sync):
            replay(sync, "y")

    return nc, {e: len(P.ops[e]) for e in P.ENGS}


_NC_CACHE = [None]


def _host_inputs(inputs):
    x = np.asarray(inputs["L0_attn_x"])
    wq = np.asarray(inputs["L0_attn_Wq"])
    wk = np.asarray(inputs["L0_attn_Wk"])
    wv = np.asarray(inputs["L0_attn_Wv"])
    wo = np.asarray(inputs["L0_attn_Wo"])
    gam = np.asarray(inputs["L0_attn_gamma"])
    bet = np.asarray(inputs["L0_attn_beta"])
    wg = np.asarray(inputs["L0_ffn_w_gate"])
    wu = np.asarray(inputs["L0_ffn_w_up"])
    wd = np.asarray(inputs["L0_ffn_w_down"])

    def whead(w):
        return np.ascontiguousarray(
            w.reshape(H, NDC, 128, E).transpose(2, 1, 0, 3).reshape(128, NDC * H * E))

    m = {}
    m["wq"] = whead(wq)
    m["wk"] = whead(wk)
    m["wv"] = whead(wv)
    m["wo"] = np.ascontiguousarray(wo.reshape(NP, 128, D).transpose(1, 0, 2).reshape(128, NP * D))
    gb = np.zeros((128, 16), dtype=np.float32)
    gb[:, 0:8] = gam.reshape(8, 128).T
    gb[:, 8:16] = bet.reshape(8, 128).T
    m["gb"] = gb
    m["wg"] = np.ascontiguousarray(
        wg.reshape(NHC, 128, NDC, 128).transpose(0, 3, 2, 1).reshape(NHC, 128, D))
    m["wu"] = np.ascontiguousarray(
        wu.reshape(NHC, 128, NDC, 128).transpose(0, 3, 2, 1).reshape(NHC, 128, D))
    m["wd"] = np.ascontiguousarray(wd.T.reshape(NHC, 128, D))

    cbv = np.zeros((128, 1792), dtype=np.float32)
    cbv[:, 0] = 1.0
    for j in range(8):
        cbv[:, 64 + j * 64 + j] = 1.0
    for j in range(8):
        cbv[64 + j, 576 + (j * 2) * 64: 576 + (j * 2) * 64 + 64] = 1.0
        cbv[j, 576 + (j * 2 + 1) * 64: 576 + (j * 2 + 1) * 64 + 64] = 1.0
    tri = np.triu(np.ones((128, 128), dtype=np.float32))
    cbv[:, 1600:1728] = tri
    cbv[:, 1728:1792] = 1.0
    m["cb"] = cbv
    cfv = np.zeros((128, 144), dtype=np.float32)
    cfv[0, 0:128] = 1.0
    _CONSTVALS = [1e-05, 0.001, -LN127, -0.0625, 0.0625, 0.125, -0.125,
                  1.0 / 1024.0, 1.0 / 127.0, 0.5, -0.5, 2.0, -1.0, 3.0]
    for _i, _cv in enumerate(_CONSTVALS):
        cfv[:, 128 + _i] = _cv
    m["cf"] = cfv

    per_core_x = []
    for c in range(NCORE):
        xsh = np.empty((2, 128, NDC * SEQ), dtype=np.float32)
        for si in range(2):
            xi = x[2 * c + si]
            xsh[si] = xi.T.reshape(NDC, 128, SEQ).transpose(1, 0, 2).reshape(128, NDC * SEQ)
        per_core_x.append(np.ascontiguousarray(xsh))
    return m, per_core_x


def kernel(**inputs):
    m, per_core_x = _host_inputs(inputs)
    if _NC_CACHE[0] is None:
        _NC_CACHE[0] = build_program()[0]
    nc = _NC_CACHE[0]
    in_maps = []
    for c in range(NCORE):
        im = dict(m)
        im["x"] = per_core_x[c]
        in_maps.append(im)
    res = run_bass_kernel_spmd(nc, in_maps, core_ids=list(range(NCORE)))
    out = np.empty((B_TOT, SEQ, D), dtype=np.float32)
    for c in range(NCORE):
        o = res.results[c]["o"]
        for si in range(2):
            out[2 * c + si] = o[si].reshape(128, NDC, SEQ).transpose(2, 1, 0).reshape(SEQ, D)
    return out


if __name__ == "__main__":
    nc, counts = build_program()
    print("instruction counts:", counts, "total:", sum(counts.values()))
